# version 07
# baseline (speedup 1.0000x reference)
import numpy as np
import sys

sys.path.insert(0, "/opt/trn_rl_repo")
import concourse.bass as bass
import concourse.mybir as mybir
from concourse.alu_op_type import AluOpType as OP
from concourse.bass import IndirectOffsetOnAxis
from concourse.bass_utils import run_bass_kernel_spmd
import ml_dtypes

F32 = mybir.dt.float32
BF16 = mybir.dt.bfloat16
I32 = mybir.dt.int32
U8 = mybir.dt.uint8
AF = mybir.ActivationFunctionType
AX = mybir.AxisListType
BF = ml_dtypes.bfloat16

B, T, D = 8, 4096, 1024
DI, DS = 2048, 16
DH = 4096
E = 8
CAP = 5120
TC = 512
NCH = T // TC
NDC = DI // 128
NMC = D // 128
NHC = DH // 128
SG = 4
POOL_B = 204000
import os
KPH = int(os.environ.get("KPH", "4"))
KCH = int(os.environ.get("KCH", "8"))
KSUB = int(os.environ.get("KSUB", "99"))
KJ4 = int(os.environ.get("KJ4", "3"))
KWC = int(os.environ.get("KWC", "1"))
KP3 = int(os.environ.get("KP3", "9"))
KOFF = os.environ.get("KOFF", "dev")
KDBG = int(os.environ.get("KDBG", "0"))

_ESZ = {F32: 4, BF16: 2, I32: 4, U8: 1}


class View:
    def __init__(self, pool, off, n, dt, p0=0, p1=128):
        self.pool, self.off, self.n, self.dt = pool, off, n, dt
        self.p0, self.p1 = p0, p1
        self.esz = _ESZ[dt]

    def __call__(self, a=0, b=None, p0=None, p1=None):
        if b is None:
            b = self.n
        p0 = self.p0 if p0 is None else p0
        p1 = self.p1 if p1 is None else p1
        ap = self.pool[p0:p1, self.off + a * self.esz: self.off + b * self.esz]
        if self.dt != U8:
            ap = ap.bitcast(self.dt)
        return ap


def _finish(nc, sched):
    with nc.Block() as block:
        @block.sync
        def _(eng):
            for step in sched["D"]:
                step(eng)

        @block.vector
        def _(eng):
            for step in sched["V"]:
                step(eng)

        @block.scalar
        def _(eng):
            for step in sched["S"]:
                step(eng)

        @block.tensor
        def _(eng):
            for step in sched["T"]:
                step(eng)

        @block.gpsimd
        def _(eng):
            for step in sched["G"]:
                step(eng)
    return nc


def build_nc():
    nc = bass.Bass("TRN2", target_bir_lowering=False)

    def ein(name, shape, dt=F32):
        return nc.dram_tensor(name, shape, dt, kind="ExternalInput")

    x_in_h = ein("x_fm", [D, T])
    W_in_h = ein("W_in", [D, 2 * DI])
    b_in_h = ein("b_inp", [128, 32])
    convw_h = ein("conv_wp", [128, NDC * 4])
    convb_h = ein("conv_bp", [128, NDC])
    W_xp_h = ein("W_xpp", [128, NDC * 32])
    b_xp_h = ein("b_xp", [2 * DS])
    W_dt_h = ein("W_dt", [DI, DI])
    b_dt_h = ein("b_dtp", [128, NDC])
    A_log_h = ein("A_logp", [128, NDC * DS])
    Dsk_h = ein("D_skipp", [128, NDC])
    W_out_h = ein("W_out", [DI, D])
    b_out_h = ein("b_outp", [128, NMC])
    alpha_h = ein("alphap", [128, NMC])
    beta_h = ein("betap", [128, NMC])
    rw_h = ein("router_wp", [128, NMC * 8])
    rb_h = ein("router_b", [E])
    w1_h = ein("w1e", [D, DH])
    b1_h = ein("b1ep", [128, NHC])
    w2_h = ein("w2e", [DH, D])
    b2_h = ein("b2e", [D])
    c_h = ein("c_row", [128, 8])
    wmod_h = ein("wmod", [D, 6 * D])
    identb_h = ein("identb", [128, 128], BF16)
    identf_h = ein("identf", [128, 128], F32)
    cols_h = ein("colconsts", [128, 4], BF16)
    onerow_h = ein("onerow", [1, 128], BF16)
    selmat_h = ein("selmat", [128, 128], F32)
    rowsel_h = ein("rowsel", [32, 32 * 128], BF16)
    iota8_h = ein("iota8", [128, 256], F32)
    myrows_h = ein("myrows", [128, 1], I32)
    mysel0_h = ein("mysel0", [128, 1], I32)
    mysel1_h = ein("mysel1", [128, 1], I32)
    fconst_h = ein("fconst", [128, 4], F32)
    offhost_h = ein("offhost", [128, 256], I32)

    out_h = nc.dram_tensor("out", [T, D], F32, kind="ExternalOutput")
    if KDBG:
        dbg_oc = nc.dram_tensor("dbg_oc", [128, 256], I32, kind="ExternalOutput")
        dbg_off = nc.dram_tensor("dbg_off", [128, 2048], I32, kind="ExternalOutput")
        dbg_pos = nc.dram_tensor("dbg_pos", [128, 2048], F32, kind="ExternalOutput")
        dbg_msk = nc.dram_tensor("dbg_msk", [128, 2048], F32, kind="ExternalOutput")
        dbg_ltm = nc.dram_tensor("dbg_ltm", [128, 256], F32, kind="ExternalOutput")
        dbg_m1 = nc.dram_tensor("dbg_m1", [128, 256], F32, kind="ExternalOutput")
        dbg_xm = nc.dram_tensor("dbg_xm", [T, D], BF16, kind="ExternalOutput")
        dbg_h2s = nc.dram_tensor("dbg_h2s", [T, D], BF16, kind="ExternalOutput")
        dbg_ptm = nc.dram_tensor("dbg_ptm", [128, 256], F32, kind="ExternalOutput")
        dbg_gtm = nc.dram_tensor("dbg_gtm", [128, 256], F32, kind="ExternalOutput")

    st_Win = nc.dram_tensor("st_Win", [NMC, 128, 2 * DI], BF16)
    st_Wdt = nc.dram_tensor("st_Wdt", [NDC, 128, DI], BF16)
    st_Wout = nc.dram_tensor("st_Wout", [NDC, 128, D], BF16)
    st_w1 = nc.dram_tensor("st_w1", [NMC, 128, DH], BF16)
    st_w2 = nc.dram_tensor("st_w2", [NHC, 128, D], BF16)
    h2stage = nc.dram_tensor("h2stage", [T, D], BF16)
    xmid = nc.dram_tensor("xmid", [T, D], BF16)
    gm_in = nc.dram_tensor("gm_in", [16, T], F32)
    gm_all = nc.dram_tensor("gm_all", [128, T], F32)
    h2_all = nc.dram_tensor("h2_all", [B * T, D], BF16)
    h2ag = nc.dram_tensor("h2ag", [NCH, B * TC, D], BF16)
    pos_dram = nc.dram_tensor("pos_dram", [2048, 128], F32)
    offs_dram = nc.dram_tensor("offs_dram", [1024, 256], I32)
    buf_dram = nc.dram_tensor("buf_dram", [CAP + 1, D], BF16)
    outstage = nc.dram_tensor("outstage", [CAP, D], BF16)
    out_all = nc.dram_tensor("out_all", [E * CAP, D], BF16)

    import contextlib
    ctx = contextlib.ExitStack()
    semT = ctx.enter_context(nc.semaphore("semT"))
    semV = ctx.enter_context(nc.semaphore("semV"))
    semS = ctx.enter_context(nc.semaphore("semS"))
    semD = ctx.enter_context(nc.semaphore("semD"))
    semG = ctx.enter_context(nc.semaphore("semG"))
    semC = ctx.enter_context(nc.semaphore("semC"))
    semSA = ctx.enter_context(nc.semaphore("semSA"))
    semSB = ctx.enter_context(nc.semaphore("semSB"))
    semPO = ctx.enter_context(nc.semaphore("semPO"))
    semW0 = ctx.enter_context(nc.semaphore("semW0"))
    semW1 = ctx.enter_context(nc.semaphore("semW1"))
    semE0 = ctx.enter_context(nc.semaphore("semE0"))
    semE1 = ctx.enter_context(nc.semaphore("semE1"))
    semE2 = ctx.enter_context(nc.semaphore("semE2"))
    semE3 = ctx.enter_context(nc.semaphore("semE3"))
    semOT = ctx.enter_context(nc.semaphore("semOT"))
    semWM = ctx.enter_context(nc.semaphore("semWM"))
    SEM = {"T": semT, "V": semV, "S": semS, "D": semD, "G": semG,
           "C": semC, "SA": semSA, "SB": semSB, "PO": semPO,
           "W0": semW0, "W1": semW1, "E0": semE0, "E1": semE1,
           "E2": semE2, "E3": semE3, "OT": semOT, "WM": semWM}

    pool = ctx.enter_context(nc.sbuf_tensor("pool", [128, POOL_B], U8))
    offcol_t = ctx.enter_context(nc.sbuf_tensor("offcol_t", [128, 256], I32))
    myrows_t = ctx.enter_context(nc.sbuf_tensor("myrows_t", [128, 1], I32))
    mysel0_t = ctx.enter_context(nc.sbuf_tensor("mysel0_t", [128, 1], I32))
    mysel1_t = ctx.enter_context(nc.sbuf_tensor("mysel1_t", [128, 1], I32))
    r0i_t = ctx.enter_context(nc.sbuf_tensor("r0i_t", [128, 32], I32))
    r1i_t = ctx.enter_context(nc.sbuf_tensor("r1i_t", [128, 32], I32))
    posr0_t = ctx.enter_context(nc.sbuf_tensor("posr0_t", [128, 128], F32))
    posr1_t = ctx.enter_context(nc.sbuf_tensor("posr1_t", [128, 128], F32))
    g0v_t = ctx.enter_context(nc.sbuf_tensor("g0v_t", [128, 1024], BF16))
    g1v_t = ctx.enter_context(nc.sbuf_tensor("g1v_t", [128, 1024], BF16))

    class TW:
        def __init__(self, t):
            self.t = t
        def __call__(self, a=0, b=None, p0=0, p1=128):
            if b is None:
                return self.t[p0:p1, :]
            return self.t[p0:p1, a:b]
    _off = [0]

    def alloc(n, dt, p0=0, p1=128):
        off = _off[0]
        sz = (n * _ESZ[dt] + 31) & ~31
        _off[0] = off + sz
        assert _off[0] <= POOL_B, f"pool overflow {_off[0]}"
        return View(pool, off, n, dt, p0, p1)

    identb = alloc(128, BF16)
    identf = alloc(128, F32)
    colc = alloc(4, BF16)
    onerow = alloc(128, BF16, 0, 1)
    selmat = alloc(128, F32)
    rowsel = alloc(32 * 128, BF16, 0, 32)
    iota8 = alloc(256, F32)
    myrows = TW(myrows_t)
    mysel0 = TW(mysel0_t)
    mysel1 = TW(mysel1_t)
    vb_in = alloc(32, F32)
    vconvw = alloc(NDC * 4, F32)
    vconvb = alloc(NDC, F32)
    vb_xp = alloc(1, F32, 0, 32)
    vb_dt = alloc(NDC, F32)
    vA = alloc(NDC * DS, F32)
    vD = alloc(NDC, F32)
    vb_out = alloc(NMC, F32)
    valpha = alloc(NMC, F32)
    vbeta = alloc(NMC, F32)
    vb1 = alloc(NHC, F32)
    vrb = alloc(1, F32, 0, 8)
    csil = alloc(8, F32)
    modc = alloc(48, F32)
    cac = alloc(8, F32)
    cbc = alloc(8, F32)
    wxp = alloc(NDC * 32, BF16)
    wrt = alloc(NMC * 8, BF16)
    state = alloc(NDC * DS, F32)
    ltm = alloc(256, F32)
    gtm = alloc(256, F32)
    m1c = alloc(256, F32)
    m2c = alloc(256, F32)
    rsc0 = alloc(256, F32)
    rsc1 = alloc(256, F32)
    rsc2 = alloc(32, F32)
    rsc3 = alloc(32, F32)
    b2row = alloc(128, F32, 0, 8)
    b2rowb = alloc(128, BF16, 0, 8)
    gbrow = alloc(128, F32, 0, 8)
    gbrowb = alloc(128, BF16, 0, 8)
    b2bc = alloc(D, BF16)
    gbbc = alloc(D, BF16)
    postm = alloc(256, F32)
    offcol = TW(offcol_t)
    fco = alloc(4, F32)
    totc = alloc(1, F32)
    offc = alloc(1, F32)
    e0c = alloc(32, F32)
    p0c = alloc(32, F32)
    w0c = alloc(32, F32)
    e1c = alloc(32, F32)
    p1c = alloc(32, F32)
    w1c = alloc(32, F32)
    r0i = TW(r0i_t)
    r1i = TW(r1i_t)
    gmev = alloc(512, F32, 0, 16)
    gmslab = alloc(512, F32)
    zrow = alloc(1024, BF16)
    xtail = alloc(NDC * 3, BF16)
    persist_end = _off[0]

    _off[0] = (persist_end + 31) & ~31
    arena = _off[0]
    xs = alloc(NMC * TC, BF16)
    tn = alloc(NMC * TC, BF16)
    hmod = alloc(NMC * TC, BF16)
    xin = alloc(NDC * (TC + 3), BF16)
    zs = alloc(NDC * TC, BF16)
    xc = alloc(NDC * TC, BF16)
    dxc = alloc(NDC * TC, BF16)
    ys = alloc(NDC * TC, BF16)
    Bb0 = alloc(SG * TC, BF16)
    Cb0 = alloc(SG * TC, BF16)
    dA0 = alloc(SG * TC, BF16)
    dA1 = alloc(SG * TC, BF16)
    dBx0 = alloc(SG * TC, BF16)
    dBx1 = alloc(SG * TC, BF16)
    Hsb = alloc(SG * TC, BF16)
    wbuf0 = alloc(2048, BF16)
    wbuf1 = alloc(2048, BF16)
    srow = alloc(2048, F32, 0, 1)
    srowb = alloc(1024, BF16, 0, 1)
    bcm = alloc(TC, BF16)
    bcr = alloc(TC, BF16)
    bcrow = alloc(TC, BF16, 0, 32)
    l8 = alloc(TC, F32, 0, 8)
    evrow0 = alloc(1024, BF16)
    evrow1 = alloc(1024, BF16)
    evrow2 = alloc(1024, BF16)
    evrow3 = alloc(1024, BF16)
    mamba_end = _off[0]
    spl = View(pool, bcm.off, 512, F32)
    dl = View(pool, xin.off, NDC * TC, BF16)
    outs = View(pool, zs.off, NMC * TC, BF16)
    wmscr = View(pool, xs.off, 6144, F32)
    modacc = View(pool, ys.off, 48, F32)

    _off[0] = arena
    w1sb = alloc(NMC * DH, BF16)
    w2sb = alloc(NHC * D, BF16)
    rbase = _off[0]
    maskg = alloc(2048, F32)
    cumg = alloc(2048, F32)
    posg = alloc(2048, F32)
    offsg = alloc(2048, F32)
    offsi = View(pool, maskg.off, 2048, I32)
    r1_end = _off[0]
    posr0 = TW(posr0_t)
    posr1 = TW(posr1_t)
    _off[0] = rbase
    h1sb = alloc(NHC * 256, BF16)
    buffm = alloc(8 * 256, BF16)
    bft = alloc(2 * 1024, BF16)
    otm = alloc(1024, BF16)
    r1b_end = _off[0]
    scat0 = View(pool, h1sb.off, 8 * D, BF16)
    scat1 = View(pool, h1sb.off + 16384, 8 * D, BF16)
    g0v = TW(g0v_t)
    g1v = TW(g1v_t)
    xtt = View(pool, bft.off, D, BF16)
    yout = View(pool, bft.off + 2048, D, F32)
    moe_end = max(r1_end, r1b_end)
    assert max(mamba_end, moe_end) <= POOL_B, (mamba_end, moe_end)

    pmm0 = ctx.enter_context(nc.psum_tensor("pmm0", [128, 512], F32))
    pmm1 = ctx.enter_context(nc.psum_tensor("pmm1", [128, 512], F32))
    pbc = ctx.enter_context(nc.psum_tensor("pbc", [128, 512], F32))
    pstat = ctx.enter_context(nc.psum_tensor("pstat", [128, 512], F32))
    ptrf = ctx.enter_context(nc.psum_tensor("ptrf", [128, 128], F32))
    ptr0 = ctx.enter_context(nc.psum_tensor("ptr0", [128, 1024], BF16))
    ptr1 = ctx.enter_context(nc.psum_tensor("ptr1", [128, 1024], BF16))

    sched = {k: [] for k in "TVSDG"}
    cnt = {k: 0 for k in "TVSDGC"}
    for _k in ("SA", "SB", "PO", "W0", "W1", "E0", "E1", "E2", "E3", "OT", "WM"):
        cnt[_k] = 0

    def emit(e, fn, inc=1, **waits):
        w = [(SEM[k], v) for k, v in waits.items() if v]

        def step(eng, fn=fn, w=w, inc=inc, e=e):
            for sem, val in w:
                eng.wait_ge(sem, val)
            r = fn(eng)
            if inc and r is not None:
                r.then_inc(SEM[e], inc)
        q = {"C": "G", "SA": "D", "SB": "D", "PO": "D", "W0": "D",
             "W1": "D", "E0": "D", "E1": "D", "E2": "D", "E3": "D",
             "OT": "D", "WM": "D"}.get(e, e)
        sched[q].append(step)
        if inc:
            cnt[e] += inc
        return cnt[e]

    def dma(q, out, in_, **waits):
        return emit(q, lambda eng, o=out, i=in_: eng.dma_start(out=o, in_=i),
                    inc=16, **waits)

    M = {"mi": 0}

    for dst, src in [
        (identb(), identb_h.ap()), (identf(), identf_h.ap()),
        (colc(), cols_h.ap()), (onerow(), onerow_h.ap()),
        (selmat(), selmat_h.ap()), (rowsel(), rowsel_h.ap()),
        (iota8(), iota8_h.ap()), (myrows(), myrows_h.ap()),
        (mysel0(), mysel0_h.ap()), (mysel1(), mysel1_h.ap()),
        (csil(), c_h.ap()),
        (vb_in(), b_in_h.ap()),
        (vconvw(), convw_h.ap()),
        (vconvb(), convb_h.ap()),
        (vb_xp(), b_xp_h.ap().rearrange("(s o) -> s o", o=1)),
        (vb_dt(), b_dt_h.ap()),
        (vA(), A_log_h.ap()),
        (vD(), Dsk_h.ap()),
        (vb_out(), b_out_h.ap()),
        (valpha(), alpha_h.ap()),
        (vbeta(), beta_h.ap()),
        (vb1(), b1_h.ap()),
        (vrb(), rb_h.ap().rearrange("(s o) -> s o", o=1)),
        (b2row(), b2_h.ap().rearrange("(a b) -> a b", b=128)),
        (fconst(), fconst_h.ap()) if False else (fco(), fconst_h.ap()),
    ]:
        dma("D", dst, src)
    d_consts = cnt["D"]

    dma("G", st_Win.ap().rearrange("a p m -> (a p) m"), W_in_h.ap())
    dma("G", st_Wdt.ap().rearrange("a p m -> (a p) m"), W_dt_h.ap())
    dma("G", st_Wout.ap().rearrange("a p m -> (a p) m"), W_out_h.ap())
    dma("G", wxp(), W_xp_h.ap())
    g_mamba_w = dma("G", wrt(), rw_h.ap())
    dma("G", st_w1.ap().rearrange("a p m -> (a p) m"), w1_h.ap())
    g_moe_w = dma("G", st_w2.ap().rearrange("a p m -> (a p) m"), w2_h.ap())

    s_ = emit("S", lambda eng: eng.activation(out=vA(), in_=vA(), func=AF.Exp),
              D=d_consts)
    emit("V", lambda eng: eng.tensor_scalar(
        out=vA(), in0=vA(), scalar1=-1.0, scalar2=None, op0=OP.mult), S=s_)
    s_c = emit("S", lambda eng: eng.activation(out=csil(), in_=csil(),
                                               func=AF.Silu), D=d_consts)

    v_acc = 0
    for k in range(NMC):
        d_w = emit("WM", lambda eng, k=k: eng.dma_start(
            out=wmscr(), in_=wmod_h.ap()[k * 128:(k + 1) * 128, :]),
            inc=16, V=v_acc)

        def mmk(eng, k=k):
            r = None
            for mc in range(48):
                r = eng.matmul(pstat[:, mc:mc + 1],
                               wmscr(mc * 128, (mc + 1) * 128),
                               csil(k, k + 1), start=True, stop=True)
            return r
        t_mm = emit("T", mmk, WM=d_w, S=s_c, V=v_acc)
        if k == 0:
            v_acc = emit("V", lambda eng: eng.tensor_copy(
                out=modacc(), in_=pstat[:, 0:48]), T=t_mm)
        else:
            v_acc = emit("V", lambda eng: eng.tensor_tensor(
                out=modacc(), in0=modacc(), in1=pstat[:, 0:48], op=OP.add),
                T=t_mm)
    M["pstat_V"] = v_acc
    v_mod0 = emit("V", lambda eng: eng.tensor_copy(out=modc(), in_=modacc()),
                  V=v_acc)
    s_ab = emit("S", lambda eng: (
        eng.activation(out=rsc2(0, 8), in_=modc(16, 24), func=AF.Abs),
        eng.activation(out=rsc3(0, 8), in_=modc(40, 48), func=AF.Abs))[-1],
        V=v_mod0)

    def _cl1(eng):
        eng.tensor_scalar(out=rsc2(0, 8), in0=rsc2(0, 8), scalar1=0.25,
                          scalar2=None, op0=OP.max)
        eng.tensor_scalar(out=rsc3(0, 8), in0=rsc3(0, 8), scalar1=0.25,
                          scalar2=None, op0=OP.max)
        eng.tensor_scalar(out=rsc0(0, 8), in0=modc(16, 24), scalar1=0.0,
                          scalar2=None, op0=OP.is_ge)
        return eng.tensor_scalar(out=rsc0(8, 16), in0=modc(40, 48),
                                 scalar1=0.0, scalar2=None, op0=OP.is_ge)
    v_mod = emit("V", _cl1, S=s_ab)
    v_mod = emit("V", lambda eng: eng.tensor_scalar(
        out=rsc0(0, 16), in0=rsc0(0, 16), scalar1=2.0, scalar2=-1.0,
        op0=OP.mult, op1=OP.add), V=v_mod)

    def _cl3(eng):
        eng.tensor_tensor(out=modc(16, 24), in0=rsc0(0, 8), in1=rsc2(0, 8),
                          op=OP.mult)
        eng.tensor_tensor(out=modc(40, 48), in0=rsc0(8, 16), in1=rsc3(0, 8),
                          op=OP.mult)
        eng.tensor_scalar(out=cac(), in0=modc(8, 16), scalar1=1.0,
                          scalar2=None, op0=OP.add)
        return eng.tensor_scalar(out=cbc(), in0=modc(32, 40), scalar1=1.0,
                                 scalar2=None, op0=OP.add)
    v_mod = emit("V", _cl3, V=v_mod)

    v_z = emit("V", lambda eng: eng.memset(zrow(), 0.0))
    for r in range(0, CAP + 1, 128):
        n = min(128, CAP + 1 - r)
        dma("G", buf_dram.ap()[r:r + n, :], zrow()[0:n, :], V=v_z)
    g_bufz = cnt["G"]
    emit("V", lambda eng: eng.memset(state(), 0.0), D=d_consts)
    v_tail0 = emit("V", lambda eng: eng.memset(xtail(), 0.0))

    def bcast3(view):
        return view().unsqueeze(1).broadcast_to([128, NMC, TC])

    def stat_ln(xview, s_sq, msq_col, unbiased, eps, xV=0, xG=0):
        def mm_mean(eng):
            r = None
            for k in range(NMC):
                r = eng.matmul(pstat[0:1, :], colc(1, 2),
                               xview(k * TC, (k + 1) * TC),
                               start=(k == 0), stop=(k == NMC - 1))
            return r

        def mm_msq(eng):
            r = None
            for k in range(NMC):
                r = eng.matmul(pbc[0:1, :], colc(msq_col, msq_col + 1),
                               tn(k * TC, (k + 1) * TC),
                               start=(k == 0), stop=(k == NMC - 1))
            return r
        emit("T", mm_mean, S=M.get("pstat_S", 0),
             V=max(M.get("pstat_V", 0), xV), G=xG)
        t2 = emit("T", mm_msq, S=max(s_sq, M.get("pbc_S", 0)))

        def _stats(eng):
            eng.tensor_copy(out=srow(0, 512), in_=pstat[0:1, :])
            eng.tensor_copy(out=srow(512, 1024), in_=pbc[0:1, :])
            eng.tensor_tensor(out=srow(1536, 2048), in0=srow(0, 512),
                              in1=srow(0, 512), op=OP.mult)
            if unbiased:
                eng.tensor_scalar(out=srow(1536, 2048), in0=srow(1536, 2048),
                                  scalar1=1024.0 / 1023.0, scalar2=None,
                                  op0=OP.mult)
            return eng.tensor_tensor(out=srow(512, 1024), in0=srow(512, 1024),
                                     in1=srow(1536, 2048), op=OP.subtract)
        v1 = emit("V", _stats, T=t2)
        M["pstat_V"] = v1
        if not unbiased:
            s1 = emit("S", lambda eng: (
                eng.activation(out=srow(1536, 2048), in_=srow(512, 1024),
                               func=AF.Ln, bias=fco(0, 1, 0, 1)),
                eng.activation(out=srow(1024, 1536), in_=srow(1536, 2048),
                               func=AF.Exp, scale=fco(1, 2, 0, 1)))[-1], V=v1)
            v3 = emit("V", lambda eng: eng.nop(), S=s1)
        else:
            s1 = emit("S", lambda eng: (
                eng.activation(out=srow(1536, 2048), in_=srow(512, 1024),
                               func=AF.Ln),
                eng.activation(out=srow(1536, 2048), in_=srow(1536, 2048),
                               func=AF.Exp, scale=fco(2, 3, 0, 1)))[-1], V=v1)
            v2 = emit("V", lambda eng: eng.tensor_scalar(
                out=srow(1536, 2048), in0=srow(1536, 2048), scalar1=eps,
                scalar2=None, op0=OP.add), S=s1)
            v3 = emit("V", lambda eng: eng.reciprocal(
                out=srow(1024, 1536), in_=srow(1536, 2048)), V=v2)
        v4 = emit("V", lambda eng: (
            eng.tensor_copy(out=srowb(0, 512), in_=srow(0, 512)),
            eng.tensor_copy(out=srowb(512, 1024), in_=srow(1024, 1536)))[-1],
            V=v3)
        t3 = emit("T", lambda eng: eng.matmul(
            pbc[:, :], onerow(), srowb(0, 512), start=True, stop=True),
            V=v4, S=M.get("pbc_S", 0))
        s2 = emit("S", lambda eng: eng.activation(out=bcm(), in_=pbc[:, :],
                                                  func=AF.Copy), T=t3)
        t4 = emit("T", lambda eng: eng.matmul(
            pbc[:, :], onerow(), srowb(512, 1024), start=True, stop=True),
            S=s2)
        s3 = emit("S", lambda eng: eng.activation(out=bcr(), in_=pbc[:, :],
                                                  func=AF.Copy), T=t4)
        M["pbc_S"] = s3
        return s3

    def pm_wait(b):
        return dict(S=M.get(f"pmm{b}_S", 0), V=M.get(f"pmm{b}_V", 0))

    def stream_mm(stash, total_cols, slice_cols, k_cnt, rhs_view, evac_fn,
                  rhs_V, rhs_S=0):
        nsl = total_cols // slice_cols
        mper = slice_cols // 128
        wb = [wbuf0, wbuf1]
        dwait = {}
        for e_s in range(min(2, nsl)):
            dwait[e_s] = emit(
                ["W0", "W1"][e_s % 2],
                lambda eng, e_s=e_s: eng.dma_start(
                    out=wb[e_s % 2](0, k_cnt * slice_cols)
                    .rearrange("p (a t) -> p a t", t=slice_cols),
                    in_=stash.ap()[:, :, e_s * slice_cols:
                                   (e_s + 1) * slice_cols]
                    .rearrange("a p t -> p a t")),
                inc=16, T=M.get(f"wb{e_s % 2}_T", 0), G=g_mamba_w)
        s_last = 0
        for e_s in range(nsl):
            t_mm = 0
            for m in range(mper):
                bi = M["mi"] % 2
                bank = [pmm0, pmm1][bi]
                pw = pm_wait(bi)

                def mm(eng, e_s=e_s, m=m, bank=bank):
                    r = None
                    for k in range(k_cnt):
                        r = eng.matmul(
                            bank[:, :],
                            wb[e_s % 2](k * slice_cols + m * 128,
                                        k * slice_cols + (m + 1) * 128),
                            rhs_view(k * TC, (k + 1) * TC),
                            start=(k == 0), stop=(k == k_cnt - 1))
                    return r
                t_mm = emit("T", mm, V=max(rhs_V, pw["V"]),
                            S=max(pw["S"], rhs_S),
                            **{["W0", "W1"][e_s % 2]: dwait[e_s]})
                mc = e_s * mper + m
                s_last = emit("S", lambda eng, mc=mc, bank=bank:
                              evac_fn(eng, mc, bank), T=t_mm)
                M[f"pmm{bi}_S"] = s_last
                M["mi"] += 1
            M[f"wb{e_s % 2}_T"] = t_mm
            nxt = e_s + 2
            if nxt < nsl:
                dwait[nxt] = emit(
                    ["W0", "W1"][nxt % 2],
                    lambda eng, nxt=nxt: eng.dma_start(
                        out=wb[nxt % 2](0, k_cnt * slice_cols)
                        .rearrange("p (a t) -> p a t", t=slice_cols),
                        in_=stash.ap()[:, :, nxt * slice_cols:
                                       (nxt + 1) * slice_cols]
                        .rearrange("a p t -> p a t")),
                    inc=16, T=M.get(f"wb{nxt % 2}_T", 0))
        return s_last

    for c in range(KCH if KPH >= 1 else 0):
        c0 = c * TC
        prevT = M.get("chunk_T", 0)
        prevV = M.get("chunk_V", 0)
        prevS = M.get("chunk_S", 0)
        g_x = dma("G", xs().rearrange("p (a t) -> p a t", t=TC),
                  x_in_h.ap()[:, c0:c0 + TC].rearrange("(a p) t -> p a t",
                                                       p=128),
                  T=prevT, V=max(prevV, v_mod if c == 0 else 0))
        s_sq = emit("S", lambda eng: eng.activation(out=tn(), in_=xs(),
                                                    func=AF.Square),
                    G=g_x, T=prevT, V=prevV, S=prevS)
        s_bc = stat_ln(xs, s_sq, 1, False, 1e-5, xG=g_x)

        def _norm(src):
            def f(eng, src=src):
                bm = bcast3(bcm)
                br = bcast3(bcr)
                t3 = tn().rearrange("p (a t) -> p a t", t=TC)
                x3 = src().rearrange("p (a t) -> p a t", t=TC)
                eng.tensor_tensor(out=t3, in0=x3, in1=bm, op=OP.subtract)
                return eng.tensor_tensor(out=t3, in0=t3, in1=br, op=OP.mult)
            return f
        v_n1 = emit("V", _norm(xs), S=s_bc)

        def _hmod(eng):
            r = None
            for dc in range(NMC):
                r = eng.tensor_scalar(
                    out=hmod(dc * TC, (dc + 1) * TC),
                    in0=tn(dc * TC, (dc + 1) * TC),
                    scalar1=cac(dc, dc + 1), scalar2=modc(dc, dc + 1),
                    op0=OP.mult, op1=OP.add)
            return r
        v_hm = emit("V", _hmod, V=v_n1, T=prevT)

        if KSUB < 1:
            continue
        def ev_xz(eng, mc, bank):
            if mc < 16:
                o = xin(mc * (TC + 3) + 3, mc * (TC + 3) + 3 + TC)
            else:
                o = zs((mc - 16) * TC, (mc - 15) * TC)
            return eng.activation(out=o, in_=bank[:, :], func=AF.Identity,
                                  bias=vb_in(mc, mc + 1))
        s_xz = stream_mm(st_Win, 2 * DI, 256, NMC, hmod, ev_xz, v_hm)

        if KSUB < 2:
            continue
        v_rt = emit("V", lambda eng: eng.tensor_copy(
            out=xin().rearrange("p (a t) -> p a t", t=TC + 3)[:, :, 0:3],
            in_=xtail().rearrange("p (a t) -> p a t", t=3)),
            S=s_xz, V=v_tail0 if c == 0 else 0)

        def _conv(eng):
            r = None
            for dc in range(NDC):
                base = dc * (TC + 3)
                eng.tensor_scalar(
                    out=xc(dc * TC, (dc + 1) * TC),
                    in0=xin(base, base + TC),
                    scalar1=vconvw(dc * 4, dc * 4 + 1),
                    scalar2=vconvb(dc, dc + 1), op0=OP.mult, op1=OP.add)
                for k2 in (1, 2, 3):
                    r = eng.scalar_tensor_tensor(
                        out=xc(dc * TC, (dc + 1) * TC),
                        in0=xin(base + k2, base + k2 + TC),
                        scalar=vconvw(dc * 4 + k2, dc * 4 + k2 + 1),
                        in1=xc(dc * TC, (dc + 1) * TC),
                        op0=OP.mult, op1=OP.add)
            return r
        v_cv = emit("V", _conv, V=v_rt)
        v_tail = emit("V", lambda eng: eng.tensor_copy(
            out=xtail().rearrange("p (a t) -> p a t", t=3),
            in_=xin().rearrange("p (a t) -> p a t", t=TC + 3)[:, :, TC:TC + 3]),
            V=v_cv)
        s_si = emit("S", lambda eng: eng.activation(out=xc(), in_=xc(),
                                                    func=AF.Silu), V=v_cv)
        s_zsil = emit("S", lambda eng: eng.activation(out=zs(), in_=zs(),
                                                      func=AF.Silu), S=s_xz,
                      V=M.get("zs_V", 0))

        if KSUB < 3:
            continue
        def mm_xp(eng):
            r = None
            for k in range(NDC):
                r = eng.matmul(pstat[0:32, :], wxp(k * 32, (k + 1) * 32),
                               xc(k * TC, (k + 1) * TC),
                               start=(k == 0), stop=(k == NDC - 1))
            return r
        t_xp = emit("T", mm_xp, S=max(s_si, M.get("pstat_S", 0)),
                    V=M.get("pstat_V", 0), G=g_mamba_w if c == 0 else 0)
        s_bcr2 = emit("S", lambda eng: eng.activation(
            out=bcrow(), in_=pstat[0:32, :], func=AF.Identity, bias=vb_xp()),
            T=t_xp)
        M["pstat_S"] = s_bcr2

        if KSUB < 4:
            continue
        def ev_dt(eng, mc, bank):
            eng.activation(out=spl(), in_=bank[:, :], func=AF.Exp,
                           bias=vb_dt(mc, mc + 1))
            return eng.activation(out=dl(mc * TC, (mc + 1) * TC),
                                  in_=spl(), func=AF.Ln, bias=1.0)
        s_dt = stream_mm(st_Wdt, DI, 128, NDC, xc, ev_dt,
                         max(v_cv, v_tail), rhs_S=s_si)

        if KSUB < 5:
            continue
        v_dxc = emit("V", lambda eng: eng.tensor_tensor(
            out=dxc(), in0=dl(), in1=xc(), op=OP.mult), S=s_dt)

        def _yinit(eng):
            r = None
            for dc in range(NDC):
                r = eng.tensor_scalar(out=ys(dc * TC, (dc + 1) * TC),
                                      in0=xc(dc * TC, (dc + 1) * TC),
                                      scalar1=vD(dc, dc + 1), scalar2=None,
                                      op0=OP.mult)
            return r
        v_yi = emit("V", _yinit, V=v_dxc)

        if KSUB < 6:
            continue
        v_sc = v_yi
        for sg in range(DS // SG):
            Bbv, Cbv = Bb0, Cb0
            bbfree = v_sc
            s_bb = M.get("pbc_S", 0)
            for s4 in range(SG):
                s_idx = sg * SG + s4
                t_b = emit("T", lambda eng, s_idx=s_idx: eng.matmul(
                    pbc[:, :], rowsel(s_idx * 128, (s_idx + 1) * 128),
                    bcrow(), start=True, stop=True),
                    S=max(s_bcr2, s_bb),
                    V=bbfree if s4 == 0 else 0)
                s_bb = emit("S", lambda eng, s4=s4, Bbv=Bbv: eng.activation(
                    out=Bbv(s4 * TC, (s4 + 1) * TC), in_=pbc[:, :],
                    func=AF.Copy), T=t_b)
                t_cq = emit("T", lambda eng, s_idx=s_idx: eng.matmul(
                    pbc[:, :], rowsel((16 + s_idx) * 128, (17 + s_idx) * 128),
                    bcrow(), start=True, stop=True), S=s_bb)
                s_bb = emit("S", lambda eng, s4=s4, Cbv=Cbv: eng.activation(
                    out=Cbv(s4 * TC, (s4 + 1) * TC), in_=pbc[:, :],
                    func=AF.Copy), T=t_cq)
            M["pbc_S"] = s_bb
            for dc in range(NDC):
                dAv = dA0 if dc % 2 == 0 else dA1
                dBxv = dBx0 if dc % 2 == 0 else dBx1
                dafree = M.get(f"da{dc % 2}_V", 0)

                def _dA(eng, dc=dc, sg=sg, dAv=dAv):
                    r = None
                    for s4 in range(SG):
                        r = eng.activation(
                            out=dAv(s4 * TC, (s4 + 1) * TC),
                            in_=dl(dc * TC, (dc + 1) * TC), func=AF.Exp,
                            scale=vA(dc * DS + sg * SG + s4,
                                     dc * DS + sg * SG + s4 + 1))
                    return r
                s_dA = emit("S", _dA, S=s_bb, V=dafree)

                def _scan(eng, dc=dc, sg=sg, dAv=dAv, dBxv=dBxv, Bbv=Bbv):
                    eng.tensor_tensor(
                        out=dBxv().rearrange("p (a t) -> p a t", t=TC),
                        in0=dxc(dc * TC, (dc + 1) * TC).unsqueeze(1)
                        .broadcast_to([128, SG, TC]),
                        in1=Bbv().rearrange("p (a t) -> p a t", t=TC),
                        op=OP.mult)
                    r = None
                    for s4 in range(SG):
                        r = eng.tensor_tensor_scan(
                            out=Hsb(s4 * TC, (s4 + 1) * TC),
                            data0=dAv(s4 * TC, (s4 + 1) * TC),
                            data1=dBxv(s4 * TC, (s4 + 1) * TC),
                            initial=state(dc * DS + sg * SG + s4,
                                          dc * DS + sg * SG + s4 + 1),
                            op0=OP.mult, op1=OP.add)
                    return r
                v_mid = emit("V", _scan, S=s_dA,
                             T=M.get(f"yac{dc % 2}_T", 0))

                def _rdout(eng, dc=dc, sg=sg, dBxv=dBxv, Cbv=Cbv):
                    eng.tensor_copy(
                        out=state(dc * DS + sg * SG, dc * DS + sg * SG + SG),
                        in_=Hsb().rearrange("p (a t) -> p a t", t=TC)
                        [:, :, TC - 1:TC])
                    return eng.tensor_tensor(out=dBxv(), in0=Hsb(), in1=Cbv(),
                                             op=OP.mult)
                v_sc = emit("V", _rdout, V=v_mid)
                M[f"da{dc % 2}_V"] = v_sc
                ybank = [pmm0, pmm1][dc % 2]

                def _yacc(eng, dc=dc, sg=sg, dBxv=dBxv, ybank=ybank):
                    r = None
                    if sg == 0:
                        r = eng.matmul(ybank[:, :], identb(),
                                       ys(dc * TC, (dc + 1) * TC),
                                       start=True, stop=False)
                    for s4 in range(SG):
                        r = eng.matmul(ybank[:, :], identb(),
                                       dBxv(s4 * TC, (s4 + 1) * TC),
                                       start=False,
                                       stop=(sg == DS // SG - 1 and
                                             s4 == SG - 1))
                    return r
                t_ya = emit("T", _yacc, V=v_sc,
                            S=M.get(f"yev{dc % 2}_S", 0) if sg == 0 else 0)
                M[f"yac{dc % 2}_T"] = t_ya
                if sg == DS // SG - 1:
                    s_ye = emit("S", lambda eng, dc=dc, ybank=ybank:
                                eng.activation(out=ys(dc * TC, (dc + 1) * TC),
                                               in_=ybank[:, :], func=AF.Copy),
                                T=t_ya)
                    M[f"yev{dc % 2}_S"] = s_ye
                    M["ys_S"] = s_ye

        if KSUB < 7:
            continue
        v_g = emit("V", lambda eng: eng.tensor_tensor(
            out=ys(), in0=ys(), in1=zs(), op=OP.mult), V=v_sc,
            S=max(s_zsil, M.get("ys_S", 0)))
        M["zs_V"] = v_g
        M["pmm0_S"] = M.get("yev0_S", 0)
        M["pmm1_S"] = M.get("yev1_S", 0)

        def ev_o(eng, mc, bank):
            return eng.activation(out=outs(mc * TC, (mc + 1) * TC),
                                  in_=bank[:, :], func=AF.Identity,
                                  bias=vb_out(mc, mc + 1))
        s_o = stream_mm(st_Wout, D, 128, NDC, ys, ev_o, v_g)

        if KSUB < 8:
            continue
        v_r2 = emit("V", lambda eng: eng.tensor_tensor(
            out=hmod(), in0=hmod(), in1=outs(), op=OP.add), S=s_o)
        s_sq2 = emit("S", lambda eng: eng.activation(out=tn(), in_=hmod(),
                                                     func=AF.Square), V=v_r2)
        s_bc2 = stat_ln(hmod, s_sq2, 2, True, 1e-6, xV=v_r2)
        v_n2 = emit("V", _norm(hmod), S=s_bc2)

        def _lnfin(eng):
            r = None
            for dc in range(NMC):
                eng.tensor_scalar(out=outs(dc * TC, (dc + 1) * TC),
                                  in0=tn(dc * TC, (dc + 1) * TC),
                                  scalar1=valpha(dc, dc + 1),
                                  scalar2=vbeta(dc, dc + 1),
                                  op0=OP.mult, op1=OP.add)
                r = eng.scalar_tensor_tensor(
                    out=xs(dc * TC, (dc + 1) * TC),
                    in0=outs(dc * TC, (dc + 1) * TC),
                    scalar=modc(16 + dc, 17 + dc),
                    in1=xs(dc * TC, (dc + 1) * TC), op0=OP.mult, op1=OP.add)
            return r
        v_xu = emit("V", _lnfin, V=v_n2)

        s_sq3 = emit("S", lambda eng: eng.activation(out=tn(), in_=xs(),
                                                     func=AF.Square), V=v_xu)
        s_bc3 = stat_ln(xs, s_sq3, 1, False, 1e-5, xV=v_xu)
        v_n3 = emit("V", _norm(xs), S=s_bc3)

        def _h2(eng):
            r = None
            for dc in range(NMC):
                r = eng.tensor_scalar(
                    out=hmod(dc * TC, (dc + 1) * TC),
                    in0=tn(dc * TC, (dc + 1) * TC),
                    scalar1=cbc(dc, dc + 1), scalar2=modc(24 + dc, 25 + dc),
                    op0=OP.mult, op1=OP.add)
            return r
        v_h2 = emit("V", _h2, V=v_n3)

        if KSUB < 9:
            continue
        def mm_rt(eng):
            r = None
            for k in range(NMC):
                r = eng.matmul(pstat[0:8, :], wrt(k * 8, (k + 1) * 8),
                               hmod(k * TC, (k + 1) * TC),
                               start=(k == 0), stop=(k == NMC - 1))
            return r
        t_rt = emit("T", mm_rt, V=max(v_h2, M.get("pstat_V", 0)),
                    S=M.get("pstat_S", 0), G=g_mamba_w if c == 0 else 0)
        s_l8 = emit("S", lambda eng: eng.activation(
            out=l8(), in_=pstat[0:8, :], func=AF.Identity, bias=vrb()), T=t_rt)
        M["pstat_S"] = s_l8
        t_lt = 0
        for tb in range(4):
            t_lt = emit("T", lambda eng, tb=tb: eng.transpose(
                ptrf[:, tb * 8:(tb + 1) * 8], l8(tb * 128, (tb + 1) * 128),
                identf(0, 8, 0, 8)), S=s_l8,
                V=M.get("ptrf_V", 0) if tb == 0 else 0)
        v_lt = emit("V", lambda eng, c=c: eng.tensor_copy(
            out=ltm(c * 32, (c + 1) * 32), in_=ptrf[:, 0:32]), T=t_lt)
        M["ptrf_V"] = v_lt

        ev = [evrow0, evrow1]
        evx = [evrow2, evrow3]
        for tb in range(4):
            t_h = 0
            for mc2 in range(NMC):
                t_h = emit("T", lambda eng, tb=tb, mc2=mc2: eng.transpose(
                    ptr0[:, mc2 * 128:(mc2 + 1) * 128],
                    hmod(mc2 * TC + tb * 128, mc2 * TC + (tb + 1) * 128),
                    identb()),
                    V=max(v_h2, M.get("ptr0_V", 0)) if mc2 == 0 else 0,
                    **{["E0", "E1"][tb % 2]:
                       M.get(f"evh{tb % 2}_D", 0) if mc2 == 0 else 0})
            v_eh = emit("V", lambda eng, tb=tb: eng.tensor_copy(
                out=ev[tb % 2](), in_=ptr0[:, :]), T=t_h)
            M["ptr0_V"] = v_eh
            M[f"evh{tb % 2}_D"] = emit(
                ["E0", "E1"][tb % 2],
                lambda eng, tb=tb, c0=c0: eng.dma_start(
                    out=h2stage.ap()[c0 + tb * 128:c0 + (tb + 1) * 128, :],
                    in_=ev[tb % 2]()), inc=16, V=v_eh)
            t_x = 0
            for mc2 in range(NMC):
                t_x = emit("T", lambda eng, tb=tb, mc2=mc2: eng.transpose(
                    ptr1[:, mc2 * 128:(mc2 + 1) * 128],
                    xs(mc2 * TC + tb * 128, mc2 * TC + (tb + 1) * 128),
                    identb()),
                    V=max(v_xu, M.get("ptr1_V", 0)) if mc2 == 0 else 0,
                    **{["E2", "E3"][tb % 2]:
                       M.get(f"evx{tb % 2}_D", 0) if mc2 == 0 else 0})
            v_ex = emit("V", lambda eng, tb=tb: eng.tensor_copy(
                out=evx[tb % 2](), in_=ptr1[:, :]), T=t_x)
            M["ptr1_V"] = v_ex
            M[f"evx{tb % 2}_D"] = emit(
                ["E2", "E3"][tb % 2],
                lambda eng, tb=tb, c0=c0: eng.dma_start(
                    out=xmid.ap()[c0 + tb * 128:c0 + (tb + 1) * 128, :],
                    in_=evx[tb % 2]()), inc=16, V=v_ex)
        c_agc = emit("C", lambda eng, c=c, c0=c0: eng.collective_compute(
            "AllGather", OP.bypass, replica_groups=[list(range(8))],
            ins=[h2stage.ap()[c0:c0 + TC, :].opt()],
            outs=[h2ag.ap()[c].opt()]),
            E0=cnt["E0"], E1=cnt["E1"])
        M["h2cp_G"] = dma(
            "G", h2_all.ap().rearrange("(r u) d -> r u d", u=T)[:, c0:c0 + TC, :],
            h2ag.ap()[c].rearrange("(r u) d -> r u d", u=TC), C=c_agc)
        M["chunk_T"] = cnt["T"]
        M["chunk_V"] = cnt["V"]
        M["chunk_S"] = cnt["S"]

    d_stage = cnt["D"]
    barT, barV, barS = cnt["T"], cnt["V"], cnt["S"]
    if KPH < 2:
        d_final = cnt["D"]
        emit("D", lambda eng: None, inc=0, D=d_final, G=cnt["G"], T=cnt["T"],
             V=cnt["V"], S=cnt["S"])
        return _finish(nc, sched)

    g_h2cp = M.get("h2cp_G", 0)

    s_exp = emit("S", lambda eng: eng.activation(out=gtm(), in_=ltm(),
                                                 func=AF.Exp),
                 V=barV, T=barT)

    g3v = lambda: gtm().rearrange("p (a b) -> p a b", b=8)
    v_sm = emit("V", lambda eng: eng.tensor_reduce(
        out=rsc2(), in_=g3v(), axis=AX.X, op=OP.add), S=s_exp)
    v_sm = emit("V", lambda eng: eng.reciprocal(out=rsc2(), in_=rsc2()),
                V=v_sm)
    v_sm = emit("V", lambda eng: eng.tensor_tensor(
        out=g3v(), in0=g3v(), in1=rsc2().unsqueeze(2)
        .broadcast_to([128, 32, 8]), op=OP.mult), V=v_sm)

    def maskpass(srcv, dst, v_in):
        s3 = lambda: srcv().rearrange("p (a b) -> p a b", b=8)
        d3 = lambda: dst().rearrange("p (a b) -> p a b", b=8)
        r0 = lambda: rsc0().rearrange("p (a b) -> p a b", b=8)
        r1 = lambda: rsc1().rearrange("p (a b) -> p a b", b=8)
        v = emit("V", lambda eng: eng.tensor_reduce(
            out=rsc2(), in_=s3(), axis=AX.X, op=OP.max), V=v_in)
        v = emit("V", lambda eng: eng.tensor_tensor(
            out=d3(), in0=s3(), in1=rsc2().unsqueeze(2)
            .broadcast_to([128, 32, 8]), op=OP.is_ge), V=v)
        v = emit("V", lambda eng: eng.tensor_copy(out=r0(), in_=d3()), V=v)
        for sh, (sa, sd) in zip((1, 2, 4), ((r0, r1), (r1, r0), (r0, r1))):
            v = emit("V", lambda eng, sh=sh, sa=sa, sd=sd: (
                eng.tensor_copy(out=sd()[:, :, 0:sh], in_=sa()[:, :, 0:sh]),
                eng.tensor_tensor(out=sd()[:, :, sh:8], in0=sa()[:, :, sh:8],
                                  in1=sa()[:, :, 0:8 - sh], op=OP.add))[-1],
                V=v)
        v = emit("V", lambda eng: eng.tensor_scalar(
            out=rsc0(), in0=rsc1(), scalar1=1.0, scalar2=None, op0=OP.is_le),
            V=v)
        v = emit("V", lambda eng: eng.tensor_tensor(
            out=dst(), in0=dst(), in1=rsc0(), op=OP.mult), V=v)
        return v
    v_tk = maskpass(lambda: ltm(), m1c, v_sm)
    v_tk = emit("V", lambda eng: eng.scalar_tensor_tensor(
        out=gmslab(0, 256), in0=m1c(), scalar=-1000.0, in1=ltm(),
        op0=OP.mult, op1=OP.add), V=v_tk)
    v_tk = maskpass(lambda: gmslab(0, 256), m2c, v_tk)

    def _gmpack(eng):
        g4 = gmslab().rearrange("p (a b) -> p a b", b=16)
        eng.tensor_copy(out=g4[:, :, 0:8],
                        in_=gtm().rearrange("p (a b) -> p a b", b=8))
        return eng.tensor_tensor(
            out=g4[:, :, 8:16],
            in0=m1c().rearrange("p (a b) -> p a b", b=8),
            in1=m2c().rearrange("p (a b) -> p a b", b=8), op=OP.add)
    v_gm = emit("V", _gmpack, V=v_tk)

    for j in range(32):
        t_g = emit("T", lambda eng, j=j: eng.transpose(
            ptrf[0:16, 0:128], gmslab(j * 16, (j + 1) * 16), identf()),
            V=v_gm if j == 0 else 0, S=M.get("gmev_S", 0))
        s_ge = emit("S", lambda eng, j=j: eng.activation(
            out=gmev((j % 4) * 128, (j % 4 + 1) * 128), in_=ptrf[0:16, 0:128],
            func=AF.Copy), T=t_g,
            E0=M.get("gmev_D", 0) if j % 4 == 0 else 0)
        M["gmev_S"] = s_ge
        if j % 4 == 3:
            M["gmev_D"] = emit("E0", lambda eng, j=j: eng.dma_start(
                out=gm_in.ap()[:, (j - 3) * 128:(j + 1) * 128],
                in_=gmev()), inc=16, S=s_ge)
    d_gm = cnt["D"]

    c_gm = emit("C", lambda eng: eng.collective_compute(
        "AllGather", OP.bypass, replica_groups=[list(range(8))],
        ins=[gm_in.ap().opt()], outs=[gm_all.ap().opt()]),
        E0=M.get("gmev_D", 0))

    d_asm = 0
    for e_i in range(E):
        for s1 in range(2):
            p0_ = e_i * 16 + s1 * 8
            src = gm_all.ap().rearrange("(cc r) t -> cc r t", r=16)
            d_asm = emit("PO", lambda eng, e_i=e_i, s1=s1, p0_=p0_, src=src:
                         eng.dma_start(
                             out=maskg(0, 2048, p0_, p0_ + 8),
                             in_=src[:, 8 + e_i, s1 * 2048:(s1 + 1) * 2048]),
                         inc=16,
                         C=c_gm, V=barV if (e_i == 0 and s1 == 0) else 0,
                         T=barT if (e_i == 0 and s1 == 0) else 0,
                         S=barS if (e_i == 0 and s1 == 0) else 0)

    v_ones = emit("V", lambda eng: eng.memset(offsg(), 1.0), PO=d_asm)
    v_cum = emit("V", lambda eng: eng.tensor_tensor_scan(
        out=cumg(), data0=offsg(), data1=maskg(), initial=0.0,
        op0=OP.mult, op1=OP.add), V=v_ones)
    v_tot = emit("V", lambda eng: eng.tensor_copy(out=totc(),
                                                  in_=cumg(2047, 2048)),
                 V=v_cum)
    t_off = emit("T", lambda eng: eng.matmul(pstat[:, 0:1], selmat(), totc(),
                                             start=True, stop=True),
                 V=max(v_tot, M.get("pstat_V", 0)), S=M.get("pstat_S", 0))
    v_offc = emit("V", lambda eng: eng.tensor_copy(out=offc(),
                                                   in_=pstat[:, 0:1]),
                  T=t_off)
    M["pstat_V"] = v_offc

    def _posval(eng):
        eng.tensor_scalar(out=posg(), in0=cumg(), scalar1=offc(),
                          scalar2=-1.0, op0=OP.add, op1=OP.add)
        eng.tensor_scalar(out=cumg(), in0=posg(), scalar1=float(CAP),
                          scalar2=None, op0=OP.is_lt)
        eng.tensor_tensor(out=maskg(), in0=maskg(), in1=cumg(), op=OP.mult)
        eng.tensor_scalar(out=offsg(), in0=posg(), scalar1=-float(CAP),
                          scalar2=None, op0=OP.add)
        eng.tensor_tensor(out=offsg(), in0=offsg(), in1=maskg(), op=OP.mult)
        eng.tensor_scalar(out=offsg(), in0=offsg(), scalar1=float(CAP),
                          scalar2=None, op0=OP.add)
        eng.tensor_scalar(out=offsg(), in0=offsg(), scalar1=0.0,
                          scalar2=float(CAP), op0=OP.max, op1=OP.min)
        return eng.tensor_copy(out=offsi(), in_=offsg())
    v_pv = emit("V", _posval, V=v_offc)

    d_po = 0
    pos_r = pos_dram.ap().rearrange("(e s2 two g) c -> e s2 two (g c)", e=8, s2=8, two=2)
    off_r = offs_dram.ap().rearrange("(e s2 two g) c -> e s2 two (g c)", e=8, s2=8, two=2)
    for e_i in range(E):
        for s1 in range(2):
            p0_ = e_i * 16 + s1 * 8
            d_po = emit("PO", lambda eng, e_i=e_i, s1=s1, p0_=p0_:
                        eng.dma_start(out=pos_r[e_i, :, s1, :],
                                      in_=posg(0, 2048, p0_, p0_ + 8)),
                        inc=16, V=v_pv)
            d_po = emit("PO", lambda eng, e_i=e_i, s1=s1, p0_=p0_:
                        eng.dma_start(out=off_r[e_i, :, s1, :],
                                      in_=offsi(0, 2048, p0_, p0_ + 8)),
                        inc=16, V=v_pv)

    if KOFF == "host":
        g_oc = dma("G", offcol(), offhost_h.ap(), D=d_po)
    else:
        g_oc = emit("G", lambda eng: eng.indirect_dma_start(
            out=offcol(),
            out_offset=None,
            in_=offs_dram.ap(),
            in_offset=IndirectOffsetOnAxis(ap=myrows(), axis=0)),
            inc=16, PO=d_po)
    g_p0 = emit("G", lambda eng: eng.indirect_dma_start(
        out=posr0(), out_offset=None,
        in_=pos_dram.ap(),
        in_offset=IndirectOffsetOnAxis(ap=mysel0(), axis=0)),
        inc=16, PO=d_po, T=barT, V=barV)
    g_p1 = emit("G", lambda eng: eng.indirect_dma_start(
        out=posr1(), out_offset=None,
        in_=pos_dram.ap(),
        in_offset=IndirectOffsetOnAxis(ap=mysel1(), axis=0)),
        inc=16, PO=d_po)
    t_pt0 = emit("T", lambda eng: eng.transpose(ptrf[:, 0:128], posr0(),
                                                identf()), G=g_p0,
                 V=M.get("ptrf_V", 0), S=M.get("gmev_S", 0))
    v_pt0 = emit("V", lambda eng: eng.tensor_copy(out=postm(0, 128),
                                                  in_=ptrf[:, 0:128]),
                 T=t_pt0)
    t_pt1 = emit("T", lambda eng: eng.transpose(ptrf[:, 0:128], posr1(),
                                                identf()), G=g_p1, V=v_pt0)
    v_pt1 = emit("V", lambda eng: eng.tensor_copy(out=postm(128, 256),
                                                  in_=ptrf[:, 0:128]),
                 T=t_pt1)
    M["ptrf_V"] = v_pt1
    if KDBG:
        dma("D", dbg_xm.ap(), xmid.ap(), E2=cnt["E2"], E3=cnt["E3"])
        dma("D", dbg_h2s.ap(), h2stage.ap(), E0=cnt["E0"], E1=cnt["E1"])
        dma("D", dbg_ptm.ap(), postm(), V=v_pt1)
        dma("D", dbg_gtm.ap(), gtm(), V=v_pv)
        dma("D", dbg_oc.ap(), offcol(), G=g_oc)
        dma("D", dbg_off.ap(), offs_dram.ap().rearrange("(a b) c -> a (b c)", b=8), PO=d_po)
        dma("D", dbg_pos.ap(), posg(), V=v_pv)
        dma("D", dbg_msk.ap(), maskg(), V=v_pv)
        dma("D", dbg_ltm.ap(), ltm(), V=v_pv)
        dma("D", dbg_m1.ap(), m1c(), V=v_pv)

    if KPH < 3:
        d_final = cnt["D"]
        emit("D", lambda eng: None, inc=0, D=d_final, G=cnt["G"], T=cnt["T"],
             V=cnt["V"], S=cnt["S"],
             C=cnt["C"] if int(os.environ.get("KWH2", "0")) else 0)
        return _finish(nc, sched)
    d_w1 = dma("D", w1sb().rearrange("p (a m) -> p a m", m=DH),
               st_w1.ap().rearrange("a p m -> p a m"),
               T=barT, V=barV, S=barS, G=g_moe_w)
    d_w2 = dma("D", w2sb().rearrange("p (a m) -> p a m", m=D),
               st_w2.ap().rearrange("a p m -> p a m"))

    t_gbt = emit("T", lambda eng: eng.transpose(ptrf[0:8, 0:128],
                                                modc(40, 48), identf()),
                 V=max(v_pt1, barV), S=barS)
    v_gbr = emit("V", lambda eng: eng.tensor_copy(out=gbrow(),
                                                  in_=ptrf[0:8, 0:128]),
                 T=t_gbt)
    M["ptrf_V"] = v_gbr
    v_gbb = emit("V", lambda eng: (
        eng.tensor_copy(out=gbrowb(), in_=gbrow()),
        eng.tensor_copy(out=b2rowb(), in_=b2row()))[-1], V=v_gbr)
    for dc in range(8):
        t_bb = emit("T", lambda eng, dc=dc: eng.matmul(
            pbc[:, 0:128], rowsel(dc * 128, (dc + 1) * 128, 0, 8),
            gbrowb(), start=True, stop=True), V=v_gbb,
            S=M.get("pbc_S", 0))
        s_b2 = emit("S", lambda eng, dc=dc: eng.activation(
            out=gbbc(dc * 128, (dc + 1) * 128), in_=pbc[:, 0:128],
            func=AF.Copy), T=t_bb)
        t_bb = emit("T", lambda eng, dc=dc: eng.matmul(
            pbc[:, 0:128], rowsel(dc * 128, (dc + 1) * 128, 0, 8),
            b2rowb(), start=True, stop=True), S=s_b2)
        s_b2 = emit("S", lambda eng, dc=dc: eng.activation(
            out=b2bc(dc * 128, (dc + 1) * 128), in_=pbc[:, 0:128],
            func=AF.Copy), T=t_bb)
        M["pbc_S"] = s_b2

    h2r = h2_all.ap().rearrange("(a b) d -> a b d", b=256)
    for bt in range(32 if KP3 >= 1 else 0):
        buf_v = [scat0, scat1][bt % 2]
        sem_k = ["SA", "SB"][bt % 2]
        d_t = emit(sem_k, lambda eng, bt=bt, buf_v=buf_v: eng.dma_start(
            out=buf_v().rearrange("p (a d) -> p a d", d=D),
            in_=h2r[:, 8 * bt:8 * bt + 8, :]), inc=16,
            G=max(M.get(f"scat{bt % 2}_G", 0), g_bufz,
                  g_h2cp if bt < 2 else 0),
            V=v_pv if bt < 2 else 0, T=barT if bt < 2 else 0,
            PO=d_po if bt < 2 else 0)
        g_last = 0
        for q in range(8):
            j = 8 * bt + q
            g_last = emit(
                "G", lambda eng, j=j, q=q, buf_v=buf_v: eng.indirect_dma_start(
                    out=buf_dram.ap(),
                    out_offset=IndirectOffsetOnAxis(ap=offcol(j, j + 1),
                                                    axis=0),
                    in_=buf_v(q * D, (q + 1) * D), in_offset=None),
                inc=16, G=g_oc if j == 0 else 0,
                **{sem_k: d_t} if q == 0 else {})
        M[f"scat{bt % 2}_G"] = g_last
    g_scat = cnt["G"]

    NTT = CAP // 256
    for tt in range((NTT if KP3 >= 2 else 0)):
        d_bf = dma("D", bft().rearrange("p (a d) -> p a d", d=D),
                   buf_dram.ap()[0:CAP, :].rearrange("(a b) d -> a b d", b=128)
                   [2 * tt:2 * tt + 2, :, :].rearrange("a p d -> p a d"),
                   G=g_scat, T=M.get("bft_T", 0))
        t_bt = 0
        for dc in range(8):
            for tb in range(2):
                t_bt = emit("T", lambda eng, dc=dc, tb=tb: eng.transpose(
                    ptr0[:, tb * 128:(tb + 1) * 128],
                    bft(tb * 1024 + dc * 128, tb * 1024 + (dc + 1) * 128),
                    identb()),
                    D=d_bf if (dc == 0 and tb == 0) else 0,
                    V=M.get("ptr0_V", 0) if tb == 0 else 0)
            v_bf = emit("V", lambda eng, dc=dc: eng.tensor_copy(
                out=buffm(dc * 256, (dc + 1) * 256), in_=ptr0[:, 0:256]),
                T=t_bt)
            M["ptr0_V"] = v_bf
        M["bft_T"] = t_bt
        v_bfm = M["ptr0_V"]

        s_h1 = 0
        for hc in range(NHC):
            bi = M["mi"] % 2
            bank = [pmm0, pmm1][bi]
            pw = pm_wait(bi)

            def mm_h1(eng, hc=hc, bank=bank):
                r = None
                for k in range(NMC):
                    r = eng.matmul(bank[:, 0:256],
                                   w1sb(k * DH + hc * 128,
                                        k * DH + (hc + 1) * 128),
                                   buffm(k * 256, (k + 1) * 256),
                                   start=(k == 0), stop=(k == NMC - 1))
                return r
            t_h1 = emit("T", mm_h1, D=d_w1 if (tt == 0 and hc == 0) else 0,
                        V=max(v_bfm, pw["V"]), S=pw["S"])
            s_h1 = emit("S", lambda eng, hc=hc, bank=bank: eng.activation(
                out=h1sb(hc * 256, (hc + 1) * 256), in_=bank[:, 0:256],
                func=AF.Gelu, bias=vb1(hc, hc + 1)), T=t_h1)
            M[f"pmm{bi}_S"] = s_h1
            M["mi"] += 1

        for tb in range(2):
            t_o2 = 0
            for nh in range(2):
                pw = pm_wait(nh)

                def mm_o2(eng, tb=tb, nh=nh):
                    r = None
                    for k in range(NHC):
                        r = eng.matmul(
                            [pmm0, pmm1][nh][:, :],
                            h1sb(k * 256 + tb * 128, k * 256 + (tb + 1) * 128),
                            w2sb(k * D + nh * 512, k * D + (nh + 1) * 512),
                            start=(k == 0), stop=(k == NHC - 1))
                    return r
                t_o2 = emit("T", mm_o2, S=max(s_h1, pw["S"]), V=pw["V"],
                            D=d_w2 if (tt == 0 and tb == 0 and nh == 0) else 0)

            def _oev(eng):
                eng.tensor_tensor(out=otm(0, 512), in0=pmm0[:, :],
                                  in1=b2bc(0, 512), op=OP.add)
                return eng.tensor_tensor(out=otm(512, 1024), in0=pmm1[:, :],
                                         in1=b2bc(512, 1024), op=OP.add)
            v_oev = emit("V", _oev, T=t_o2, OT=M.get("otm_D", 0))
            M["pmm0_V"] = v_oev
            M["pmm1_V"] = v_oev
            M["pmm0_S"] = 0
            M["pmm1_S"] = 0
            M["otm_D"] = emit(
                "OT", lambda eng, tt=tt, tb=tb: eng.dma_start(
                    out=outstage.ap()[tt * 256 + tb * 128:
                                      tt * 256 + (tb + 1) * 128, :],
                    in_=otm()), inc=16, V=v_oev)
    d_ost = cnt["D"]

    if KP3 < 3:
        d_final = cnt["D"]
        emit("D", lambda eng: None, inc=0, D=d_final, G=cnt["G"], T=cnt["T"],
             V=cnt["V"], S=cnt["S"])
        return _finish(nc, sched)
    c_out = emit("C", lambda eng: eng.collective_compute(
        "AllGather", OP.bypass, replica_groups=[list(range(8))],
        ins=[outstage.ap().opt()], outs=[out_all.ap().opt()]),
        D=d_ost, OT=M.get("otm_D", 0))

    if KPH < 4:
        d_final = cnt["D"]
        emit("D", lambda eng: None, inc=0, D=d_final, G=cnt["G"], T=cnt["T"],
             V=cnt["V"], S=cnt["S"])
        return _finish(nc, sched)
    v_own = emit("V", lambda eng: eng.tensor_scalar(
        out=rsc0(), in0=postm(), scalar1=float(CAP), scalar2=None,
        op0=OP.is_lt), V=v_pt1)
    v_own = emit("V", lambda eng: eng.tensor_tensor(
        out=rsc0(), in0=gtm(), in1=rsc0(), op=OP.mult), V=v_own)

    def red(mask, other, wsum, psum_, esum, v_in):
        v = emit("V", lambda eng: eng.tensor_tensor(
            out=rsc1(), in0=mask(), in1=rsc0(), op=OP.mult), V=v_in)
        v = emit("V", lambda eng: eng.tensor_reduce(
            out=wsum(), in_=rsc1().rearrange("p (a b) -> p a b", b=8),
            axis=AX.X, op=OP.add), V=v)
        v = emit("V", lambda eng: eng.tensor_tensor(
            out=rsc1(), in0=mask(), in1=postm(), op=OP.mult), V=v)
        v = emit("V", lambda eng: eng.tensor_reduce(
            out=psum_(), in_=rsc1().rearrange("p (a b) -> p a b", b=8),
            axis=AX.X, op=OP.add), V=v)
        v = emit("V", lambda eng: eng.tensor_tensor(
            out=rsc1(), in0=mask(), in1=iota8(), op=OP.mult), V=v)
        v = emit("V", lambda eng: eng.tensor_reduce(
            out=esum(), in_=rsc1().rearrange("p (a b) -> p a b", b=8),
            axis=AX.X, op=OP.add), V=v)
        return v
    v_own = red(m1c, None, w0c, p0c, e0c, v_own)
    v_own = red(m2c, None, w1c, p1c, e1c, v_own)
    v_own = emit("V", lambda eng: (
        eng.tensor_scalar(out=p0c(), in0=p0c(), scalar1=float(CAP - 1),
                          scalar2=None, op0=OP.min),
        eng.tensor_scalar(out=p1c(), in0=p1c(), scalar1=float(CAP - 1),
                          scalar2=None, op0=OP.min))[-1], V=v_own)
    v_own = emit("V", lambda eng: (
        eng.scalar_tensor_tensor(out=p0c(), in0=e0c(), scalar=float(CAP),
                                 in1=p0c(), op0=OP.mult, op1=OP.add),
        eng.scalar_tensor_tensor(out=p1c(), in0=e1c(), scalar=float(CAP),
                                 in1=p1c(), op0=OP.mult, op1=OP.add))[-1],
        V=v_own)
    v_own = emit("V", lambda eng: (
        eng.tensor_scalar(out=p0c(), in0=p0c(), scalar1=0.0,
                          scalar2=float(E * CAP - 1), op0=OP.max, op1=OP.min),
        eng.tensor_scalar(out=p1c(), in0=p1c(), scalar1=0.0,
                          scalar2=float(E * CAP - 1), op0=OP.max,
                          op1=OP.min))[-1], V=v_own)
    v_own = emit("V", lambda eng: (
        eng.tensor_copy(out=r0i(), in_=p0c()),
        eng.tensor_copy(out=r1i(), in_=p1c()))[-1], V=v_own)

    v_cb = 0
    for j in range(32):
        g_g1 = 0
        if KJ4 >= 1:
            g_g0 = emit("G", lambda eng, j=j: eng.indirect_dma_start(
                out=g0v(), out_offset=None, in_=out_all.ap(),
                in_offset=IndirectOffsetOnAxis(ap=r0i(j, j + 1), axis=0)),
                inc=16, V=max(v_own, v_cb), C=c_out if KWC else 0)
            g_g1 = g_g0
        if KJ4 >= 2:
            g_g1 = emit("G", lambda eng, j=j: eng.indirect_dma_start(
                out=g1v(), out_offset=None, in_=out_all.ap(),
                in_offset=IndirectOffsetOnAxis(ap=r1i(j, j + 1), axis=0)),
                inc=16, V=max(v_own, v_cb))
        d_xt = dma("D", xtt(), xmid.ap()[j * 128:(j + 1) * 128, :],
                   V=v_cb, C=c_out if (j == 0 and KWC) else 0)

        def _comb(eng, j=j):
            if KJ4 >= 3:
                eng.tensor_scalar(out=yout(), in0=g0v(), scalar1=w0c(j, j + 1),
                                  scalar2=None, op0=OP.mult)
                eng.scalar_tensor_tensor(out=yout(), in0=g1v(),
                                         scalar=w1c(j, j + 1), in1=yout(),
                                         op0=OP.mult, op1=OP.add)
                eng.tensor_tensor(out=yout(), in0=yout(), in1=gbbc(),
                                  op=OP.mult)
                return eng.tensor_tensor(out=yout(), in0=yout(), in1=xtt(),
                                         op=OP.add)
            return eng.tensor_copy(out=yout(), in_=xtt())
        v_cb = emit("V", _comb, G=g_g1, D=d_xt,
                    V=v_own if j == 0 else 0)
        dma("D", out_h.ap()[j * 128:(j + 1) * 128, :], yout(), V=v_cb)

    d_final = cnt["D"]
    emit("D", lambda eng: None, inc=0, D=d_final, G=cnt["G"],
         SA=cnt["SA"], SB=cnt["SB"], PO=cnt["PO"], W0=cnt["W0"],
         W1=cnt["W1"], E0=cnt["E0"], E1=cnt["E1"], E2=cnt["E2"],
         E3=cnt["E3"], OT=cnt["OT"], WM=cnt["WM"])
    return _finish(nc, sched)


def kernel(**inputs):
    inp = {k: np.asarray(v) for k, v in inputs.items()}
    nc = build_nc()

    ident = np.eye(128)
    colconsts = np.zeros((128, 4), np.float32)
    colconsts[:, 0] = 1.0
    colconsts[:, 1] = 1.0 / 1024.0
    colconsts[:, 2] = 1.0 / 1023.0
    onerow = np.ones((1, 128), np.float32)
    sl = np.empty(16, np.int64)
    for s1 in range(2):
        for s2 in range(8):
            sl[s1 * 8 + s2] = s2 * 2 + s1
    selmat = np.zeros((128, 128), np.float32)
    for e_i in range(8):
        for a in range(16):
            for b_ in range(16):
                if sl[a] < sl[b_]:
                    selmat[e_i * 16 + a, e_i * 16 + b_] = 1.0
    rowsel = np.zeros((32, 32 * 128), np.float32)
    for s in range(32):
        rowsel[s, s * 128:(s + 1) * 128] = 1.0
    iota8 = np.tile(np.arange(8, dtype=np.float32), 32)[None, :].repeat(128, 0)

    x = inp["L0_attn_x"]
    c_all = inp["L0_c"]

    in_maps = []
    for ci in range(8):
        m = {
            "x_fm": np.ascontiguousarray(x[ci].T),
            "W_in": inp["L0_attn_W_in"],
            "b_inp": np.ascontiguousarray(inp["L0_attn_b_in"].reshape(32, 128).T),
            "conv_wp": np.ascontiguousarray(inp["L0_attn_conv_w"][:, 0, :].reshape(16, 128, 4).transpose(1, 0, 2).reshape(128, 64)),
            "conv_bp": np.ascontiguousarray(inp["L0_attn_conv_b"].reshape(16, 128).T),
            "W_xpp": np.ascontiguousarray(inp["L0_attn_W_xp"].reshape(16, 128, 32).transpose(1, 0, 2).reshape(128, 512)),
            "b_xp": inp["L0_attn_b_xp"],
            "W_dt": inp["L0_attn_W_dt"],
            "b_dtp": np.ascontiguousarray(inp["L0_attn_b_dt"].reshape(16, 128).T),
            "A_logp": np.ascontiguousarray(inp["L0_attn_A_log"].reshape(16, 128, 16).transpose(1, 0, 2).reshape(128, 256)),
            "D_skipp": np.ascontiguousarray(inp["L0_attn_D_skip"].reshape(16, 128).T),
            "W_out": inp["L0_attn_W_out"],
            "b_outp": np.ascontiguousarray(inp["L0_attn_b_out"].reshape(8, 128).T),
            "alphap": np.ascontiguousarray(inp["L0_attn_alpha"].reshape(8, 128).T),
            "betap": np.ascontiguousarray(inp["L0_attn_beta"].reshape(8, 128).T),
            "router_wp": np.ascontiguousarray(inp["L0_ffn_router_w"].reshape(8, 128, 8).transpose(1, 0, 2).reshape(128, 64)),
            "router_b": inp["L0_ffn_router_b"],
            "w1e": np.ascontiguousarray(inp["L0_ffn_w1"][ci]),
            "b1ep": np.ascontiguousarray(inp["L0_ffn_b1"][ci].reshape(32, 128).T),
            "w2e": np.ascontiguousarray(inp["L0_ffn_w2"][ci]),
            "b2e": np.ascontiguousarray(inp["L0_ffn_b2"][ci]),
            "c_row": np.ascontiguousarray(c_all[ci].reshape(8, 128).T),
            "wmod": inp["L0_wmod"],
            "identb": ident.astype(BF),
            "identf": ident.astype(np.float32),
            "colconsts": colconsts.astype(BF),
            "onerow": onerow.astype(BF),
            "selmat": selmat,
            "rowsel": rowsel.astype(BF),
            "iota8": iota8,
            "myrows": (np.arange(128, dtype=np.int32)
                       + ci * 128).reshape(128, 1),
            "mysel0": np.array([(p % 8) * 256 + ci * 32 + (p // 8)
                                for p in range(128)], np.int32).reshape(128, 1),
            "mysel1": np.array([(p % 8) * 256 + ci * 32 + 16 + (p // 8)
                                for p in range(128)], np.int32).reshape(128, 1),
            "fconst": np.tile(np.array([1e-5, -0.5, 0.5, 0.0],
                                       np.float32), (128, 1)),
            "offhost": np.random.default_rng(7).integers(
                0, 5121, (128, 256)).astype(np.int32),
        }
        in_maps.append(m)

    res = run_bass_kernel_spmd(nc, in_maps, core_ids=list(range(8)))
    out = np.stack([np.asarray(r["out"]) for r in res.results], 0)
    return out.astype(np.float32)


if __name__ == "__main__":
    import os
    import reference as R
    import jax
    with jax.default_device(jax.devices("cpu")[0]):
        inp = R.setup_inputs(0)
        inp = {k: np.asarray(v) for k, v in inp.items()}
        ref = np.asarray(jax.jit(R.reference)(**{k: R.jnp.asarray(v)
                                                 for k, v in inp.items()}))
    got = kernel(**inp)
    err = got - ref
    rvr = float((err ** 2).mean() / (ref ** 2).mean())
    print("resvar ratio:", rvr)


# version 13
# speedup vs baseline: 1.0784x; 1.0784x over previous
import numpy as np
import sys

sys.path.insert(0, "/opt/trn_rl_repo")
import concourse.bass as bass
import concourse.mybir as mybir
from concourse.alu_op_type import AluOpType as OP
from concourse.bass import IndirectOffsetOnAxis
from concourse.bass_utils import run_bass_kernel_spmd
import ml_dtypes

F32 = mybir.dt.float32
BF16 = mybir.dt.bfloat16
I32 = mybir.dt.int32
U8 = mybir.dt.uint8
AF = mybir.ActivationFunctionType
AX = mybir.AxisListType
BF = ml_dtypes.bfloat16

B, T, D = 8, 4096, 1024
DI, DS = 2048, 16
DH = 4096
E = 8
CAP = 5120
TC = 512
NCH = T // TC
NDC = DI // 128
NMC = D // 128
NHC = DH // 128
SG = 4
POOL_B = 204000
import os
KPH = int(os.environ.get("KPH", "4"))
KCH = int(os.environ.get("KCH", "8"))
KSUB = int(os.environ.get("KSUB", "99"))
KJ4 = int(os.environ.get("KJ4", "3"))
KWC = int(os.environ.get("KWC", "1"))
KP3 = int(os.environ.get("KP3", "9"))
KOFF = os.environ.get("KOFF", "dev")
KDBG = int(os.environ.get("KDBG", "0"))

_ESZ = {F32: 4, BF16: 2, I32: 4, U8: 1}


class View:
    def __init__(self, pool, off, n, dt, p0=0, p1=128):
        self.pool, self.off, self.n, self.dt = pool, off, n, dt
        self.p0, self.p1 = p0, p1
        self.esz = _ESZ[dt]

    def __call__(self, a=0, b=None, p0=None, p1=None):
        if b is None:
            b = self.n
        p0 = self.p0 if p0 is None else p0
        p1 = self.p1 if p1 is None else p1
        ap = self.pool[p0:p1, self.off + a * self.esz: self.off + b * self.esz]
        if self.dt != U8:
            ap = ap.bitcast(self.dt)
        return ap


def _finish(nc, sched):
    with nc.Block() as block:
        @block.sync
        def _(eng):
            for step in sched["D"]:
                step(eng)

        @block.vector
        def _(eng):
            for step in sched["V"]:
                step(eng)

        @block.scalar
        def _(eng):
            for step in sched["S"]:
                step(eng)

        @block.tensor
        def _(eng):
            for step in sched["T"]:
                step(eng)

        @block.gpsimd
        def _(eng):
            for step in sched["G"]:
                step(eng)
    return nc


def build_nc():
    nc = bass.Bass("TRN2", target_bir_lowering=False)

    def ein(name, shape, dt=F32):
        return nc.dram_tensor(name, shape, dt, kind="ExternalInput")

    x_in_h = ein("x_fm", [D, T])
    W_in_h = ein("W_in", [D, 2 * DI])
    b_in_h = ein("b_inp", [128, 32])
    convw_h = ein("conv_wp", [128, NDC * 4])
    convb_h = ein("conv_bp", [128, NDC])
    W_xp_h = ein("W_xpp", [128, NDC * 32])
    b_xp_h = ein("b_xp", [2 * DS])
    W_dt_h = ein("W_dt", [DI, DI])
    b_dt_h = ein("b_dtp", [128, NDC])
    A_log_h = ein("A_logp", [128, NDC * DS])
    Dsk_h = ein("D_skipp", [128, NDC])
    W_out_h = ein("W_out", [DI, D])
    b_out_h = ein("b_outp", [128, NMC])
    alpha_h = ein("alphap", [128, NMC])
    beta_h = ein("betap", [128, NMC])
    rw_h = ein("router_wp", [128, NMC * 8])
    rb_h = ein("router_b", [E])
    w1_h = ein("w1e", [D, DH])
    b1_h = ein("b1ep", [128, NHC])
    w2_h = ein("w2e", [DH, D])
    b2_h = ein("b2e", [D])
    c_h = ein("c_row", [128, 8])
    wmod_h = ein("wmod", [D, 6 * D])
    identb_h = ein("identb", [128, 128], BF16)
    identf_h = ein("identf", [128, 128], F32)
    cols_h = ein("colconsts", [128, 4], BF16)
    onerow_h = ein("onerow", [1, 128], BF16)
    selmat_h = ein("selmat", [128, 128], F32)
    rowsel_h = ein("rowsel", [32, 32 * 128], BF16)
    iota8_h = ein("iota8", [128, 256], F32)
    myrows_h = ein("myrows", [128, 1], I32)
    mysel0_h = ein("mysel0", [128, 1], I32)
    mysel1_h = ein("mysel1", [128, 1], I32)
    fconst_h = ein("fconst", [128, 4], F32)
    offhost_h = ein("offhost", [128, 256], I32)

    out_h = nc.dram_tensor("out", [T, D], F32, kind="ExternalOutput")
    if KDBG:
        dbg_oc = nc.dram_tensor("dbg_oc", [128, 256], I32, kind="ExternalOutput")
        dbg_off = nc.dram_tensor("dbg_off", [128, 2048], I32, kind="ExternalOutput")
        dbg_pos = nc.dram_tensor("dbg_pos", [128, 2048], F32, kind="ExternalOutput")
        dbg_msk = nc.dram_tensor("dbg_msk", [128, 2048], F32, kind="ExternalOutput")
        dbg_ltm = nc.dram_tensor("dbg_ltm", [128, 256], F32, kind="ExternalOutput")
        dbg_m1 = nc.dram_tensor("dbg_m1", [128, 256], F32, kind="ExternalOutput")
        dbg_xm = nc.dram_tensor("dbg_xm", [T, D], BF16, kind="ExternalOutput")
        dbg_h2s = nc.dram_tensor("dbg_h2s", [T, D], BF16, kind="ExternalOutput")
        dbg_ptm = nc.dram_tensor("dbg_ptm", [128, 256], F32, kind="ExternalOutput")
        dbg_gtm = nc.dram_tensor("dbg_gtm", [128, 256], F32, kind="ExternalOutput")

    st_Win = nc.dram_tensor("st_Win", [16, 128, 2048], BF16)
    st_Wdt = nc.dram_tensor("st_Wdt", [16, 128, 2048], BF16)
    st_Wout = nc.dram_tensor("st_Wout", [8, 128, 2048], BF16)
    st_w1 = nc.dram_tensor("st_w1", [NMC, 128, DH], BF16)
    st_w2 = nc.dram_tensor("st_w2", [NHC, 128, D], BF16)
    h2stage = nc.dram_tensor("h2stage", [T, D], BF16)
    xmid = nc.dram_tensor("xmid", [T, D], BF16)
    gm_in = nc.dram_tensor("gm_in", [16, T], F32)
    gm_all = nc.dram_tensor("gm_all", [128, T], F32)
    h2_all = nc.dram_tensor("h2_all", [B * T, D], BF16)
    pos_dram = nc.dram_tensor("pos_dram", [2048, 128], F32)
    offs_dram = nc.dram_tensor("offs_dram", [1024, 256], I32)
    buf_dram = nc.dram_tensor("buf_dram", [CAP + 1, D], BF16)
    outstage = nc.dram_tensor("outstage", [CAP, D], BF16)
    out_all = nc.dram_tensor("out_all", [E * CAP, D], BF16)

    import contextlib
    ctx = contextlib.ExitStack()
    semT = ctx.enter_context(nc.semaphore("semT"))
    semV = ctx.enter_context(nc.semaphore("semV"))
    semS = ctx.enter_context(nc.semaphore("semS"))
    semD = ctx.enter_context(nc.semaphore("semD"))
    semG = ctx.enter_context(nc.semaphore("semG"))
    semC = ctx.enter_context(nc.semaphore("semC"))
    semSA = ctx.enter_context(nc.semaphore("semSA"))
    semSB = ctx.enter_context(nc.semaphore("semSB"))
    semPO = ctx.enter_context(nc.semaphore("semPO"))
    semW0 = ctx.enter_context(nc.semaphore("semW0"))
    semW1 = ctx.enter_context(nc.semaphore("semW1"))
    semE0 = ctx.enter_context(nc.semaphore("semE0"))
    semE1 = ctx.enter_context(nc.semaphore("semE1"))
    semE2 = ctx.enter_context(nc.semaphore("semE2"))
    semE3 = ctx.enter_context(nc.semaphore("semE3"))
    semOT = ctx.enter_context(nc.semaphore("semOT"))
    semWM = ctx.enter_context(nc.semaphore("semWM"))
    SEM = {"T": semT, "V": semV, "S": semS, "D": semD, "G": semG,
           "C": semC, "SA": semSA, "SB": semSB, "PO": semPO,
           "W0": semW0, "W1": semW1, "E0": semE0, "E1": semE1,
           "E2": semE2, "E3": semE3, "OT": semOT, "WM": semWM}

    pool = ctx.enter_context(nc.sbuf_tensor("pool", [128, POOL_B], U8))
    offcol_t = ctx.enter_context(nc.sbuf_tensor("offcol_t", [128, 256], I32))
    myrows_t = ctx.enter_context(nc.sbuf_tensor("myrows_t", [128, 1], I32))
    mysel0_t = ctx.enter_context(nc.sbuf_tensor("mysel0_t", [128, 1], I32))
    mysel1_t = ctx.enter_context(nc.sbuf_tensor("mysel1_t", [128, 1], I32))
    r0i_t = ctx.enter_context(nc.sbuf_tensor("r0i_t", [128, 32], I32))
    r1i_t = ctx.enter_context(nc.sbuf_tensor("r1i_t", [128, 32], I32))
    posr0_t = ctx.enter_context(nc.sbuf_tensor("posr0_t", [128, 128], F32))
    posr1_t = ctx.enter_context(nc.sbuf_tensor("posr1_t", [128, 128], F32))
    g0v_t = ctx.enter_context(nc.sbuf_tensor("g0v_t", [128, 1024], BF16))
    g1v_t = ctx.enter_context(nc.sbuf_tensor("g1v_t", [128, 1024], BF16))

    class TW:
        def __init__(self, t):
            self.t = t
        def __call__(self, a=0, b=None, p0=0, p1=128):
            if b is None:
                return self.t[p0:p1, :]
            return self.t[p0:p1, a:b]
    _off = [0]

    def alloc(n, dt, p0=0, p1=128):
        off = _off[0]
        sz = (n * _ESZ[dt] + 31) & ~31
        _off[0] = off + sz
        assert _off[0] <= POOL_B, f"pool overflow {_off[0]}"
        return View(pool, off, n, dt, p0, p1)

    identb = alloc(128, BF16)
    identf = alloc(128, F32)
    colc = alloc(4, BF16)
    onerow = alloc(128, BF16, 0, 1)
    selmat = alloc(128, F32)
    rowsel = alloc(32 * 128, BF16, 0, 32)
    iota8 = alloc(256, F32)
    myrows = TW(myrows_t)
    mysel0 = TW(mysel0_t)
    mysel1 = TW(mysel1_t)
    vb_in = alloc(32, F32)
    vconvw = alloc(NDC * 4, F32)
    vconvb = alloc(NDC, F32)
    vb_xp = alloc(1, F32, 0, 32)
    vb_dt = alloc(NDC, F32)
    vA = alloc(NDC * DS, F32)
    vD = alloc(NDC, F32)
    vb_out = alloc(NMC, F32)
    valpha = alloc(NMC, F32)
    vbeta = alloc(NMC, F32)
    vb1 = alloc(NHC, F32)
    vrb = alloc(1, F32, 0, 8)
    csil = alloc(8, F32)
    modc = alloc(48, F32)
    cac = alloc(8, F32)
    cbc = alloc(8, F32)
    wxp = alloc(NDC * 32, BF16)
    wrt = alloc(NMC * 8, BF16)
    state = alloc(NDC * DS, F32)
    ltm = alloc(256, F32)
    gtm = alloc(256, F32)
    m1c = alloc(256, F32)
    m2c = alloc(256, F32)
    rsc0 = alloc(256, F32)
    rsc1 = alloc(256, F32)
    rsc2 = alloc(32, F32)
    rsc3 = alloc(32, F32)
    b2row = alloc(128, F32, 0, 8)
    b2rowb = alloc(128, BF16, 0, 8)
    gbrow = alloc(128, F32, 0, 8)
    gbrowb = alloc(128, BF16, 0, 8)
    b2bc = alloc(D, BF16)
    gbbc = alloc(D, BF16)
    postm = alloc(256, F32)
    offcol = TW(offcol_t)
    fco = alloc(4, F32)
    totc = alloc(1, F32)
    offc = alloc(1, F32)
    e0c = alloc(32, F32)
    p0c = alloc(32, F32)
    w0c = alloc(32, F32)
    e1c = alloc(32, F32)
    p1c = alloc(32, F32)
    w1c = alloc(32, F32)
    r0i = TW(r0i_t)
    r1i = TW(r1i_t)
    gmev = alloc(512, F32, 0, 16)
    gmslab = alloc(512, F32)
    zrow = alloc(1024, BF16)
    xtail = alloc(NDC * 3, BF16)
    persist_end = _off[0]

    _off[0] = (persist_end + 31) & ~31
    arena = _off[0]
    xs = alloc(NMC * TC, BF16)
    tn = alloc(NMC * TC, BF16)
    hmod = alloc(NMC * TC, BF16)
    xin = alloc(NDC * (TC + 3), BF16)
    zs = alloc(NDC * TC, BF16)
    xc = alloc(NDC * TC, BF16)
    dxc = alloc(NDC * TC, BF16)
    ys = alloc(NDC * TC, BF16)
    Bb0 = alloc(SG * TC, BF16)
    Cb0 = alloc(SG * TC, BF16)
    dA0 = alloc(SG * TC, BF16)
    dA1 = alloc(SG * TC, BF16)
    dBxb = alloc(SG * TC, BF16)
    Hsb = alloc(SG * TC, BF16)
    wbuf0 = alloc(2048, BF16)
    wbuf1 = alloc(2048, BF16)
    srow = alloc(2048, F32, 0, 1)
    srowb = alloc(1024, BF16, 0, 1)
    bcm = alloc(TC, BF16)
    bcr = alloc(TC, BF16)
    bcrow = alloc(TC, BF16, 0, 32)
    l8 = alloc(TC, F32, 0, 8)
    evrow0 = alloc(1024, BF16)
    evrow1 = alloc(1024, BF16)
    evrow2 = alloc(1024, BF16)
    evrow3 = alloc(1024, BF16)
    mamba_end = _off[0]
    spl = View(pool, bcm.off, 512, F32)
    dl = View(pool, xin.off, NDC * TC, BF16)
    outs = View(pool, zs.off, NMC * TC, BF16)
    wmscr = View(pool, xs.off, 6144, F32)
    modacc = View(pool, ys.off, 48, F32)

    _off[0] = arena
    w1sb = alloc(NMC * DH, BF16)
    w2sb = alloc(NHC * D, BF16)
    rbase = _off[0]
    maskg = alloc(2048, F32)
    cumg = alloc(2048, F32)
    posg = alloc(2048, F32)
    offsg = alloc(2048, F32)
    offsi = View(pool, maskg.off, 2048, I32)
    r1_end = _off[0]
    posr0 = TW(posr0_t)
    posr1 = TW(posr1_t)
    _off[0] = rbase
    h1sb = alloc(NHC * 256, BF16)
    buffm = alloc(8 * 256, BF16)
    bft = alloc(2 * 1024, BF16)
    otm = alloc(1024, BF16)
    r1b_end = _off[0]
    scat0 = View(pool, h1sb.off, 4 * D, BF16)
    scat1 = View(pool, h1sb.off + 8192, 4 * D, BF16)
    g0v = TW(g0v_t)
    g1v = TW(g1v_t)
    xtt = View(pool, bft.off, D, BF16)
    yout = View(pool, bft.off + 2048, D, F32)
    moe_end = max(r1_end, r1b_end)
    assert max(mamba_end, moe_end) <= POOL_B, (mamba_end, moe_end)

    pmm0 = ctx.enter_context(nc.psum_tensor("pmm0", [128, 512], F32))
    pmm1 = ctx.enter_context(nc.psum_tensor("pmm1", [128, 512], F32))
    pbc = ctx.enter_context(nc.psum_tensor("pbc", [128, 512], F32))
    pstat = ctx.enter_context(nc.psum_tensor("pstat", [128, 512], F32))
    ptrf = ctx.enter_context(nc.psum_tensor("ptrf", [128, 128], F32))
    ptr0 = ctx.enter_context(nc.psum_tensor("ptr0", [128, 1024], BF16))
    ptr1 = ctx.enter_context(nc.psum_tensor("ptr1", [128, 1024], BF16))

    sched = {k: [] for k in "TVSDG"}
    cnt = {k: 0 for k in "TVSDGC"}
    for _k in ("SA", "SB", "PO", "W0", "W1", "E0", "E1", "E2", "E3", "OT", "WM"):
        cnt[_k] = 0

    def emit(e, fn, inc=1, **waits):
        w = [(SEM[k], v) for k, v in waits.items() if v]

        def step(eng, fn=fn, w=w, inc=inc, e=e):
            for sem, val in w:
                eng.wait_ge(sem, val)
            r = fn(eng)
            if inc and r is not None:
                r.then_inc(SEM[e], inc)
        q = {"C": "G", "SA": "D", "SB": "D", "PO": "D", "W0": "D",
             "W1": "D", "E0": "D", "E1": "D", "E2": "D", "E3": "D",
             "OT": "D", "WM": "D"}.get(e, e)
        sched[q].append(step)
        if inc:
            cnt[e] += inc
        return cnt[e]

    def dma(q, out, in_, **waits):
        return emit(q, lambda eng, o=out, i=in_: eng.dma_start(out=o, in_=i),
                    inc=16, **waits)

    M = {"mi": 0}

    for dst, src in [
        (identb(), identb_h.ap()), (identf(), identf_h.ap()),
        (colc(), cols_h.ap()), (onerow(), onerow_h.ap()),
        (selmat(), selmat_h.ap()), (rowsel(), rowsel_h.ap()),
        (iota8(), iota8_h.ap()), (myrows(), myrows_h.ap()),
        (mysel0(), mysel0_h.ap()), (mysel1(), mysel1_h.ap()),
        (csil(), c_h.ap()),
        (vb_in(), b_in_h.ap()),
        (vconvw(), convw_h.ap()),
        (vconvb(), convb_h.ap()),
        (vb_xp(), b_xp_h.ap().rearrange("(s o) -> s o", o=1)),
        (vb_dt(), b_dt_h.ap()),
        (vA(), A_log_h.ap()),
        (vD(), Dsk_h.ap()),
        (vb_out(), b_out_h.ap()),
        (valpha(), alpha_h.ap()),
        (vbeta(), beta_h.ap()),
        (vb1(), b1_h.ap()),
        (vrb(), rb_h.ap().rearrange("(s o) -> s o", o=1)),
        (b2row(), b2_h.ap().rearrange("(a b) -> a b", b=128)),
        (fconst(), fconst_h.ap()) if False else (fco(), fconst_h.ap()),
    ]:
        dma("D", dst, src)
    d_consts = cnt["D"]

    for k in range(NMC):
        dma("G", st_Win.ap()[:, :, k * 256:(k + 1) * 256]
            .rearrange("e p c -> p e c"),
            W_in_h.ap()[k * 128:(k + 1) * 128, :]
            .rearrange("p (e c) -> p e c", c=256))
    for k in range(NDC):
        dma("G", st_Wdt.ap()[:, :, k * 128:(k + 1) * 128]
            .rearrange("e p c -> p e c"),
            W_dt_h.ap()[k * 128:(k + 1) * 128, :]
            .rearrange("p (e c) -> p e c", c=128))
    for k in range(NDC):
        dma("G", st_Wout.ap()[:, :, k * 128:(k + 1) * 128]
            .rearrange("e p c -> p e c"),
            W_out_h.ap()[k * 128:(k + 1) * 128, :]
            .rearrange("p (e c) -> p e c", c=128))
    dma("G", wxp(), W_xp_h.ap())
    g_mamba_w = dma("G", wrt(), rw_h.ap())
    dma("G", st_w1.ap().rearrange("a p m -> (a p) m"), w1_h.ap())
    g_moe_w = dma("G", st_w2.ap().rearrange("a p m -> (a p) m"), w2_h.ap())

    s_ = emit("S", lambda eng: eng.activation(out=vA(), in_=vA(), func=AF.Exp),
              D=d_consts)
    emit("V", lambda eng: eng.tensor_scalar(
        out=vA(), in0=vA(), scalar1=-1.0, scalar2=None, op0=OP.mult), S=s_)
    s_c = emit("S", lambda eng: eng.activation(out=csil(), in_=csil(),
                                               func=AF.Silu), D=d_consts)

    v_acc = 0
    for k in range(NMC):
        d_w = emit("WM", lambda eng, k=k: eng.dma_start(
            out=wmscr(), in_=wmod_h.ap()[k * 128:(k + 1) * 128, :]),
            inc=16, V=v_acc)

        def mmk(eng, k=k):
            r = None
            for mc in range(48):
                r = eng.matmul(pstat[:, mc:mc + 1],
                               wmscr(mc * 128, (mc + 1) * 128),
                               csil(k, k + 1), start=True, stop=True)
            return r
        t_mm = emit("T", mmk, WM=d_w, S=s_c, V=v_acc)
        if k == 0:
            v_acc = emit("V", lambda eng: eng.tensor_copy(
                out=modacc(), in_=pstat[:, 0:48]), T=t_mm)
        else:
            v_acc = emit("V", lambda eng: eng.tensor_tensor(
                out=modacc(), in0=modacc(), in1=pstat[:, 0:48], op=OP.add),
                T=t_mm)
    M["pstat_V"] = v_acc
    v_mod0 = emit("V", lambda eng: eng.tensor_copy(out=modc(), in_=modacc()),
                  V=v_acc)
    s_ab = emit("S", lambda eng: (
        eng.activation(out=rsc2(0, 8), in_=modc(16, 24), func=AF.Abs),
        eng.activation(out=rsc3(0, 8), in_=modc(40, 48), func=AF.Abs))[-1],
        V=v_mod0)

    def _cl1(eng):
        eng.tensor_scalar(out=rsc2(0, 8), in0=rsc2(0, 8), scalar1=0.25,
                          scalar2=None, op0=OP.max)
        eng.tensor_scalar(out=rsc3(0, 8), in0=rsc3(0, 8), scalar1=0.25,
                          scalar2=None, op0=OP.max)
        eng.tensor_scalar(out=rsc0(0, 8), in0=modc(16, 24), scalar1=0.0,
                          scalar2=None, op0=OP.is_ge)
        return eng.tensor_scalar(out=rsc0(8, 16), in0=modc(40, 48),
                                 scalar1=0.0, scalar2=None, op0=OP.is_ge)
    v_mod = emit("V", _cl1, S=s_ab)
    v_mod = emit("V", lambda eng: eng.tensor_scalar(
        out=rsc0(0, 16), in0=rsc0(0, 16), scalar1=2.0, scalar2=-1.0,
        op0=OP.mult, op1=OP.add), V=v_mod)

    def _cl3(eng):
        eng.tensor_tensor(out=modc(16, 24), in0=rsc0(0, 8), in1=rsc2(0, 8),
                          op=OP.mult)
        eng.tensor_tensor(out=modc(40, 48), in0=rsc0(8, 16), in1=rsc3(0, 8),
                          op=OP.mult)
        eng.tensor_scalar(out=cac(), in0=modc(8, 16), scalar1=1.0,
                          scalar2=None, op0=OP.add)
        return eng.tensor_scalar(out=cbc(), in0=modc(32, 40), scalar1=1.0,
                                 scalar2=None, op0=OP.add)
    v_mod = emit("V", _cl3, V=v_mod)

    v_z = emit("V", lambda eng: eng.memset(zrow(), 0.0))
    for r in range(0, CAP + 1, 128):
        n = min(128, CAP + 1 - r)
        dma("G", buf_dram.ap()[r:r + n, :], zrow()[0:n, :], V=v_z)
    g_bufz = cnt["G"]
    emit("V", lambda eng: eng.memset(state(), 0.0), D=d_consts)
    v_tail0 = emit("V", lambda eng: eng.memset(xtail(), 0.0))

    def bcast3(view):
        return view().unsqueeze(1).broadcast_to([128, NMC, TC])

    def stat_ln(xview, s_sq, msq_col, unbiased, eps, xV=0, xG=0):
        def mm_mean(eng):
            r = None
            for k in range(NMC):
                r = eng.matmul(pstat[0:1, :], colc(1, 2),
                               xview(k * TC, (k + 1) * TC),
                               start=(k == 0), stop=(k == NMC - 1))
            return r

        def mm_msq(eng):
            r = None
            for k in range(NMC):
                r = eng.matmul(pbc[0:1, :], colc(msq_col, msq_col + 1),
                               tn(k * TC, (k + 1) * TC),
                               start=(k == 0), stop=(k == NMC - 1))
            return r
        emit("T", mm_mean, S=M.get("pstat_S", 0),
             V=max(M.get("pstat_V", 0), xV), G=xG)
        t2 = emit("T", mm_msq, S=max(s_sq, M.get("pbc_S", 0)))

        def _stats(eng):
            eng.tensor_copy(out=srow(0, 512), in_=pstat[0:1, :])
            eng.tensor_copy(out=srow(512, 1024), in_=pbc[0:1, :])
            eng.tensor_tensor(out=srow(1536, 2048), in0=srow(0, 512),
                              in1=srow(0, 512), op=OP.mult)
            if unbiased:
                eng.tensor_scalar(out=srow(1536, 2048), in0=srow(1536, 2048),
                                  scalar1=1024.0 / 1023.0, scalar2=None,
                                  op0=OP.mult)
            return eng.tensor_tensor(out=srow(512, 1024), in0=srow(512, 1024),
                                     in1=srow(1536, 2048), op=OP.subtract)
        v1 = emit("V", _stats, T=t2)
        M["pstat_V"] = v1
        if not unbiased:
            s1 = emit("S", lambda eng: (
                eng.activation(out=srow(1536, 2048), in_=srow(512, 1024),
                               func=AF.Ln, bias=fco(0, 1, 0, 1)),
                eng.activation(out=srow(1024, 1536), in_=srow(1536, 2048),
                               func=AF.Exp, scale=fco(1, 2, 0, 1)))[-1], V=v1)
            v3 = emit("V", lambda eng: eng.nop(), S=s1)
        else:
            s1 = emit("S", lambda eng: (
                eng.activation(out=srow(1536, 2048), in_=srow(512, 1024),
                               func=AF.Ln),
                eng.activation(out=srow(1536, 2048), in_=srow(1536, 2048),
                               func=AF.Exp, scale=fco(2, 3, 0, 1)))[-1], V=v1)
            v2 = emit("V", lambda eng: eng.tensor_scalar(
                out=srow(1536, 2048), in0=srow(1536, 2048), scalar1=eps,
                scalar2=None, op0=OP.add), S=s1)
            v3 = emit("V", lambda eng: eng.reciprocal(
                out=srow(1024, 1536), in_=srow(1536, 2048)), V=v2)
        v4 = emit("V", lambda eng: (
            eng.tensor_copy(out=srowb(0, 512), in_=srow(0, 512)),
            eng.tensor_copy(out=srowb(512, 1024), in_=srow(1024, 1536)))[-1],
            V=v3)
        t3 = emit("T", lambda eng: eng.matmul(
            pbc[:, :], onerow(), srowb(0, 512), start=True, stop=True),
            V=v4, S=M.get("pbc_S", 0))
        s2 = emit("S", lambda eng: eng.activation(out=bcm(), in_=pbc[:, :],
                                                  func=AF.Copy), T=t3)
        t4 = emit("T", lambda eng: eng.matmul(
            pbc[:, :], onerow(), srowb(512, 1024), start=True, stop=True),
            S=s2)
        s3 = emit("S", lambda eng: eng.activation(out=bcr(), in_=pbc[:, :],
                                                  func=AF.Copy), T=t4)
        M["pbc_S"] = s3
        return s3

    def pm_wait(b):
        return dict(S=M.get(f"pmm{b}_S", 0), V=M.get(f"pmm{b}_V", 0))

    def stream_mm(stash, total_cols, slice_cols, k_cnt, rhs_view, evac_fn,
                  rhs_V, rhs_S=0):
        nsl = total_cols // slice_cols
        mper = slice_cols // 128
        wb = [wbuf0, wbuf1]
        dwait = {}
        for e_s in range(min(2, nsl)):
            dwait[e_s] = emit(
                ["W0", "W1"][e_s % 2],
                lambda eng, e_s=e_s: eng.dma_start(
                    out=wb[e_s % 2](0, k_cnt * slice_cols),
                    in_=stash.ap()[e_s]),
                inc=16, T=M.get(f"wb{e_s % 2}_T", 0), G=g_mamba_w)
        s_last = 0
        for e_s in range(nsl):
            t_mm = 0
            for m in range(mper):
                bi = M["mi"] % 2
                bank = [pmm0, pmm1][bi]
                pw = pm_wait(bi)

                def mm(eng, e_s=e_s, m=m, bank=bank):
                    r = None
                    for k in range(k_cnt):
                        r = eng.matmul(
                            bank[:, :],
                            wb[e_s % 2](k * slice_cols + m * 128,
                                        k * slice_cols + (m + 1) * 128),
                            rhs_view(k * TC, (k + 1) * TC),
                            start=(k == 0), stop=(k == k_cnt - 1))
                    return r
                t_mm = emit("T", mm, V=max(rhs_V, pw["V"]),
                            S=max(pw["S"], rhs_S),
                            **{["W0", "W1"][e_s % 2]: dwait[e_s]})
                mc = e_s * mper + m
                s_last = emit("S", lambda eng, mc=mc, bank=bank:
                              evac_fn(eng, mc, bank), T=t_mm)
                M[f"pmm{bi}_S"] = s_last
                M["mi"] += 1
            M[f"wb{e_s % 2}_T"] = t_mm
            nxt = e_s + 2
            if nxt < nsl:
                dwait[nxt] = emit(
                    ["W0", "W1"][nxt % 2],
                    lambda eng, nxt=nxt: eng.dma_start(
                        out=wb[nxt % 2](0, k_cnt * slice_cols),
                        in_=stash.ap()[nxt]),
                    inc=16, T=M.get(f"wb{nxt % 2}_T", 0))
        return s_last

    for c in range(KCH if KPH >= 1 else 0):
        c0 = c * TC
        prevT = M.get("chunk_T", 0)
        prevV = M.get("chunk_V", 0)
        prevS = M.get("chunk_S", 0)
        g_x = dma("G", xs().rearrange("p (a t) -> p a t", t=TC),
                  x_in_h.ap()[:, c0:c0 + TC].rearrange("(a p) t -> p a t",
                                                       p=128),
                  T=prevT, V=max(prevV, v_mod if c == 0 else 0))
        s_sq = emit("S", lambda eng: eng.activation(out=tn(), in_=xs(),
                                                    func=AF.Square),
                    G=g_x, T=prevT, V=prevV, S=prevS)
        s_bc = stat_ln(xs, s_sq, 1, False, 1e-5, xG=g_x)

        def _norm(src):
            def f(eng, src=src):
                bm = bcast3(bcm)
                br = bcast3(bcr)
                t3 = tn().rearrange("p (a t) -> p a t", t=TC)
                x3 = src().rearrange("p (a t) -> p a t", t=TC)
                eng.tensor_tensor(out=t3, in0=x3, in1=bm, op=OP.subtract)
                return eng.tensor_tensor(out=t3, in0=t3, in1=br, op=OP.mult)
            return f
        v_n1 = emit("V", _norm(xs), S=s_bc)

        def _hmod(eng):
            r = None
            for dc in range(NMC):
                r = eng.tensor_scalar(
                    out=hmod(dc * TC, (dc + 1) * TC),
                    in0=tn(dc * TC, (dc + 1) * TC),
                    scalar1=cac(dc, dc + 1), scalar2=modc(dc, dc + 1),
                    op0=OP.mult, op1=OP.add)
            return r
        v_hm = emit("V", _hmod, V=v_n1, T=prevT)

        if KSUB < 1:
            continue
        def ev_xz(eng, mc, bank):
            if mc < 16:
                o = xin(mc * (TC + 3) + 3, mc * (TC + 3) + 3 + TC)
            else:
                o = zs((mc - 16) * TC, (mc - 15) * TC)
            return eng.activation(out=o, in_=bank[:, :], func=AF.Identity,
                                  bias=vb_in(mc, mc + 1))
        s_xz = stream_mm(st_Win, 2 * DI, 256, NMC, hmod, ev_xz, v_hm)

        if KSUB < 2:
            continue
        v_rt = emit("V", lambda eng: eng.tensor_copy(
            out=xin().rearrange("p (a t) -> p a t", t=TC + 3)[:, :, 0:3],
            in_=xtail().rearrange("p (a t) -> p a t", t=3)),
            S=s_xz, V=v_tail0 if c == 0 else 0)

        def _conv(eng):
            r = None
            for dc in range(NDC):
                base = dc * (TC + 3)
                eng.tensor_scalar(
                    out=xc(dc * TC, (dc + 1) * TC),
                    in0=xin(base, base + TC),
                    scalar1=vconvw(dc * 4, dc * 4 + 1),
                    scalar2=vconvb(dc, dc + 1), op0=OP.mult, op1=OP.add)
                for k2 in (1, 2, 3):
                    r = eng.scalar_tensor_tensor(
                        out=xc(dc * TC, (dc + 1) * TC),
                        in0=xin(base + k2, base + k2 + TC),
                        scalar=vconvw(dc * 4 + k2, dc * 4 + k2 + 1),
                        in1=xc(dc * TC, (dc + 1) * TC),
                        op0=OP.mult, op1=OP.add)
            return r
        v_cv = emit("V", _conv, V=v_rt)
        v_tail = emit("V", lambda eng: eng.tensor_copy(
            out=xtail().rearrange("p (a t) -> p a t", t=3),
            in_=xin().rearrange("p (a t) -> p a t", t=TC + 3)[:, :, TC:TC + 3]),
            V=v_cv)
        s_si = emit("S", lambda eng: eng.activation(out=xc(), in_=xc(),
                                                    func=AF.Silu), V=v_cv)
        s_zsil = emit("S", lambda eng: eng.activation(out=zs(), in_=zs(),
                                                      func=AF.Silu), S=s_xz,
                      V=M.get("zs_V", 0))

        if KSUB < 3:
            continue
        def mm_xp(eng):
            r = None
            for k in range(NDC):
                r = eng.matmul(pstat[0:32, :], wxp(k * 32, (k + 1) * 32),
                               xc(k * TC, (k + 1) * TC),
                               start=(k == 0), stop=(k == NDC - 1))
            return r
        t_xp = emit("T", mm_xp, S=max(s_si, M.get("pstat_S", 0)),
                    V=M.get("pstat_V", 0), G=g_mamba_w if c == 0 else 0)
        s_bcr2 = emit("S", lambda eng: eng.activation(
            out=bcrow(), in_=pstat[0:32, :], func=AF.Identity, bias=vb_xp()),
            T=t_xp)
        M["pstat_S"] = s_bcr2

        if KSUB < 4:
            continue
        def ev_dt(eng, mc, bank):
            eng.activation(out=spl(), in_=bank[:, :], func=AF.Exp,
                           bias=vb_dt(mc, mc + 1))
            return eng.activation(out=dl(mc * TC, (mc + 1) * TC),
                                  in_=spl(), func=AF.Ln, bias=1.0)
        s_dt = stream_mm(st_Wdt, DI, 128, NDC, xc, ev_dt,
                         max(v_cv, v_tail), rhs_S=s_si)

        if KSUB < 5:
            continue
        v_dxc = emit("V", lambda eng: eng.tensor_tensor(
            out=dxc(), in0=dl(), in1=xc(), op=OP.mult), S=s_dt)

        def _yinit(eng):
            r = None
            for dc in range(NDC):
                r = eng.tensor_scalar(out=ys(dc * TC, (dc + 1) * TC),
                                      in0=xc(dc * TC, (dc + 1) * TC),
                                      scalar1=vD(dc, dc + 1), scalar2=None,
                                      op0=OP.mult)
            return r
        v_yi = emit("V", _yinit, V=v_dxc)

        if KSUB < 6:
            continue
        v_sc = v_yi
        for sg in range(DS // SG):
            Bbv, Cbv = Bb0, Cb0
            bbfree = v_sc
            s_bb = M.get("pbc_S", 0)
            for s4 in range(SG):
                s_idx = sg * SG + s4
                t_b = emit("T", lambda eng, s_idx=s_idx: eng.matmul(
                    pbc[:, :], rowsel(s_idx * 128, (s_idx + 1) * 128),
                    bcrow(), start=True, stop=True),
                    S=max(s_bcr2, s_bb),
                    V=bbfree if s4 == 0 else 0)
                s_bb = emit("S", lambda eng, s4=s4, Bbv=Bbv: eng.activation(
                    out=Bbv(s4 * TC, (s4 + 1) * TC), in_=pbc[:, :],
                    func=AF.Copy), T=t_b)
                t_cq = emit("T", lambda eng, s_idx=s_idx: eng.matmul(
                    pbc[:, :], rowsel((16 + s_idx) * 128, (17 + s_idx) * 128),
                    bcrow(), start=True, stop=True), S=s_bb)
                s_bb = emit("S", lambda eng, s4=s4, Cbv=Cbv: eng.activation(
                    out=Cbv(s4 * TC, (s4 + 1) * TC), in_=pbc[:, :],
                    func=AF.Copy), T=t_cq)
            M["pbc_S"] = s_bb
            for dc in range(NDC):
                dAv = dA0 if dc % 2 == 0 else dA1
                dafree = M.get(f"da{dc % 2}_V", 0)

                def _dA(eng, dc=dc, sg=sg, dAv=dAv):
                    r = None
                    for s4 in range(SG):
                        r = eng.activation(
                            out=dAv(s4 * TC, (s4 + 1) * TC),
                            in_=dl(dc * TC, (dc + 1) * TC), func=AF.Exp,
                            scale=vA(dc * DS + sg * SG + s4,
                                     dc * DS + sg * SG + s4 + 1))
                    return r
                s_dA = emit("S", _dA, S=s_bb, V=dafree)

                def _scan(eng, dc=dc, sg=sg, dAv=dAv, Bbv=Bbv):
                    eng.tensor_tensor(
                        out=dBxb().rearrange("p (a t) -> p a t", t=TC),
                        in0=dxc(dc * TC, (dc + 1) * TC).unsqueeze(1)
                        .broadcast_to([128, SG, TC]),
                        in1=Bbv().rearrange("p (a t) -> p a t", t=TC),
                        op=OP.mult)
                    r = None
                    for s4 in range(SG):
                        r = eng.tensor_tensor_scan(
                            out=Hsb(s4 * TC, (s4 + 1) * TC),
                            data0=dAv(s4 * TC, (s4 + 1) * TC),
                            data1=dBxb(s4 * TC, (s4 + 1) * TC),
                            initial=state(dc * DS + sg * SG + s4,
                                          dc * DS + sg * SG + s4 + 1),
                            op0=OP.mult, op1=OP.add)
                    return r
                v_mid = emit("V", _scan, S=s_dA)

                def _rdout(eng, dc=dc, sg=sg, Cbv=Cbv):
                    eng.tensor_copy(
                        out=state(dc * DS + sg * SG, dc * DS + sg * SG + SG),
                        in_=Hsb().rearrange("p (a t) -> p a t", t=TC)
                        [:, :, TC - 1:TC])
                    eng.tensor_tensor(out=dBxb(), in0=Hsb(), in1=Cbv(),
                                      op=OP.mult)
                    p3 = dBxb().rearrange("p (a t) -> p a t", t=TC)
                    r = None
                    for s4 in range(SG):
                        r = eng.tensor_tensor(out=ys(dc * TC, (dc + 1) * TC),
                                              in0=ys(dc * TC, (dc + 1) * TC),
                                              in1=p3[:, s4, :], op=OP.add)
                    return r
                v_sc = emit("V", _rdout, V=v_mid)
                M[f"da{dc % 2}_V"] = v_sc

        if KSUB < 7:
            continue
        v_g = emit("V", lambda eng: eng.tensor_tensor(
            out=ys(), in0=ys(), in1=zs(), op=OP.mult), V=v_sc, S=s_zsil)
        M["zs_V"] = v_g

        def ev_o(eng, mc, bank):
            return eng.activation(out=outs(mc * TC, (mc + 1) * TC),
                                  in_=bank[:, :], func=AF.Identity,
                                  bias=vb_out(mc, mc + 1))
        s_o = stream_mm(st_Wout, D, 128, NDC, ys, ev_o, v_g)

        if KSUB < 8:
            continue
        v_r2 = emit("V", lambda eng: eng.tensor_tensor(
            out=hmod(), in0=hmod(), in1=outs(), op=OP.add), S=s_o)
        s_sq2 = emit("S", lambda eng: eng.activation(out=tn(), in_=hmod(),
                                                     func=AF.Square), V=v_r2)
        s_bc2 = stat_ln(hmod, s_sq2, 2, True, 1e-6, xV=v_r2)
        v_n2 = emit("V", _norm(hmod), S=s_bc2)

        def _lnfin(eng):
            r = None
            for dc in range(NMC):
                eng.tensor_scalar(out=outs(dc * TC, (dc + 1) * TC),
                                  in0=tn(dc * TC, (dc + 1) * TC),
                                  scalar1=valpha(dc, dc + 1),
                                  scalar2=vbeta(dc, dc + 1),
                                  op0=OP.mult, op1=OP.add)
                r = eng.scalar_tensor_tensor(
                    out=xs(dc * TC, (dc + 1) * TC),
                    in0=outs(dc * TC, (dc + 1) * TC),
                    scalar=modc(16 + dc, 17 + dc),
                    in1=xs(dc * TC, (dc + 1) * TC), op0=OP.mult, op1=OP.add)
            return r
        v_xu = emit("V", _lnfin, V=v_n2)

        s_sq3 = emit("S", lambda eng: eng.activation(out=tn(), in_=xs(),
                                                     func=AF.Square), V=v_xu)
        s_bc3 = stat_ln(xs, s_sq3, 1, False, 1e-5, xV=v_xu)
        v_n3 = emit("V", _norm(xs), S=s_bc3)

        def _h2(eng):
            r = None
            for dc in range(NMC):
                r = eng.tensor_scalar(
                    out=hmod(dc * TC, (dc + 1) * TC),
                    in0=tn(dc * TC, (dc + 1) * TC),
                    scalar1=cbc(dc, dc + 1), scalar2=modc(24 + dc, 25 + dc),
                    op0=OP.mult, op1=OP.add)
            return r
        v_h2 = emit("V", _h2, V=v_n3)

        if KSUB < 9:
            continue
        def mm_rt(eng):
            r = None
            for k in range(NMC):
                r = eng.matmul(pstat[0:8, :], wrt(k * 8, (k + 1) * 8),
                               hmod(k * TC, (k + 1) * TC),
                               start=(k == 0), stop=(k == NMC - 1))
            return r
        t_rt = emit("T", mm_rt, V=max(v_h2, M.get("pstat_V", 0)),
                    S=M.get("pstat_S", 0), G=g_mamba_w if c == 0 else 0)
        s_l8 = emit("S", lambda eng: eng.activation(
            out=l8(), in_=pstat[0:8, :], func=AF.Identity, bias=vrb()), T=t_rt)
        M["pstat_S"] = s_l8
        t_lt = 0
        for tb in range(4):
            t_lt = emit("T", lambda eng, tb=tb: eng.transpose(
                ptrf[:, tb * 8:(tb + 1) * 8], l8(tb * 128, (tb + 1) * 128),
                identf(0, 8, 0, 8)), S=s_l8,
                V=M.get("ptrf_V", 0) if tb == 0 else 0)
        v_lt = emit("V", lambda eng, c=c: eng.tensor_copy(
            out=ltm(c * 32, (c + 1) * 32), in_=ptrf[:, 0:32]), T=t_lt)
        M["ptrf_V"] = v_lt

        ev = [evrow0, evrow1]
        evx = [evrow2, evrow3]
        for tb in range(4):
            t_h = 0
            for mc2 in range(NMC):
                t_h = emit("T", lambda eng, tb=tb, mc2=mc2: eng.transpose(
                    ptr0[:, mc2 * 128:(mc2 + 1) * 128],
                    hmod(mc2 * TC + tb * 128, mc2 * TC + (tb + 1) * 128),
                    identb()),
                    V=max(v_h2, M.get("ptr0_V", 0)) if mc2 == 0 else 0,
                    **{["E0", "E1"][tb % 2]:
                       M.get(f"evh{tb % 2}_D", 0) if mc2 == 0 else 0})
            v_eh = emit("V", lambda eng, tb=tb: eng.tensor_copy(
                out=ev[tb % 2](), in_=ptr0[:, :]), T=t_h)
            M["ptr0_V"] = v_eh
            M[f"evh{tb % 2}_D"] = emit(
                ["E0", "E1"][tb % 2],
                lambda eng, tb=tb, c0=c0: eng.dma_start(
                    out=h2stage.ap()[c0 + tb * 128:c0 + (tb + 1) * 128, :],
                    in_=ev[tb % 2]()), inc=16, V=v_eh)
            t_x = 0
            for mc2 in range(NMC):
                t_x = emit("T", lambda eng, tb=tb, mc2=mc2: eng.transpose(
                    ptr1[:, mc2 * 128:(mc2 + 1) * 128],
                    xs(mc2 * TC + tb * 128, mc2 * TC + (tb + 1) * 128),
                    identb()),
                    V=max(v_xu, M.get("ptr1_V", 0)) if mc2 == 0 else 0,
                    **{["E2", "E3"][tb % 2]:
                       M.get(f"evx{tb % 2}_D", 0) if mc2 == 0 else 0})
            v_ex = emit("V", lambda eng, tb=tb: eng.tensor_copy(
                out=evx[tb % 2](), in_=ptr1[:, :]), T=t_x)
            M["ptr1_V"] = v_ex
            M[f"evx{tb % 2}_D"] = emit(
                ["E2", "E3"][tb % 2],
                lambda eng, tb=tb, c0=c0: eng.dma_start(
                    out=xmid.ap()[c0 + tb * 128:c0 + (tb + 1) * 128, :],
                    in_=evx[tb % 2]()), inc=16, V=v_ex)
        M["chunk_T"] = cnt["T"]
        M["chunk_V"] = cnt["V"]
        M["chunk_S"] = cnt["S"]

    d_stage = cnt["D"]
    barT, barV, barS = cnt["T"], cnt["V"], cnt["S"]
    barE0, barE1, barE2, barE3 = cnt["E0"], cnt["E1"], cnt["E2"], cnt["E3"]
    if KPH < 2:
        d_final = cnt["D"]
        emit("D", lambda eng: None, inc=0, D=d_final, G=cnt["G"], T=cnt["T"],
             V=cnt["V"], S=cnt["S"])
        return _finish(nc, sched)

    c_h2 = emit("C", lambda eng: eng.collective_compute(
        "AllGather", OP.bypass, replica_groups=[list(range(8))],
        ins=[h2stage.ap().opt()], outs=[h2_all.ap().opt()]),
        D=d_stage, E0=barE0, E1=barE1)

    s_exp = emit("S", lambda eng: eng.activation(out=gtm(), in_=ltm(),
                                                 func=AF.Exp),
                 V=barV, T=barT)

    g3v = lambda: gtm().rearrange("p (a b) -> p a b", b=8)
    v_sm = emit("V", lambda eng: eng.tensor_reduce(
        out=rsc2(), in_=g3v(), axis=AX.X, op=OP.add), S=s_exp)
    v_sm = emit("V", lambda eng: eng.reciprocal(out=rsc2(), in_=rsc2()),
                V=v_sm)
    v_sm = emit("V", lambda eng: eng.tensor_tensor(
        out=g3v(), in0=g3v(), in1=rsc2().unsqueeze(2)
        .broadcast_to([128, 32, 8]), op=OP.mult), V=v_sm)

    def maskpass(srcv, dst, v_in):
        s3 = lambda: srcv().rearrange("p (a b) -> p a b", b=8)
        d3 = lambda: dst().rearrange("p (a b) -> p a b", b=8)
        r0 = lambda: rsc0().rearrange("p (a b) -> p a b", b=8)
        r1 = lambda: rsc1().rearrange("p (a b) -> p a b", b=8)
        v = emit("V", lambda eng: eng.tensor_reduce(
            out=rsc2(), in_=s3(), axis=AX.X, op=OP.max), V=v_in)
        v = emit("V", lambda eng: eng.tensor_tensor(
            out=d3(), in0=s3(), in1=rsc2().unsqueeze(2)
            .broadcast_to([128, 32, 8]), op=OP.is_ge), V=v)
        v = emit("V", lambda eng: eng.tensor_copy(out=r0(), in_=d3()), V=v)
        for sh, (sa, sd) in zip((1, 2, 4), ((r0, r1), (r1, r0), (r0, r1))):
            v = emit("V", lambda eng, sh=sh, sa=sa, sd=sd: (
                eng.tensor_copy(out=sd()[:, :, 0:sh], in_=sa()[:, :, 0:sh]),
                eng.tensor_tensor(out=sd()[:, :, sh:8], in0=sa()[:, :, sh:8],
                                  in1=sa()[:, :, 0:8 - sh], op=OP.add))[-1],
                V=v)
        v = emit("V", lambda eng: eng.tensor_scalar(
            out=rsc0(), in0=rsc1(), scalar1=1.0, scalar2=None, op0=OP.is_le),
            V=v)
        v = emit("V", lambda eng: eng.tensor_tensor(
            out=dst(), in0=dst(), in1=rsc0(), op=OP.mult), V=v)
        return v
    v_tk = maskpass(lambda: ltm(), m1c, v_sm)
    v_tk = emit("V", lambda eng: eng.scalar_tensor_tensor(
        out=gmslab(0, 256), in0=m1c(), scalar=-1000.0, in1=ltm(),
        op0=OP.mult, op1=OP.add), V=v_tk)
    v_tk = maskpass(lambda: gmslab(0, 256), m2c, v_tk)

    def _gmpack(eng):
        g4 = gmslab().rearrange("p (a b) -> p a b", b=16)
        eng.tensor_copy(out=g4[:, :, 0:8],
                        in_=gtm().rearrange("p (a b) -> p a b", b=8))
        return eng.tensor_tensor(
            out=g4[:, :, 8:16],
            in0=m1c().rearrange("p (a b) -> p a b", b=8),
            in1=m2c().rearrange("p (a b) -> p a b", b=8), op=OP.add)
    v_gm = emit("V", _gmpack, V=v_tk)

    for j in range(32):
        t_g = emit("T", lambda eng, j=j: eng.transpose(
            ptrf[0:16, 0:128], gmslab(j * 16, (j + 1) * 16), identf()),
            V=v_gm if j == 0 else 0, S=M.get("gmev_S", 0))
        s_ge = emit("S", lambda eng, j=j: eng.activation(
            out=gmev((j % 4) * 128, (j % 4 + 1) * 128), in_=ptrf[0:16, 0:128],
            func=AF.Copy), T=t_g,
            E0=M.get("gmev_D", 0) if j % 4 == 0 else 0)
        M["gmev_S"] = s_ge
        if j % 4 == 3:
            M["gmev_D"] = emit("E0", lambda eng, j=j: eng.dma_start(
                out=gm_in.ap()[:, (j - 3) * 128:(j + 1) * 128],
                in_=gmev()), inc=16, S=s_ge)
    d_gm = cnt["D"]

    c_gm = emit("C", lambda eng: eng.collective_compute(
        "AllGather", OP.bypass, replica_groups=[list(range(8))],
        ins=[gm_in.ap().opt()], outs=[gm_all.ap().opt()]),
        E0=M.get("gmev_D", 0))

    d_asm = 0
    for e_i in range(E):
        for s1 in range(2):
            p0_ = e_i * 16 + s1 * 8
            src = gm_all.ap().rearrange("(cc r) t -> cc r t", r=16)
            d_asm = emit("PO", lambda eng, e_i=e_i, s1=s1, p0_=p0_, src=src:
                         eng.dma_start(
                             out=maskg(0, 2048, p0_, p0_ + 8),
                             in_=src[:, 8 + e_i, s1 * 2048:(s1 + 1) * 2048]),
                         inc=16,
                         C=c_gm, V=barV if (e_i == 0 and s1 == 0) else 0,
                         T=barT if (e_i == 0 and s1 == 0) else 0,
                         S=barS if (e_i == 0 and s1 == 0) else 0)

    v_ones = emit("V", lambda eng: eng.memset(offsg(), 1.0), PO=d_asm)
    v_cum = emit("V", lambda eng: eng.tensor_tensor_scan(
        out=cumg(), data0=offsg(), data1=maskg(), initial=0.0,
        op0=OP.mult, op1=OP.add), V=v_ones)
    v_tot = emit("V", lambda eng: eng.tensor_copy(out=totc(),
                                                  in_=cumg(2047, 2048)),
                 V=v_cum)
    t_off = emit("T", lambda eng: eng.matmul(pstat[:, 0:1], selmat(), totc(),
                                             start=True, stop=True),
                 V=max(v_tot, M.get("pstat_V", 0)), S=M.get("pstat_S", 0))
    v_offc = emit("V", lambda eng: eng.tensor_copy(out=offc(),
                                                   in_=pstat[:, 0:1]),
                  T=t_off)
    M["pstat_V"] = v_offc

    def _posval(eng):
        eng.tensor_scalar(out=posg(), in0=cumg(), scalar1=offc(),
                          scalar2=-1.0, op0=OP.add, op1=OP.add)
        eng.tensor_scalar(out=cumg(), in0=posg(), scalar1=float(CAP),
                          scalar2=None, op0=OP.is_lt)
        eng.tensor_tensor(out=maskg(), in0=maskg(), in1=cumg(), op=OP.mult)
        eng.tensor_scalar(out=offsg(), in0=posg(), scalar1=-float(CAP),
                          scalar2=None, op0=OP.add)
        eng.tensor_tensor(out=offsg(), in0=offsg(), in1=maskg(), op=OP.mult)
        eng.tensor_scalar(out=offsg(), in0=offsg(), scalar1=float(CAP),
                          scalar2=None, op0=OP.add)
        eng.tensor_scalar(out=offsg(), in0=offsg(), scalar1=0.0,
                          scalar2=float(CAP), op0=OP.max, op1=OP.min)
        return eng.tensor_copy(out=offsi(), in_=offsg())
    v_pv = emit("V", _posval, V=v_offc)

    d_po = 0
    pos_r = pos_dram.ap().rearrange("(e s2 two g) c -> e s2 two (g c)", e=8, s2=8, two=2)
    off_r = offs_dram.ap().rearrange("(e s2 two g) c -> e s2 two (g c)", e=8, s2=8, two=2)
    for e_i in range(E):
        for s1 in range(2):
            p0_ = e_i * 16 + s1 * 8
            d_po = emit("PO", lambda eng, e_i=e_i, s1=s1, p0_=p0_:
                        eng.dma_start(out=pos_r[e_i, :, s1, :],
                                      in_=posg(0, 2048, p0_, p0_ + 8)),
                        inc=16, V=v_pv)
            d_po = emit("PO", lambda eng, e_i=e_i, s1=s1, p0_=p0_:
                        eng.dma_start(out=off_r[e_i, :, s1, :],
                                      in_=offsi(0, 2048, p0_, p0_ + 8)),
                        inc=16, V=v_pv)

    if KOFF == "host":
        g_oc = dma("G", offcol(), offhost_h.ap(), D=d_po)
    else:
        g_oc = emit("G", lambda eng: eng.indirect_dma_start(
            out=offcol(),
            out_offset=None,
            in_=offs_dram.ap(),
            in_offset=IndirectOffsetOnAxis(ap=myrows(), axis=0)),
            inc=16, PO=d_po)
    g_p0 = emit("G", lambda eng: eng.indirect_dma_start(
        out=posr0(), out_offset=None,
        in_=pos_dram.ap(),
        in_offset=IndirectOffsetOnAxis(ap=mysel0(), axis=0)),
        inc=16, PO=d_po, T=barT, V=barV)
    g_p1 = emit("G", lambda eng: eng.indirect_dma_start(
        out=posr1(), out_offset=None,
        in_=pos_dram.ap(),
        in_offset=IndirectOffsetOnAxis(ap=mysel1(), axis=0)),
        inc=16, PO=d_po)
    t_pt0 = emit("T", lambda eng: eng.transpose(ptrf[:, 0:128], posr0(),
                                                identf()), G=g_p0,
                 V=M.get("ptrf_V", 0), S=M.get("gmev_S", 0))
    v_pt0 = emit("V", lambda eng: eng.tensor_copy(out=postm(0, 128),
                                                  in_=ptrf[:, 0:128]),
                 T=t_pt0)
    t_pt1 = emit("T", lambda eng: eng.transpose(ptrf[:, 0:128], posr1(),
                                                identf()), G=g_p1, V=v_pt0)
    v_pt1 = emit("V", lambda eng: eng.tensor_copy(out=postm(128, 256),
                                                  in_=ptrf[:, 0:128]),
                 T=t_pt1)
    M["ptrf_V"] = v_pt1
    if KDBG:
        dma("D", dbg_xm.ap(), xmid.ap(), E2=cnt["E2"], E3=cnt["E3"])
        dma("D", dbg_h2s.ap(), h2stage.ap(), E0=cnt["E0"], E1=cnt["E1"])
        dma("D", dbg_ptm.ap(), postm(), V=v_pt1)
        dma("D", dbg_gtm.ap(), gtm(), V=v_pv)
        dma("D", dbg_oc.ap(), offcol(), G=g_oc)
        dma("D", dbg_off.ap(), offs_dram.ap().rearrange("(a b) c -> a (b c)", b=8), PO=d_po)
        dma("D", dbg_pos.ap(), posg(), V=v_pv)
        dma("D", dbg_msk.ap(), maskg(), V=v_pv)
        dma("D", dbg_ltm.ap(), ltm(), V=v_pv)
        dma("D", dbg_m1.ap(), m1c(), V=v_pv)

    if KPH < 3:
        d_final = cnt["D"]
        emit("D", lambda eng: None, inc=0, D=d_final, G=cnt["G"], T=cnt["T"],
             V=cnt["V"], S=cnt["S"],
             C=cnt["C"] if int(os.environ.get("KWH2", "0")) else 0)
        return _finish(nc, sched)
    d_w1 = dma("D", w1sb().rearrange("p (a m) -> p a m", m=DH),
               st_w1.ap().rearrange("a p m -> p a m"),
               T=barT, V=barV, S=barS, G=g_moe_w)
    d_w2 = dma("D", w2sb().rearrange("p (a m) -> p a m", m=D),
               st_w2.ap().rearrange("a p m -> p a m"))

    t_gbt = emit("T", lambda eng: eng.transpose(ptrf[0:8, 0:128],
                                                modc(40, 48), identf()),
                 V=max(v_pt1, barV), S=barS)
    v_gbr = emit("V", lambda eng: eng.tensor_copy(out=gbrow(),
                                                  in_=ptrf[0:8, 0:128]),
                 T=t_gbt)
    M["ptrf_V"] = v_gbr
    v_gbb = emit("V", lambda eng: (
        eng.tensor_copy(out=gbrowb(), in_=gbrow()),
        eng.tensor_copy(out=b2rowb(), in_=b2row()))[-1], V=v_gbr)
    for dc in range(8):
        t_bb = emit("T", lambda eng, dc=dc: eng.matmul(
            pbc[:, 0:128], rowsel(dc * 128, (dc + 1) * 128, 0, 8),
            gbrowb(), start=True, stop=True), V=v_gbb,
            S=M.get("pbc_S", 0))
        s_b2 = emit("S", lambda eng, dc=dc: eng.activation(
            out=gbbc(dc * 128, (dc + 1) * 128), in_=pbc[:, 0:128],
            func=AF.Copy), T=t_bb)
        t_bb = emit("T", lambda eng, dc=dc: eng.matmul(
            pbc[:, 0:128], rowsel(dc * 128, (dc + 1) * 128, 0, 8),
            b2rowb(), start=True, stop=True), S=s_b2)
        s_b2 = emit("S", lambda eng, dc=dc: eng.activation(
            out=b2bc(dc * 128, (dc + 1) * 128), in_=pbc[:, 0:128],
            func=AF.Copy), T=t_bb)
        M["pbc_S"] = s_b2

    h2r = h2_all.ap().rearrange("(a b) d -> a b d", b=256)
    for bt in range(64 if KP3 >= 1 else 0):
        buf_v = [scat0, scat1][bt % 2]
        sem_k = ["SA", "SB"][bt % 2]
        d_t = emit(sem_k, lambda eng, bt=bt, buf_v=buf_v: eng.dma_start(
            out=buf_v().rearrange("p (a d) -> p a d", d=D),
            in_=h2r[:, 4 * bt:4 * bt + 4, :]), inc=16,
            G=max(M.get(f"scat{bt % 2}_G", 0), g_bufz),
            C=c_h2 if bt < 2 else 0,
            V=v_pv if bt < 2 else 0, T=barT if bt < 2 else 0)
        g_last = 0
        for q in range(4):
            j = 4 * bt + q
            g_last = emit(
                "G", lambda eng, j=j, q=q, buf_v=buf_v: eng.indirect_dma_start(
                    out=buf_dram.ap(),
                    out_offset=IndirectOffsetOnAxis(ap=offcol(j, j + 1),
                                                    axis=0),
                    in_=buf_v(q * D, (q + 1) * D), in_offset=None),
                inc=16, G=g_oc if j == 0 else 0,
                **{sem_k: d_t} if q == 0 else {})
        M[f"scat{bt % 2}_G"] = g_last
    g_scat = cnt["G"]

    NTT = CAP // 256
    for tt in range((NTT if KP3 >= 2 else 0)):
        d_bf = dma("D", bft().rearrange("p (a d) -> p a d", d=D),
                   buf_dram.ap()[0:CAP, :].rearrange("(a b) d -> a b d", b=128)
                   [2 * tt:2 * tt + 2, :, :].rearrange("a p d -> p a d"),
                   G=g_scat, T=M.get("bft_T", 0))
        t_bt = 0
        for dc in range(8):
            for tb in range(2):
                t_bt = emit("T", lambda eng, dc=dc, tb=tb: eng.transpose(
                    ptr0[:, tb * 128:(tb + 1) * 128],
                    bft(tb * 1024 + dc * 128, tb * 1024 + (dc + 1) * 128),
                    identb()),
                    D=d_bf if (dc == 0 and tb == 0) else 0,
                    V=M.get("ptr0_V", 0) if tb == 0 else 0)
            v_bf = emit("V", lambda eng, dc=dc: eng.tensor_copy(
                out=buffm(dc * 256, (dc + 1) * 256), in_=ptr0[:, 0:256]),
                T=t_bt)
            M["ptr0_V"] = v_bf
        M["bft_T"] = t_bt
        v_bfm = M["ptr0_V"]

        s_h1 = 0
        for hc in range(NHC):
            bi = M["mi"] % 2
            bank = [pmm0, pmm1][bi]
            pw = pm_wait(bi)

            def mm_h1(eng, hc=hc, bank=bank):
                r = None
                for k in range(NMC):
                    r = eng.matmul(bank[:, 0:256],
                                   w1sb(k * DH + hc * 128,
                                        k * DH + (hc + 1) * 128),
                                   buffm(k * 256, (k + 1) * 256),
                                   start=(k == 0), stop=(k == NMC - 1))
                return r
            t_h1 = emit("T", mm_h1, D=d_w1 if (tt == 0 and hc == 0) else 0,
                        V=max(v_bfm, pw["V"]), S=pw["S"])
            s_h1 = emit("S", lambda eng, hc=hc, bank=bank: eng.activation(
                out=h1sb(hc * 256, (hc + 1) * 256), in_=bank[:, 0:256],
                func=AF.Gelu, bias=vb1(hc, hc + 1)), T=t_h1)
            M[f"pmm{bi}_S"] = s_h1
            M["mi"] += 1

        for tb in range(2):
            t_o2 = 0
            for nh in range(2):
                pw = pm_wait(nh)

                def mm_o2(eng, tb=tb, nh=nh):
                    r = None
                    for k in range(NHC):
                        r = eng.matmul(
                            [pmm0, pmm1][nh][:, :],
                            h1sb(k * 256 + tb * 128, k * 256 + (tb + 1) * 128),
                            w2sb(k * D + nh * 512, k * D + (nh + 1) * 512),
                            start=(k == 0), stop=(k == NHC - 1))
                    return r
                t_o2 = emit("T", mm_o2, S=max(s_h1, pw["S"]), V=pw["V"],
                            D=d_w2 if (tt == 0 and tb == 0 and nh == 0) else 0)

            def _oev(eng):
                eng.tensor_tensor(out=otm(0, 512), in0=pmm0[:, :],
                                  in1=b2bc(0, 512), op=OP.add)
                return eng.tensor_tensor(out=otm(512, 1024), in0=pmm1[:, :],
                                         in1=b2bc(512, 1024), op=OP.add)
            v_oev = emit("V", _oev, T=t_o2, OT=M.get("otm_D", 0))
            M["pmm0_V"] = v_oev
            M["pmm1_V"] = v_oev
            M["pmm0_S"] = 0
            M["pmm1_S"] = 0
            M["otm_D"] = emit(
                "OT", lambda eng, tt=tt, tb=tb: eng.dma_start(
                    out=outstage.ap()[tt * 256 + tb * 128:
                                      tt * 256 + (tb + 1) * 128, :],
                    in_=otm()), inc=16, V=v_oev)
    d_ost = cnt["D"]

    if KP3 < 3:
        d_final = cnt["D"]
        emit("D", lambda eng: None, inc=0, D=d_final, G=cnt["G"], T=cnt["T"],
             V=cnt["V"], S=cnt["S"])
        return _finish(nc, sched)
    c_out = emit("C", lambda eng: eng.collective_compute(
        "AllGather", OP.bypass, replica_groups=[list(range(8))],
        ins=[outstage.ap().opt()], outs=[out_all.ap().opt()]),
        D=d_ost, OT=M.get("otm_D", 0))

    if KPH < 4:
        d_final = cnt["D"]
        emit("D", lambda eng: None, inc=0, D=d_final, G=cnt["G"], T=cnt["T"],
             V=cnt["V"], S=cnt["S"])
        return _finish(nc, sched)
    v_own = emit("V", lambda eng: eng.tensor_scalar(
        out=rsc0(), in0=postm(), scalar1=float(CAP), scalar2=None,
        op0=OP.is_lt), V=v_pt1)
    v_own = emit("V", lambda eng: eng.tensor_tensor(
        out=rsc0(), in0=gtm(), in1=rsc0(), op=OP.mult), V=v_own)

    def red(mask, other, wsum, psum_, esum, v_in):
        v = emit("V", lambda eng: eng.tensor_tensor(
            out=rsc1(), in0=mask(), in1=rsc0(), op=OP.mult), V=v_in)
        v = emit("V", lambda eng: eng.tensor_reduce(
            out=wsum(), in_=rsc1().rearrange("p (a b) -> p a b", b=8),
            axis=AX.X, op=OP.add), V=v)
        v = emit("V", lambda eng: eng.tensor_tensor(
            out=rsc1(), in0=mask(), in1=postm(), op=OP.mult), V=v)
        v = emit("V", lambda eng: eng.tensor_reduce(
            out=psum_(), in_=rsc1().rearrange("p (a b) -> p a b", b=8),
            axis=AX.X, op=OP.add), V=v)
        v = emit("V", lambda eng: eng.tensor_tensor(
            out=rsc1(), in0=mask(), in1=iota8(), op=OP.mult), V=v)
        v = emit("V", lambda eng: eng.tensor_reduce(
            out=esum(), in_=rsc1().rearrange("p (a b) -> p a b", b=8),
            axis=AX.X, op=OP.add), V=v)
        return v
    v_own = red(m1c, None, w0c, p0c, e0c, v_own)
    v_own = red(m2c, None, w1c, p1c, e1c, v_own)
    v_own = emit("V", lambda eng: (
        eng.tensor_scalar(out=p0c(), in0=p0c(), scalar1=float(CAP - 1),
                          scalar2=None, op0=OP.min),
        eng.tensor_scalar(out=p1c(), in0=p1c(), scalar1=float(CAP - 1),
                          scalar2=None, op0=OP.min))[-1], V=v_own)
    v_own = emit("V", lambda eng: (
        eng.scalar_tensor_tensor(out=p0c(), in0=e0c(), scalar=float(CAP),
                                 in1=p0c(), op0=OP.mult, op1=OP.add),
        eng.scalar_tensor_tensor(out=p1c(), in0=e1c(), scalar=float(CAP),
                                 in1=p1c(), op0=OP.mult, op1=OP.add))[-1],
        V=v_own)
    v_own = emit("V", lambda eng: (
        eng.tensor_scalar(out=p0c(), in0=p0c(), scalar1=0.0,
                          scalar2=float(E * CAP - 1), op0=OP.max, op1=OP.min),
        eng.tensor_scalar(out=p1c(), in0=p1c(), scalar1=0.0,
                          scalar2=float(E * CAP - 1), op0=OP.max,
                          op1=OP.min))[-1], V=v_own)
    v_own = emit("V", lambda eng: (
        eng.tensor_copy(out=r0i(), in_=p0c()),
        eng.tensor_copy(out=r1i(), in_=p1c()))[-1], V=v_own)

    v_cb = 0
    for j in range(32):
        g_g1 = 0
        if KJ4 >= 1:
            g_g0 = emit("G", lambda eng, j=j: eng.indirect_dma_start(
                out=g0v(), out_offset=None, in_=out_all.ap(),
                in_offset=IndirectOffsetOnAxis(ap=r0i(j, j + 1), axis=0)),
                inc=16, V=max(v_own, v_cb), C=c_out if KWC else 0)
            g_g1 = g_g0
        if KJ4 >= 2:
            g_g1 = emit("G", lambda eng, j=j: eng.indirect_dma_start(
                out=g1v(), out_offset=None, in_=out_all.ap(),
                in_offset=IndirectOffsetOnAxis(ap=r1i(j, j + 1), axis=0)),
                inc=16, V=max(v_own, v_cb))
        d_xt = dma("D", xtt(), xmid.ap()[j * 128:(j + 1) * 128, :],
                   V=v_cb, C=c_out if (j == 0 and KWC) else 0,
                   E2=barE2 if j == 0 else 0, E3=barE3 if j == 0 else 0)

        def _comb(eng, j=j):
            if KJ4 >= 3:
                eng.tensor_scalar(out=yout(), in0=g0v(), scalar1=w0c(j, j + 1),
                                  scalar2=None, op0=OP.mult)
                eng.scalar_tensor_tensor(out=yout(), in0=g1v(),
                                         scalar=w1c(j, j + 1), in1=yout(),
                                         op0=OP.mult, op1=OP.add)
                eng.tensor_tensor(out=yout(), in0=yout(), in1=gbbc(),
                                  op=OP.mult)
                return eng.tensor_tensor(out=yout(), in0=yout(), in1=xtt(),
                                         op=OP.add)
            return eng.tensor_copy(out=yout(), in_=xtt())
        v_cb = emit("V", _comb, G=g_g1, D=d_xt,
                    V=v_own if j == 0 else 0)
        dma("D", out_h.ap()[j * 128:(j + 1) * 128, :], yout(), V=v_cb)

    d_final = cnt["D"]
    emit("D", lambda eng: None, inc=0, D=d_final, G=cnt["G"],
         SA=cnt["SA"], SB=cnt["SB"], PO=cnt["PO"], W0=cnt["W0"],
         W1=cnt["W1"], E0=cnt["E0"], E1=cnt["E1"], E2=cnt["E2"],
         E3=cnt["E3"], OT=cnt["OT"], WM=cnt["WM"])
    return _finish(nc, sched)


def kernel(**inputs):
    inp = {k: np.asarray(v) for k, v in inputs.items()}
    nc = build_nc()

    ident = np.eye(128)
    colconsts = np.zeros((128, 4), np.float32)
    colconsts[:, 0] = 1.0
    colconsts[:, 1] = 1.0 / 1024.0
    colconsts[:, 2] = 1.0 / 1023.0
    onerow = np.ones((1, 128), np.float32)
    sl = np.empty(16, np.int64)
    for s1 in range(2):
        for s2 in range(8):
            sl[s1 * 8 + s2] = s2 * 2 + s1
    selmat = np.zeros((128, 128), np.float32)
    for e_i in range(8):
        for a in range(16):
            for b_ in range(16):
                if sl[a] < sl[b_]:
                    selmat[e_i * 16 + a, e_i * 16 + b_] = 1.0
    rowsel = np.zeros((32, 32 * 128), np.float32)
    for s in range(32):
        rowsel[s, s * 128:(s + 1) * 128] = 1.0
    iota8 = np.tile(np.arange(8, dtype=np.float32), 32)[None, :].repeat(128, 0)

    x = inp["L0_attn_x"]
    c_all = inp["L0_c"]

    in_maps = []
    for ci in range(8):
        m = {
            "x_fm": np.ascontiguousarray(x[ci].T),
            "W_in": inp["L0_attn_W_in"],
            "b_inp": np.ascontiguousarray(inp["L0_attn_b_in"].reshape(32, 128).T),
            "conv_wp": np.ascontiguousarray(inp["L0_attn_conv_w"][:, 0, :].reshape(16, 128, 4).transpose(1, 0, 2).reshape(128, 64)),
            "conv_bp": np.ascontiguousarray(inp["L0_attn_conv_b"].reshape(16, 128).T),
            "W_xpp": np.ascontiguousarray(inp["L0_attn_W_xp"].reshape(16, 128, 32).transpose(1, 0, 2).reshape(128, 512)),
            "b_xp": inp["L0_attn_b_xp"],
            "W_dt": inp["L0_attn_W_dt"],
            "b_dtp": np.ascontiguousarray(inp["L0_attn_b_dt"].reshape(16, 128).T),
            "A_logp": np.ascontiguousarray(inp["L0_attn_A_log"].reshape(16, 128, 16).transpose(1, 0, 2).reshape(128, 256)),
            "D_skipp": np.ascontiguousarray(inp["L0_attn_D_skip"].reshape(16, 128).T),
            "W_out": inp["L0_attn_W_out"],
            "b_outp": np.ascontiguousarray(inp["L0_attn_b_out"].reshape(8, 128).T),
            "alphap": np.ascontiguousarray(inp["L0_attn_alpha"].reshape(8, 128).T),
            "betap": np.ascontiguousarray(inp["L0_attn_beta"].reshape(8, 128).T),
            "router_wp": np.ascontiguousarray(inp["L0_ffn_router_w"].reshape(8, 128, 8).transpose(1, 0, 2).reshape(128, 64)),
            "router_b": inp["L0_ffn_router_b"],
            "w1e": np.ascontiguousarray(inp["L0_ffn_w1"][ci]),
            "b1ep": np.ascontiguousarray(inp["L0_ffn_b1"][ci].reshape(32, 128).T),
            "w2e": np.ascontiguousarray(inp["L0_ffn_w2"][ci]),
            "b2e": np.ascontiguousarray(inp["L0_ffn_b2"][ci]),
            "c_row": np.ascontiguousarray(c_all[ci].reshape(8, 128).T),
            "wmod": inp["L0_wmod"],
            "identb": ident.astype(BF),
            "identf": ident.astype(np.float32),
            "colconsts": colconsts.astype(BF),
            "onerow": onerow.astype(BF),
            "selmat": selmat,
            "rowsel": rowsel.astype(BF),
            "iota8": iota8,
            "myrows": (np.arange(128, dtype=np.int32)
                       + ci * 128).reshape(128, 1),
            "mysel0": np.array([(p % 8) * 256 + ci * 32 + (p // 8)
                                for p in range(128)], np.int32).reshape(128, 1),
            "mysel1": np.array([(p % 8) * 256 + ci * 32 + 16 + (p // 8)
                                for p in range(128)], np.int32).reshape(128, 1),
            "fconst": np.tile(np.array([1e-5, -0.5, 0.5, 0.0],
                                       np.float32), (128, 1)),
            "offhost": np.random.default_rng(7).integers(
                0, 5121, (128, 256)).astype(np.int32),
        }
        in_maps.append(m)

    res = run_bass_kernel_spmd(nc, in_maps, core_ids=list(range(8)))
    out = np.stack([np.asarray(r["out"]) for r in res.results], 0)
    return out.astype(np.float32)


if __name__ == "__main__":
    import os
    import reference as R
    import jax
    with jax.default_device(jax.devices("cpu")[0]):
        inp = R.setup_inputs(0)
        inp = {k: np.asarray(v) for k, v in inp.items()}
        ref = np.asarray(jax.jit(R.reference)(**{k: R.jnp.asarray(v)
                                                 for k, v in inp.items()}))
    got = kernel(**inp)
    err = got - ref
    rvr = float((err ** 2).mean() / (ref ** 2).mean())
    print("resvar ratio:", rvr)


# version 14 (final)
# speedup vs baseline: 1.0802x; 1.0016x over previous
import numpy as np
import sys

sys.path.insert(0, "/opt/trn_rl_repo")
import concourse.bass as bass
import concourse.mybir as mybir
from concourse.alu_op_type import AluOpType as OP
from concourse.bass import IndirectOffsetOnAxis
from concourse.bass_utils import run_bass_kernel_spmd
import ml_dtypes

F32 = mybir.dt.float32
BF16 = mybir.dt.bfloat16
I32 = mybir.dt.int32
U8 = mybir.dt.uint8
AF = mybir.ActivationFunctionType
AX = mybir.AxisListType
BF = ml_dtypes.bfloat16

B, T, D = 8, 4096, 1024
DI, DS = 2048, 16
DH = 4096
E = 8
CAP = 5120
TC = 512
NCH = T // TC
NDC = DI // 128
NMC = D // 128
NHC = DH // 128
SG = 4
POOL_B = 204000
import os
KPH = int(os.environ.get("KPH", "4"))
KCH = int(os.environ.get("KCH", "8"))
KSUB = int(os.environ.get("KSUB", "99"))
KJ4 = int(os.environ.get("KJ4", "3"))
KWC = int(os.environ.get("KWC", "1"))
KP3 = int(os.environ.get("KP3", "9"))
KOFF = os.environ.get("KOFF", "dev")
KDBG = int(os.environ.get("KDBG", "0"))

_ESZ = {F32: 4, BF16: 2, I32: 4, U8: 1}


class View:
    def __init__(self, pool, off, n, dt, p0=0, p1=128):
        self.pool, self.off, self.n, self.dt = pool, off, n, dt
        self.p0, self.p1 = p0, p1
        self.esz = _ESZ[dt]

    def __call__(self, a=0, b=None, p0=None, p1=None):
        if b is None:
            b = self.n
        p0 = self.p0 if p0 is None else p0
        p1 = self.p1 if p1 is None else p1
        ap = self.pool[p0:p1, self.off + a * self.esz: self.off + b * self.esz]
        if self.dt != U8:
            ap = ap.bitcast(self.dt)
        return ap


def _finish(nc, sched):
    with nc.Block() as block:
        @block.sync
        def _(eng):
            for step in sched["D"]:
                step(eng)

        @block.vector
        def _(eng):
            for step in sched["V"]:
                step(eng)

        @block.scalar
        def _(eng):
            for step in sched["S"]:
                step(eng)

        @block.tensor
        def _(eng):
            for step in sched["T"]:
                step(eng)

        @block.gpsimd
        def _(eng):
            for step in sched["G"]:
                step(eng)
    return nc


def build_nc():
    nc = bass.Bass("TRN2", target_bir_lowering=False)

    def ein(name, shape, dt=F32):
        return nc.dram_tensor(name, shape, dt, kind="ExternalInput")

    x_in_h = ein("x_fm", [D, T])
    W_in_h = ein("W_in", [D, 2 * DI])
    b_in_h = ein("b_inp", [128, 32])
    convw_h = ein("conv_wp", [128, NDC * 4])
    convb_h = ein("conv_bp", [128, NDC])
    W_xp_h = ein("W_xpp", [128, NDC * 32])
    b_xp_h = ein("b_xp", [2 * DS])
    W_dt_h = ein("W_dt", [DI, DI])
    b_dt_h = ein("b_dtp", [128, NDC])
    A_log_h = ein("A_logp", [128, NDC * DS])
    Dsk_h = ein("D_skipp", [128, NDC])
    W_out_h = ein("W_out", [DI, D])
    b_out_h = ein("b_outp", [128, NMC])
    alpha_h = ein("alphap", [128, NMC])
    beta_h = ein("betap", [128, NMC])
    rw_h = ein("router_wp", [128, NMC * 8])
    rb_h = ein("router_b", [E])
    w1_h = ein("w1e", [D, DH])
    b1_h = ein("b1ep", [128, NHC])
    w2_h = ein("w2e", [DH, D])
    b2_h = ein("b2e", [D])
    c_h = ein("c_row", [128, 8])
    wmod_h = ein("wmod", [D, 6 * D])
    identb_h = ein("identb", [128, 128], BF16)
    identf_h = ein("identf", [128, 128], F32)
    cols_h = ein("colconsts", [128, 4], BF16)
    onerow_h = ein("onerow", [1, 128], BF16)
    selmat_h = ein("selmat", [128, 128], F32)
    rowsel_h = ein("rowsel", [32, 32 * 128], BF16)
    iota8_h = ein("iota8", [128, 256], F32)
    myrows_h = ein("myrows", [128, 1], I32)
    mysel0_h = ein("mysel0", [128, 1], I32)
    mysel1_h = ein("mysel1", [128, 1], I32)
    fconst_h = ein("fconst", [128, 4], F32)
    offhost_h = ein("offhost", [128, 256], I32)

    out_h = nc.dram_tensor("out", [T, D], F32, kind="ExternalOutput")
    if KDBG:
        dbg_oc = nc.dram_tensor("dbg_oc", [128, 256], I32, kind="ExternalOutput")
        dbg_off = nc.dram_tensor("dbg_off", [128, 2048], I32, kind="ExternalOutput")
        dbg_pos = nc.dram_tensor("dbg_pos", [128, 2048], F32, kind="ExternalOutput")
        dbg_msk = nc.dram_tensor("dbg_msk", [128, 2048], F32, kind="ExternalOutput")
        dbg_ltm = nc.dram_tensor("dbg_ltm", [128, 256], F32, kind="ExternalOutput")
        dbg_m1 = nc.dram_tensor("dbg_m1", [128, 256], F32, kind="ExternalOutput")
        dbg_xm = nc.dram_tensor("dbg_xm", [T, D], BF16, kind="ExternalOutput")
        dbg_h2s = nc.dram_tensor("dbg_h2s", [T, D], BF16, kind="ExternalOutput")
        dbg_ptm = nc.dram_tensor("dbg_ptm", [128, 256], F32, kind="ExternalOutput")
        dbg_gtm = nc.dram_tensor("dbg_gtm", [128, 256], F32, kind="ExternalOutput")

    st_Win = nc.dram_tensor("st_Win", [16, 128, 2048], BF16)
    st_Wdt = nc.dram_tensor("st_Wdt", [16, 128, 2048], BF16)
    st_Wout = nc.dram_tensor("st_Wout", [8, 128, 2048], BF16)
    st_w1 = nc.dram_tensor("st_w1", [NMC, 128, DH], BF16)
    st_w2 = nc.dram_tensor("st_w2", [NHC, 128, D], BF16)
    h2stage = nc.dram_tensor("h2stage", [T, D], BF16)
    xmid = nc.dram_tensor("xmid", [T, D], BF16)
    gm_in = nc.dram_tensor("gm_in", [16, T], F32)
    gm_all = nc.dram_tensor("gm_all", [128, T], F32)
    h2_all = nc.dram_tensor("h2_all", [B * T, D], BF16)
    pos_dram = nc.dram_tensor("pos_dram", [2048, 128], F32)
    offs_dram = nc.dram_tensor("offs_dram", [1024, 256], I32)
    buf_dram = nc.dram_tensor("buf_dram", [CAP + 1, D], BF16)
    outstage = nc.dram_tensor("outstage", [CAP, D], BF16)
    out_all = nc.dram_tensor("out_all", [E * CAP, D], BF16)

    import contextlib
    ctx = contextlib.ExitStack()
    semT = ctx.enter_context(nc.semaphore("semT"))
    semV = ctx.enter_context(nc.semaphore("semV"))
    semS = ctx.enter_context(nc.semaphore("semS"))
    semD = ctx.enter_context(nc.semaphore("semD"))
    semG = ctx.enter_context(nc.semaphore("semG"))
    semC = ctx.enter_context(nc.semaphore("semC"))
    semSA = ctx.enter_context(nc.semaphore("semSA"))
    semSB = ctx.enter_context(nc.semaphore("semSB"))
    semPO = ctx.enter_context(nc.semaphore("semPO"))
    semW0 = ctx.enter_context(nc.semaphore("semW0"))
    semW1 = ctx.enter_context(nc.semaphore("semW1"))
    semE0 = ctx.enter_context(nc.semaphore("semE0"))
    semE1 = ctx.enter_context(nc.semaphore("semE1"))
    semE2 = ctx.enter_context(nc.semaphore("semE2"))
    semE3 = ctx.enter_context(nc.semaphore("semE3"))
    semOT = ctx.enter_context(nc.semaphore("semOT"))
    semWM = ctx.enter_context(nc.semaphore("semWM"))
    SEM = {"T": semT, "V": semV, "S": semS, "D": semD, "G": semG,
           "C": semC, "SA": semSA, "SB": semSB, "PO": semPO,
           "W0": semW0, "W1": semW1, "E0": semE0, "E1": semE1,
           "E2": semE2, "E3": semE3, "OT": semOT, "WM": semWM}

    pool = ctx.enter_context(nc.sbuf_tensor("pool", [128, POOL_B], U8))
    offcol_t = ctx.enter_context(nc.sbuf_tensor("offcol_t", [128, 256], I32))
    myrows_t = ctx.enter_context(nc.sbuf_tensor("myrows_t", [128, 1], I32))
    mysel0_t = ctx.enter_context(nc.sbuf_tensor("mysel0_t", [128, 1], I32))
    mysel1_t = ctx.enter_context(nc.sbuf_tensor("mysel1_t", [128, 1], I32))
    r0i_t = ctx.enter_context(nc.sbuf_tensor("r0i_t", [128, 32], I32))
    r1i_t = ctx.enter_context(nc.sbuf_tensor("r1i_t", [128, 32], I32))
    posr0_t = ctx.enter_context(nc.sbuf_tensor("posr0_t", [128, 128], F32))
    posr1_t = ctx.enter_context(nc.sbuf_tensor("posr1_t", [128, 128], F32))
    g0v_t = ctx.enter_context(nc.sbuf_tensor("g0v_t", [128, 1024], BF16))
    g1v_t = ctx.enter_context(nc.sbuf_tensor("g1v_t", [128, 1024], BF16))

    class TW:
        def __init__(self, t):
            self.t = t
        def __call__(self, a=0, b=None, p0=0, p1=128):
            if b is None:
                return self.t[p0:p1, :]
            return self.t[p0:p1, a:b]
    _off = [0]

    def alloc(n, dt, p0=0, p1=128):
        off = _off[0]
        sz = (n * _ESZ[dt] + 31) & ~31
        _off[0] = off + sz
        assert _off[0] <= POOL_B, f"pool overflow {_off[0]}"
        return View(pool, off, n, dt, p0, p1)

    identb = alloc(128, BF16)
    identf = alloc(128, F32)
    colc = alloc(4, BF16)
    onerow = alloc(128, BF16, 0, 1)
    selmat = alloc(128, F32)
    rowsel = alloc(32 * 128, BF16, 0, 32)
    iota8 = alloc(256, F32)
    myrows = TW(myrows_t)
    mysel0 = TW(mysel0_t)
    mysel1 = TW(mysel1_t)
    vb_in = alloc(32, F32)
    vconvw = alloc(NDC * 4, F32)
    vconvb = alloc(NDC, F32)
    vb_xp = alloc(1, F32, 0, 32)
    vb_dt = alloc(NDC, F32)
    vA = alloc(NDC * DS, F32)
    vD = alloc(NDC, F32)
    vb_out = alloc(NMC, F32)
    valpha = alloc(NMC, F32)
    vbeta = alloc(NMC, F32)
    vb1 = alloc(NHC, F32)
    vrb = alloc(1, F32, 0, 8)
    csil = alloc(8, F32)
    modc = alloc(48, F32)
    cac = alloc(8, F32)
    cbc = alloc(8, F32)
    wxp = alloc(NDC * 32, BF16)
    wrt = alloc(NMC * 8, BF16)
    state = alloc(NDC * DS, F32)
    ltm = alloc(256, F32)
    gtm = alloc(256, F32)
    m1c = alloc(256, F32)
    m2c = alloc(256, F32)
    rsc0 = alloc(256, F32)
    rsc1 = alloc(256, F32)
    rsc2 = alloc(32, F32)
    rsc3 = alloc(32, F32)
    b2row = alloc(128, F32, 0, 8)
    b2rowb = alloc(128, BF16, 0, 8)
    gbrow = alloc(128, F32, 0, 8)
    gbrowb = alloc(128, BF16, 0, 8)
    b2bc = alloc(D, BF16)
    gbbc = alloc(D, BF16)
    postm = alloc(256, F32)
    offcol = TW(offcol_t)
    fco = alloc(4, F32)
    totc = alloc(1, F32)
    offc = alloc(1, F32)
    e0c = alloc(32, F32)
    p0c = alloc(32, F32)
    w0c = alloc(32, F32)
    e1c = alloc(32, F32)
    p1c = alloc(32, F32)
    w1c = alloc(32, F32)
    r0i = TW(r0i_t)
    r1i = TW(r1i_t)
    gmev = alloc(512, F32, 0, 16)
    gmslab = alloc(512, F32)
    zrow = alloc(1024, BF16)
    xtail = alloc(NDC * 3, BF16)
    persist_end = _off[0]

    _off[0] = (persist_end + 31) & ~31
    arena = _off[0]
    xs = alloc(NMC * TC, BF16)
    tn = alloc(NMC * TC, BF16)
    hmod = alloc(NMC * TC, BF16)
    xin = alloc(NDC * (TC + 3), BF16)
    zs = alloc(NDC * TC, BF16)
    xc = alloc(NDC * TC, BF16)
    dxc = alloc(NDC * TC, BF16)
    ys = alloc(NDC * TC, BF16)
    Bb0 = alloc(SG * TC, BF16)
    Cb0 = alloc(SG * TC, BF16)
    dA0 = alloc(SG * TC, BF16)
    dA1 = alloc(SG * TC, BF16)
    dBxb = alloc(SG * TC, BF16)
    Hsb = alloc(SG * TC, BF16)
    wbuf0 = alloc(2048, BF16)
    wbuf1 = alloc(2048, BF16)
    srow = alloc(2048, F32, 0, 1)
    srowb = alloc(1024, BF16, 0, 1)
    bcm = alloc(TC, BF16)
    bcr = alloc(TC, BF16)
    bcrow = alloc(TC, BF16, 0, 32)
    l8 = alloc(TC, F32, 0, 8)
    evrow0 = alloc(1024, BF16)
    evrow1 = alloc(1024, BF16)
    evrow2 = alloc(1024, BF16)
    evrow3 = alloc(1024, BF16)
    mamba_end = _off[0]
    spl = View(pool, bcm.off, 512, F32)
    dl = View(pool, xin.off, NDC * TC, BF16)
    outs = View(pool, zs.off, NMC * TC, BF16)
    wmscr = View(pool, xs.off, 6144, F32)
    modacc = View(pool, ys.off, 48, F32)

    _off[0] = arena
    w1sb = alloc(NMC * DH, BF16)
    w2sb = alloc(NHC * D, BF16)
    rbase = _off[0]
    maskg = alloc(2048, F32)
    cumg = alloc(2048, F32)
    posg = alloc(2048, F32)
    offsg = alloc(2048, F32)
    offsi = View(pool, maskg.off, 2048, I32)
    r1_end = _off[0]
    posr0 = TW(posr0_t)
    posr1 = TW(posr1_t)
    _off[0] = rbase
    h1sb = alloc(NHC * 256, BF16)
    buffm = alloc(8 * 256, BF16)
    bft = alloc(2 * 1024, BF16)
    otm = alloc(1024, BF16)
    r1b_end = _off[0]
    scat0 = View(pool, h1sb.off, 4 * D, BF16)
    scat1 = View(pool, h1sb.off + 8192, 4 * D, BF16)
    g0v = TW(g0v_t)
    g1v = TW(g1v_t)
    xtt = View(pool, bft.off, D, BF16)
    yout = View(pool, bft.off + 2048, D, F32)
    moe_end = max(r1_end, r1b_end)
    assert max(mamba_end, moe_end) <= POOL_B, (mamba_end, moe_end)

    pmm0 = ctx.enter_context(nc.psum_tensor("pmm0", [128, 512], F32))
    pmm1 = ctx.enter_context(nc.psum_tensor("pmm1", [128, 512], F32))
    pmm2 = ctx.enter_context(nc.psum_tensor("pmm2", [128, 512], F32))
    pbc = ctx.enter_context(nc.psum_tensor("pbc", [128, 512], F32))
    pstat = ctx.enter_context(nc.psum_tensor("pstat", [128, 512], F32))
    ptrf = ctx.enter_context(nc.psum_tensor("ptrf", [128, 128], F32))
    ptr0 = ctx.enter_context(nc.psum_tensor("ptr0", [128, 1024], BF16))
    ptr1 = ctx.enter_context(nc.psum_tensor("ptr1", [128, 1024], BF16))

    sched = {k: [] for k in "TVSDG"}
    cnt = {k: 0 for k in "TVSDGC"}
    for _k in ("SA", "SB", "PO", "W0", "W1", "E0", "E1", "E2", "E3", "OT", "WM"):
        cnt[_k] = 0

    def emit(e, fn, inc=1, **waits):
        w = [(SEM[k], v) for k, v in waits.items() if v]

        def step(eng, fn=fn, w=w, inc=inc, e=e):
            for sem, val in w:
                eng.wait_ge(sem, val)
            r = fn(eng)
            if inc and r is not None:
                r.then_inc(SEM[e], inc)
        q = {"C": "G", "SA": "D", "SB": "D", "PO": "D", "W0": "D",
             "W1": "D", "E0": "D", "E1": "D", "E2": "D", "E3": "D",
             "OT": "D", "WM": "D"}.get(e, e)
        sched[q].append(step)
        if inc:
            cnt[e] += inc
        return cnt[e]

    def dma(q, out, in_, **waits):
        return emit(q, lambda eng, o=out, i=in_: eng.dma_start(out=o, in_=i),
                    inc=16, **waits)

    M = {"mi": 0}

    for dst, src in [
        (identb(), identb_h.ap()), (identf(), identf_h.ap()),
        (colc(), cols_h.ap()), (onerow(), onerow_h.ap()),
        (selmat(), selmat_h.ap()), (rowsel(), rowsel_h.ap()),
        (iota8(), iota8_h.ap()), (myrows(), myrows_h.ap()),
        (mysel0(), mysel0_h.ap()), (mysel1(), mysel1_h.ap()),
        (csil(), c_h.ap()),
        (vb_in(), b_in_h.ap()),
        (vconvw(), convw_h.ap()),
        (vconvb(), convb_h.ap()),
        (vb_xp(), b_xp_h.ap().rearrange("(s o) -> s o", o=1)),
        (vb_dt(), b_dt_h.ap()),
        (vA(), A_log_h.ap()),
        (vD(), Dsk_h.ap()),
        (vb_out(), b_out_h.ap()),
        (valpha(), alpha_h.ap()),
        (vbeta(), beta_h.ap()),
        (vb1(), b1_h.ap()),
        (vrb(), rb_h.ap().rearrange("(s o) -> s o", o=1)),
        (b2row(), b2_h.ap().rearrange("(a b) -> a b", b=128)),
        (fconst(), fconst_h.ap()) if False else (fco(), fconst_h.ap()),
    ]:
        dma("D", dst, src)
    d_consts = cnt["D"]

    for k in range(NMC):
        dma("G", st_Win.ap()[:, :, k * 256:(k + 1) * 256]
            .rearrange("e p c -> p e c"),
            W_in_h.ap()[k * 128:(k + 1) * 128, :]
            .rearrange("p (e c) -> p e c", c=256))
    for k in range(NDC):
        dma("G", st_Wdt.ap()[:, :, k * 128:(k + 1) * 128]
            .rearrange("e p c -> p e c"),
            W_dt_h.ap()[k * 128:(k + 1) * 128, :]
            .rearrange("p (e c) -> p e c", c=128))
    for k in range(NDC):
        dma("G", st_Wout.ap()[:, :, k * 128:(k + 1) * 128]
            .rearrange("e p c -> p e c"),
            W_out_h.ap()[k * 128:(k + 1) * 128, :]
            .rearrange("p (e c) -> p e c", c=128))
    dma("G", wxp(), W_xp_h.ap())
    g_mamba_w = dma("G", wrt(), rw_h.ap())
    dma("G", st_w1.ap().rearrange("a p m -> (a p) m"), w1_h.ap())
    g_moe_w = dma("G", st_w2.ap().rearrange("a p m -> (a p) m"), w2_h.ap())

    s_ = emit("S", lambda eng: eng.activation(out=vA(), in_=vA(), func=AF.Exp),
              D=d_consts)
    emit("V", lambda eng: eng.tensor_scalar(
        out=vA(), in0=vA(), scalar1=-1.0, scalar2=None, op0=OP.mult), S=s_)
    s_c = emit("S", lambda eng: eng.activation(out=csil(), in_=csil(),
                                               func=AF.Silu), D=d_consts)

    v_acc = 0
    for k in range(NMC):
        d_w = emit("WM", lambda eng, k=k: eng.dma_start(
            out=wmscr(), in_=wmod_h.ap()[k * 128:(k + 1) * 128, :]),
            inc=16, V=v_acc)

        def mmk(eng, k=k):
            r = None
            for mc in range(48):
                r = eng.matmul(pstat[:, mc:mc + 1],
                               wmscr(mc * 128, (mc + 1) * 128),
                               csil(k, k + 1), start=True, stop=True)
            return r
        t_mm = emit("T", mmk, WM=d_w, S=s_c, V=v_acc)
        if k == 0:
            v_acc = emit("V", lambda eng: eng.tensor_copy(
                out=modacc(), in_=pstat[:, 0:48]), T=t_mm)
        else:
            v_acc = emit("V", lambda eng: eng.tensor_tensor(
                out=modacc(), in0=modacc(), in1=pstat[:, 0:48], op=OP.add),
                T=t_mm)
    M["pstat_V"] = v_acc
    v_mod0 = emit("V", lambda eng: eng.tensor_copy(out=modc(), in_=modacc()),
                  V=v_acc)
    s_ab = emit("S", lambda eng: (
        eng.activation(out=rsc2(0, 8), in_=modc(16, 24), func=AF.Abs),
        eng.activation(out=rsc3(0, 8), in_=modc(40, 48), func=AF.Abs))[-1],
        V=v_mod0)

    def _cl1(eng):
        eng.tensor_scalar(out=rsc2(0, 8), in0=rsc2(0, 8), scalar1=0.25,
                          scalar2=None, op0=OP.max)
        eng.tensor_scalar(out=rsc3(0, 8), in0=rsc3(0, 8), scalar1=0.25,
                          scalar2=None, op0=OP.max)
        eng.tensor_scalar(out=rsc0(0, 8), in0=modc(16, 24), scalar1=0.0,
                          scalar2=None, op0=OP.is_ge)
        return eng.tensor_scalar(out=rsc0(8, 16), in0=modc(40, 48),
                                 scalar1=0.0, scalar2=None, op0=OP.is_ge)
    v_mod = emit("V", _cl1, S=s_ab)
    v_mod = emit("V", lambda eng: eng.tensor_scalar(
        out=rsc0(0, 16), in0=rsc0(0, 16), scalar1=2.0, scalar2=-1.0,
        op0=OP.mult, op1=OP.add), V=v_mod)

    def _cl3(eng):
        eng.tensor_tensor(out=modc(16, 24), in0=rsc0(0, 8), in1=rsc2(0, 8),
                          op=OP.mult)
        eng.tensor_tensor(out=modc(40, 48), in0=rsc0(8, 16), in1=rsc3(0, 8),
                          op=OP.mult)
        eng.tensor_scalar(out=cac(), in0=modc(8, 16), scalar1=1.0,
                          scalar2=None, op0=OP.add)
        return eng.tensor_scalar(out=cbc(), in0=modc(32, 40), scalar1=1.0,
                                 scalar2=None, op0=OP.add)
    v_mod = emit("V", _cl3, V=v_mod)

    v_z = emit("V", lambda eng: eng.memset(zrow(), 0.0))
    for r in range(0, CAP + 1, 128):
        n = min(128, CAP + 1 - r)
        dma("G", buf_dram.ap()[r:r + n, :], zrow()[0:n, :], V=v_z)
    g_bufz = cnt["G"]
    emit("V", lambda eng: eng.memset(state(), 0.0), D=d_consts)
    v_tail0 = emit("V", lambda eng: eng.memset(xtail(), 0.0))

    def bcast3(view):
        return view().unsqueeze(1).broadcast_to([128, NMC, TC])

    def stat_ln(xview, s_sq, msq_col, unbiased, eps, xV=0, xG=0):
        def mm_mean(eng):
            r = None
            for k in range(NMC):
                r = eng.matmul(pstat[0:1, :], colc(1, 2),
                               xview(k * TC, (k + 1) * TC),
                               start=(k == 0), stop=(k == NMC - 1))
            return r

        def mm_msq(eng):
            r = None
            for k in range(NMC):
                r = eng.matmul(pbc[0:1, :], colc(msq_col, msq_col + 1),
                               tn(k * TC, (k + 1) * TC),
                               start=(k == 0), stop=(k == NMC - 1))
            return r
        emit("T", mm_mean, S=M.get("pstat_S", 0),
             V=max(M.get("pstat_V", 0), xV), G=xG)
        t2 = emit("T", mm_msq, S=max(s_sq, M.get("pbc_S", 0)))

        def _stats(eng):
            eng.tensor_copy(out=srow(0, 512), in_=pstat[0:1, :])
            eng.tensor_copy(out=srow(512, 1024), in_=pbc[0:1, :])
            eng.tensor_tensor(out=srow(1536, 2048), in0=srow(0, 512),
                              in1=srow(0, 512), op=OP.mult)
            if unbiased:
                eng.tensor_scalar(out=srow(1536, 2048), in0=srow(1536, 2048),
                                  scalar1=1024.0 / 1023.0, scalar2=None,
                                  op0=OP.mult)
            return eng.tensor_tensor(out=srow(512, 1024), in0=srow(512, 1024),
                                     in1=srow(1536, 2048), op=OP.subtract)
        v1 = emit("V", _stats, T=t2)
        M["pstat_V"] = v1
        if not unbiased:
            s1 = emit("S", lambda eng: (
                eng.activation(out=srow(1536, 2048), in_=srow(512, 1024),
                               func=AF.Ln, bias=fco(0, 1, 0, 1)),
                eng.activation(out=srow(1024, 1536), in_=srow(1536, 2048),
                               func=AF.Exp, scale=fco(1, 2, 0, 1)))[-1], V=v1)
            v3 = emit("V", lambda eng: eng.nop(), S=s1)
        else:
            s1 = emit("S", lambda eng: (
                eng.activation(out=srow(1536, 2048), in_=srow(512, 1024),
                               func=AF.Ln),
                eng.activation(out=srow(1536, 2048), in_=srow(1536, 2048),
                               func=AF.Exp, scale=fco(2, 3, 0, 1)))[-1], V=v1)
            v2 = emit("V", lambda eng: eng.tensor_scalar(
                out=srow(1536, 2048), in0=srow(1536, 2048), scalar1=eps,
                scalar2=None, op0=OP.add), S=s1)
            v3 = emit("V", lambda eng: eng.reciprocal(
                out=srow(1024, 1536), in_=srow(1536, 2048)), V=v2)
        v4 = emit("V", lambda eng: (
            eng.tensor_copy(out=srowb(0, 512), in_=srow(0, 512)),
            eng.tensor_copy(out=srowb(512, 1024), in_=srow(1024, 1536)))[-1],
            V=v3)
        t3 = emit("T", lambda eng: eng.matmul(
            pbc[:, :], onerow(), srowb(0, 512), start=True, stop=True),
            V=v4, S=M.get("pbc_S", 0))
        s2 = emit("S", lambda eng: eng.activation(out=bcm(), in_=pbc[:, :],
                                                  func=AF.Copy), T=t3)
        t4 = emit("T", lambda eng: eng.matmul(
            pbc[:, :], onerow(), srowb(512, 1024), start=True, stop=True),
            S=s2)
        s3 = emit("S", lambda eng: eng.activation(out=bcr(), in_=pbc[:, :],
                                                  func=AF.Copy), T=t4)
        M["pbc_S"] = s3
        return s3

    def pm_wait(b):
        return dict(S=M.get(f"pmm{b}_S", 0), V=M.get(f"pmm{b}_V", 0))

    def stream_mm(stash, total_cols, slice_cols, k_cnt, rhs_view, evac_fn,
                  rhs_V, rhs_S=0):
        nsl = total_cols // slice_cols
        mper = slice_cols // 128
        wb = [wbuf0, wbuf1]
        dwait = {}
        for e_s in range(min(2, nsl)):
            dwait[e_s] = emit(
                ["W0", "W1"][e_s % 2],
                lambda eng, e_s=e_s: eng.dma_start(
                    out=wb[e_s % 2](0, k_cnt * slice_cols),
                    in_=stash.ap()[e_s]),
                inc=16, T=M.get(f"wb{e_s % 2}_T", 0), G=g_mamba_w)
        s_last = 0
        for e_s in range(nsl):
            t_mm = 0
            for m in range(mper):
                bi = M["mi"] % 3
                bank = [pmm0, pmm1, pmm2][bi]
                pw = pm_wait(bi)

                def mm(eng, e_s=e_s, m=m, bank=bank):
                    r = None
                    for k in range(k_cnt):
                        r = eng.matmul(
                            bank[:, :],
                            wb[e_s % 2](k * slice_cols + m * 128,
                                        k * slice_cols + (m + 1) * 128),
                            rhs_view(k * TC, (k + 1) * TC),
                            start=(k == 0), stop=(k == k_cnt - 1))
                    return r
                t_mm = emit("T", mm, V=max(rhs_V, pw["V"]),
                            S=max(pw["S"], rhs_S),
                            **{["W0", "W1"][e_s % 2]: dwait[e_s]})
                mc = e_s * mper + m
                s_last = emit("S", lambda eng, mc=mc, bank=bank:
                              evac_fn(eng, mc, bank), T=t_mm)
                M[f"pmm{bi}_S"] = s_last
                M["mi"] += 1
            M[f"wb{e_s % 2}_T"] = t_mm
            nxt = e_s + 2
            if nxt < nsl:
                dwait[nxt] = emit(
                    ["W0", "W1"][nxt % 2],
                    lambda eng, nxt=nxt: eng.dma_start(
                        out=wb[nxt % 2](0, k_cnt * slice_cols),
                        in_=stash.ap()[nxt]),
                    inc=16, T=M.get(f"wb{nxt % 2}_T", 0))
        return s_last

    for c in range(KCH if KPH >= 1 else 0):
        c0 = c * TC
        prevT = M.get("chunk_T", 0)
        prevV = M.get("chunk_V", 0)
        prevS = M.get("chunk_S", 0)
        g_x = dma("G", xs().rearrange("p (a t) -> p a t", t=TC),
                  x_in_h.ap()[:, c0:c0 + TC].rearrange("(a p) t -> p a t",
                                                       p=128),
                  T=prevT, V=max(prevV, v_mod if c == 0 else 0))
        s_sq = emit("S", lambda eng: eng.activation(out=tn(), in_=xs(),
                                                    func=AF.Square),
                    G=g_x, T=prevT, V=prevV, S=prevS)
        s_bc = stat_ln(xs, s_sq, 1, False, 1e-5, xG=g_x)

        def _norm(src):
            def f(eng, src=src):
                bm = bcast3(bcm)
                br = bcast3(bcr)
                t3 = tn().rearrange("p (a t) -> p a t", t=TC)
                x3 = src().rearrange("p (a t) -> p a t", t=TC)
                eng.tensor_tensor(out=t3, in0=x3, in1=bm, op=OP.subtract)
                return eng.tensor_tensor(out=t3, in0=t3, in1=br, op=OP.mult)
            return f
        v_n1 = emit("V", _norm(xs), S=s_bc)

        def _hmod(eng):
            r = None
            for dc in range(NMC):
                r = eng.tensor_scalar(
                    out=hmod(dc * TC, (dc + 1) * TC),
                    in0=tn(dc * TC, (dc + 1) * TC),
                    scalar1=cac(dc, dc + 1), scalar2=modc(dc, dc + 1),
                    op0=OP.mult, op1=OP.add)
            return r
        v_hm = emit("V", _hmod, V=v_n1, T=prevT)

        if KSUB < 1:
            continue
        def ev_xz(eng, mc, bank):
            if mc < 16:
                o = xin(mc * (TC + 3) + 3, mc * (TC + 3) + 3 + TC)
            else:
                o = zs((mc - 16) * TC, (mc - 15) * TC)
            return eng.activation(out=o, in_=bank[:, :], func=AF.Identity,
                                  bias=vb_in(mc, mc + 1))
        s_xz = stream_mm(st_Win, 2 * DI, 256, NMC, hmod, ev_xz, v_hm)

        if KSUB < 2:
            continue
        v_rt = emit("V", lambda eng: eng.tensor_copy(
            out=xin().rearrange("p (a t) -> p a t", t=TC + 3)[:, :, 0:3],
            in_=xtail().rearrange("p (a t) -> p a t", t=3)),
            S=s_xz, V=v_tail0 if c == 0 else 0)

        def _conv(eng):
            r = None
            for dc in range(NDC):
                base = dc * (TC + 3)
                eng.tensor_scalar(
                    out=xc(dc * TC, (dc + 1) * TC),
                    in0=xin(base, base + TC),
                    scalar1=vconvw(dc * 4, dc * 4 + 1),
                    scalar2=vconvb(dc, dc + 1), op0=OP.mult, op1=OP.add)
                for k2 in (1, 2, 3):
                    r = eng.scalar_tensor_tensor(
                        out=xc(dc * TC, (dc + 1) * TC),
                        in0=xin(base + k2, base + k2 + TC),
                        scalar=vconvw(dc * 4 + k2, dc * 4 + k2 + 1),
                        in1=xc(dc * TC, (dc + 1) * TC),
                        op0=OP.mult, op1=OP.add)
            return r
        v_cv = emit("V", _conv, V=v_rt)
        v_tail = emit("V", lambda eng: eng.tensor_copy(
            out=xtail().rearrange("p (a t) -> p a t", t=3),
            in_=xin().rearrange("p (a t) -> p a t", t=TC + 3)[:, :, TC:TC + 3]),
            V=v_cv)
        s_si = emit("S", lambda eng: eng.activation(out=xc(), in_=xc(),
                                                    func=AF.Silu), V=v_cv)
        s_zsil = emit("S", lambda eng: eng.activation(out=zs(), in_=zs(),
                                                      func=AF.Silu), S=s_xz,
                      V=M.get("zs_V", 0))

        if KSUB < 3:
            continue
        def mm_xp(eng):
            r = None
            for k in range(NDC):
                r = eng.matmul(pstat[0:32, :], wxp(k * 32, (k + 1) * 32),
                               xc(k * TC, (k + 1) * TC),
                               start=(k == 0), stop=(k == NDC - 1))
            return r
        t_xp = emit("T", mm_xp, S=max(s_si, M.get("pstat_S", 0)),
                    V=M.get("pstat_V", 0), G=g_mamba_w if c == 0 else 0)
        s_bcr2 = emit("S", lambda eng: eng.activation(
            out=bcrow(), in_=pstat[0:32, :], func=AF.Identity, bias=vb_xp()),
            T=t_xp)
        M["pstat_S"] = s_bcr2

        if KSUB < 4:
            continue
        def ev_dt(eng, mc, bank):
            eng.activation(out=spl(), in_=bank[:, :], func=AF.Exp,
                           bias=vb_dt(mc, mc + 1))
            return eng.activation(out=dl(mc * TC, (mc + 1) * TC),
                                  in_=spl(), func=AF.Ln, bias=1.0)
        s_dt = stream_mm(st_Wdt, DI, 128, NDC, xc, ev_dt,
                         max(v_cv, v_tail), rhs_S=s_si)

        if KSUB < 5:
            continue
        v_dxc = emit("V", lambda eng: eng.tensor_tensor(
            out=dxc(), in0=dl(), in1=xc(), op=OP.mult), S=s_dt)

        def _yinit(eng):
            r = None
            for dc in range(NDC):
                r = eng.tensor_scalar(out=ys(dc * TC, (dc + 1) * TC),
                                      in0=xc(dc * TC, (dc + 1) * TC),
                                      scalar1=vD(dc, dc + 1), scalar2=None,
                                      op0=OP.mult)
            return r
        v_yi = emit("V", _yinit, V=v_dxc)

        if KSUB < 6:
            continue
        v_sc = v_yi
        for sg in range(DS // SG):
            Bbv, Cbv = Bb0, Cb0
            bbfree = v_sc
            s_bb = M.get("pbc_S", 0)
            for s4 in range(SG):
                s_idx = sg * SG + s4
                t_b = emit("T", lambda eng, s_idx=s_idx: eng.matmul(
                    pbc[:, :], rowsel(s_idx * 128, (s_idx + 1) * 128),
                    bcrow(), start=True, stop=True),
                    S=max(s_bcr2, s_bb),
                    V=bbfree if s4 == 0 else 0)
                s_bb = emit("S", lambda eng, s4=s4, Bbv=Bbv: eng.activation(
                    out=Bbv(s4 * TC, (s4 + 1) * TC), in_=pbc[:, :],
                    func=AF.Copy), T=t_b)
                t_cq = emit("T", lambda eng, s_idx=s_idx: eng.matmul(
                    pbc[:, :], rowsel((16 + s_idx) * 128, (17 + s_idx) * 128),
                    bcrow(), start=True, stop=True), S=s_bb)
                s_bb = emit("S", lambda eng, s4=s4, Cbv=Cbv: eng.activation(
                    out=Cbv(s4 * TC, (s4 + 1) * TC), in_=pbc[:, :],
                    func=AF.Copy), T=t_cq)
            M["pbc_S"] = s_bb
            for dc in range(NDC):
                dAv = dA0 if dc % 2 == 0 else dA1
                dafree = M.get(f"da{dc % 2}_V", 0)

                def _dA(eng, dc=dc, sg=sg, dAv=dAv):
                    r = None
                    for s4 in range(SG):
                        r = eng.activation(
                            out=dAv(s4 * TC, (s4 + 1) * TC),
                            in_=dl(dc * TC, (dc + 1) * TC), func=AF.Exp,
                            scale=vA(dc * DS + sg * SG + s4,
                                     dc * DS + sg * SG + s4 + 1))
                    return r
                s_dA = emit("S", _dA, S=s_bb, V=dafree)

                def _scan(eng, dc=dc, sg=sg, dAv=dAv, Bbv=Bbv):
                    eng.tensor_tensor(
                        out=dBxb().rearrange("p (a t) -> p a t", t=TC),
                        in0=dxc(dc * TC, (dc + 1) * TC).unsqueeze(1)
                        .broadcast_to([128, SG, TC]),
                        in1=Bbv().rearrange("p (a t) -> p a t", t=TC),
                        op=OP.mult)
                    r = None
                    for s4 in range(SG):
                        r = eng.tensor_tensor_scan(
                            out=Hsb(s4 * TC, (s4 + 1) * TC),
                            data0=dAv(s4 * TC, (s4 + 1) * TC),
                            data1=dBxb(s4 * TC, (s4 + 1) * TC),
                            initial=state(dc * DS + sg * SG + s4,
                                          dc * DS + sg * SG + s4 + 1),
                            op0=OP.mult, op1=OP.add)
                    return r
                v_mid = emit("V", _scan, S=s_dA)

                def _rdout(eng, dc=dc, sg=sg, Cbv=Cbv):
                    eng.tensor_copy(
                        out=state(dc * DS + sg * SG, dc * DS + sg * SG + SG),
                        in_=Hsb().rearrange("p (a t) -> p a t", t=TC)
                        [:, :, TC - 1:TC])
                    eng.tensor_tensor(out=dBxb(), in0=Hsb(), in1=Cbv(),
                                      op=OP.mult)
                    p3 = dBxb().rearrange("p (a t) -> p a t", t=TC)
                    r = None
                    for s4 in range(SG):
                        r = eng.tensor_tensor(out=ys(dc * TC, (dc + 1) * TC),
                                              in0=ys(dc * TC, (dc + 1) * TC),
                                              in1=p3[:, s4, :], op=OP.add)
                    return r
                v_sc = emit("V", _rdout, V=v_mid)
                M[f"da{dc % 2}_V"] = v_sc

        if KSUB < 7:
            continue
        v_g = emit("V", lambda eng: eng.tensor_tensor(
            out=ys(), in0=ys(), in1=zs(), op=OP.mult), V=v_sc, S=s_zsil)
        M["zs_V"] = v_g

        def ev_o(eng, mc, bank):
            return eng.activation(out=outs(mc * TC, (mc + 1) * TC),
                                  in_=bank[:, :], func=AF.Identity,
                                  bias=vb_out(mc, mc + 1))
        s_o = stream_mm(st_Wout, D, 128, NDC, ys, ev_o, v_g)

        if KSUB < 8:
            continue
        v_r2 = emit("V", lambda eng: eng.tensor_tensor(
            out=hmod(), in0=hmod(), in1=outs(), op=OP.add), S=s_o)
        s_sq2 = emit("S", lambda eng: eng.activation(out=tn(), in_=hmod(),
                                                     func=AF.Square), V=v_r2)
        s_bc2 = stat_ln(hmod, s_sq2, 2, True, 1e-6, xV=v_r2)
        v_n2 = emit("V", _norm(hmod), S=s_bc2)

        def _lnfin(eng):
            r = None
            for dc in range(NMC):
                eng.tensor_scalar(out=outs(dc * TC, (dc + 1) * TC),
                                  in0=tn(dc * TC, (dc + 1) * TC),
                                  scalar1=valpha(dc, dc + 1),
                                  scalar2=vbeta(dc, dc + 1),
                                  op0=OP.mult, op1=OP.add)
                r = eng.scalar_tensor_tensor(
                    out=xs(dc * TC, (dc + 1) * TC),
                    in0=outs(dc * TC, (dc + 1) * TC),
                    scalar=modc(16 + dc, 17 + dc),
                    in1=xs(dc * TC, (dc + 1) * TC), op0=OP.mult, op1=OP.add)
            return r
        v_xu = emit("V", _lnfin, V=v_n2)

        s_sq3 = emit("S", lambda eng: eng.activation(out=tn(), in_=xs(),
                                                     func=AF.Square), V=v_xu)
        s_bc3 = stat_ln(xs, s_sq3, 1, False, 1e-5, xV=v_xu)
        v_n3 = emit("V", _norm(xs), S=s_bc3)

        def _h2(eng):
            r = None
            for dc in range(NMC):
                r = eng.tensor_scalar(
                    out=hmod(dc * TC, (dc + 1) * TC),
                    in0=tn(dc * TC, (dc + 1) * TC),
                    scalar1=cbc(dc, dc + 1), scalar2=modc(24 + dc, 25 + dc),
                    op0=OP.mult, op1=OP.add)
            return r
        v_h2 = emit("V", _h2, V=v_n3)

        if KSUB < 9:
            continue
        def mm_rt(eng):
            r = None
            for k in range(NMC):
                r = eng.matmul(pstat[0:8, :], wrt(k * 8, (k + 1) * 8),
                               hmod(k * TC, (k + 1) * TC),
                               start=(k == 0), stop=(k == NMC - 1))
            return r
        t_rt = emit("T", mm_rt, V=max(v_h2, M.get("pstat_V", 0)),
                    S=M.get("pstat_S", 0), G=g_mamba_w if c == 0 else 0)
        s_l8 = emit("S", lambda eng: eng.activation(
            out=l8(), in_=pstat[0:8, :], func=AF.Identity, bias=vrb()), T=t_rt)
        M["pstat_S"] = s_l8
        t_lt = 0
        for tb in range(4):
            t_lt = emit("T", lambda eng, tb=tb: eng.transpose(
                ptrf[:, tb * 8:(tb + 1) * 8], l8(tb * 128, (tb + 1) * 128),
                identf(0, 8, 0, 8)), S=s_l8,
                V=M.get("ptrf_V", 0) if tb == 0 else 0)
        v_lt = emit("V", lambda eng, c=c: eng.tensor_copy(
            out=ltm(c * 32, (c + 1) * 32), in_=ptrf[:, 0:32]), T=t_lt)
        M["ptrf_V"] = v_lt

        ev = [evrow0, evrow1]
        evx = [evrow2, evrow3]
        for tb in range(4):
            t_h = 0
            for mc2 in range(NMC):
                t_h = emit("T", lambda eng, tb=tb, mc2=mc2: eng.transpose(
                    ptr0[:, mc2 * 128:(mc2 + 1) * 128],
                    hmod(mc2 * TC + tb * 128, mc2 * TC + (tb + 1) * 128),
                    identb()),
                    V=max(v_h2, M.get("ptr0_V", 0)) if mc2 == 0 else 0,
                    **{["E0", "E1"][tb % 2]:
                       M.get(f"evh{tb % 2}_D", 0) if mc2 == 0 else 0})
            v_eh = emit("V", lambda eng, tb=tb: eng.tensor_copy(
                out=ev[tb % 2](), in_=ptr0[:, :]), T=t_h)
            M["ptr0_V"] = v_eh
            M[f"evh{tb % 2}_D"] = emit(
                ["E0", "E1"][tb % 2],
                lambda eng, tb=tb, c0=c0: eng.dma_start(
                    out=h2stage.ap()[c0 + tb * 128:c0 + (tb + 1) * 128, :],
                    in_=ev[tb % 2]()), inc=16, V=v_eh)
            t_x = 0
            for mc2 in range(NMC):
                t_x = emit("T", lambda eng, tb=tb, mc2=mc2: eng.transpose(
                    ptr1[:, mc2 * 128:(mc2 + 1) * 128],
                    xs(mc2 * TC + tb * 128, mc2 * TC + (tb + 1) * 128),
                    identb()),
                    V=max(v_xu, M.get("ptr1_V", 0)) if mc2 == 0 else 0,
                    **{["E2", "E3"][tb % 2]:
                       M.get(f"evx{tb % 2}_D", 0) if mc2 == 0 else 0})
            v_ex = emit("V", lambda eng, tb=tb: eng.tensor_copy(
                out=evx[tb % 2](), in_=ptr1[:, :]), T=t_x)
            M["ptr1_V"] = v_ex
            M[f"evx{tb % 2}_D"] = emit(
                ["E2", "E3"][tb % 2],
                lambda eng, tb=tb, c0=c0: eng.dma_start(
                    out=xmid.ap()[c0 + tb * 128:c0 + (tb + 1) * 128, :],
                    in_=evx[tb % 2]()), inc=16, V=v_ex)
        M["chunk_T"] = cnt["T"]
        M["chunk_V"] = cnt["V"]
        M["chunk_S"] = cnt["S"]

    d_stage = cnt["D"]
    barT, barV, barS = cnt["T"], cnt["V"], cnt["S"]
    barE0, barE1, barE2, barE3 = cnt["E0"], cnt["E1"], cnt["E2"], cnt["E3"]
    if KPH < 2:
        d_final = cnt["D"]
        emit("D", lambda eng: None, inc=0, D=d_final, G=cnt["G"], T=cnt["T"],
             V=cnt["V"], S=cnt["S"])
        return _finish(nc, sched)

    c_h2 = emit("C", lambda eng: eng.collective_compute(
        "AllGather", OP.bypass, replica_groups=[list(range(8))],
        ins=[h2stage.ap().opt()], outs=[h2_all.ap().opt()]),
        D=d_stage, E0=barE0, E1=barE1)

    s_exp = emit("S", lambda eng: eng.activation(out=gtm(), in_=ltm(),
                                                 func=AF.Exp),
                 V=barV, T=barT)

    g3v = lambda: gtm().rearrange("p (a b) -> p a b", b=8)
    v_sm = emit("V", lambda eng: eng.tensor_reduce(
        out=rsc2(), in_=g3v(), axis=AX.X, op=OP.add), S=s_exp)
    v_sm = emit("V", lambda eng: eng.reciprocal(out=rsc2(), in_=rsc2()),
                V=v_sm)
    v_sm = emit("V", lambda eng: eng.tensor_tensor(
        out=g3v(), in0=g3v(), in1=rsc2().unsqueeze(2)
        .broadcast_to([128, 32, 8]), op=OP.mult), V=v_sm)

    def maskpass(srcv, dst, v_in):
        s3 = lambda: srcv().rearrange("p (a b) -> p a b", b=8)
        d3 = lambda: dst().rearrange("p (a b) -> p a b", b=8)
        r0 = lambda: rsc0().rearrange("p (a b) -> p a b", b=8)
        r1 = lambda: rsc1().rearrange("p (a b) -> p a b", b=8)
        v = emit("V", lambda eng: eng.tensor_reduce(
            out=rsc2(), in_=s3(), axis=AX.X, op=OP.max), V=v_in)
        v = emit("V", lambda eng: eng.tensor_tensor(
            out=d3(), in0=s3(), in1=rsc2().unsqueeze(2)
            .broadcast_to([128, 32, 8]), op=OP.is_ge), V=v)
        v = emit("V", lambda eng: eng.tensor_copy(out=r0(), in_=d3()), V=v)
        for sh, (sa, sd) in zip((1, 2, 4), ((r0, r1), (r1, r0), (r0, r1))):
            v = emit("V", lambda eng, sh=sh, sa=sa, sd=sd: (
                eng.tensor_copy(out=sd()[:, :, 0:sh], in_=sa()[:, :, 0:sh]),
                eng.tensor_tensor(out=sd()[:, :, sh:8], in0=sa()[:, :, sh:8],
                                  in1=sa()[:, :, 0:8 - sh], op=OP.add))[-1],
                V=v)
        v = emit("V", lambda eng: eng.tensor_scalar(
            out=rsc0(), in0=rsc1(), scalar1=1.0, scalar2=None, op0=OP.is_le),
            V=v)
        v = emit("V", lambda eng: eng.tensor_tensor(
            out=dst(), in0=dst(), in1=rsc0(), op=OP.mult), V=v)
        return v
    v_tk = maskpass(lambda: ltm(), m1c, v_sm)
    v_tk = emit("V", lambda eng: eng.scalar_tensor_tensor(
        out=gmslab(0, 256), in0=m1c(), scalar=-1000.0, in1=ltm(),
        op0=OP.mult, op1=OP.add), V=v_tk)
    v_tk = maskpass(lambda: gmslab(0, 256), m2c, v_tk)

    def _gmpack(eng):
        g4 = gmslab().rearrange("p (a b) -> p a b", b=16)
        eng.tensor_copy(out=g4[:, :, 0:8],
                        in_=gtm().rearrange("p (a b) -> p a b", b=8))
        return eng.tensor_tensor(
            out=g4[:, :, 8:16],
            in0=m1c().rearrange("p (a b) -> p a b", b=8),
            in1=m2c().rearrange("p (a b) -> p a b", b=8), op=OP.add)
    v_gm = emit("V", _gmpack, V=v_tk)

    for j in range(32):
        t_g = emit("T", lambda eng, j=j: eng.transpose(
            ptrf[0:16, 0:128], gmslab(j * 16, (j + 1) * 16), identf()),
            V=v_gm if j == 0 else 0, S=M.get("gmev_S", 0))
        s_ge = emit("S", lambda eng, j=j: eng.activation(
            out=gmev((j % 4) * 128, (j % 4 + 1) * 128), in_=ptrf[0:16, 0:128],
            func=AF.Copy), T=t_g,
            E0=M.get("gmev_D", 0) if j % 4 == 0 else 0)
        M["gmev_S"] = s_ge
        if j % 4 == 3:
            M["gmev_D"] = emit("E0", lambda eng, j=j: eng.dma_start(
                out=gm_in.ap()[:, (j - 3) * 128:(j + 1) * 128],
                in_=gmev()), inc=16, S=s_ge)
    d_gm = cnt["D"]

    c_gm = emit("C", lambda eng: eng.collective_compute(
        "AllGather", OP.bypass, replica_groups=[list(range(8))],
        ins=[gm_in.ap().opt()], outs=[gm_all.ap().opt()]),
        E0=M.get("gmev_D", 0))

    d_asm = 0
    for e_i in range(E):
        for s1 in range(2):
            p0_ = e_i * 16 + s1 * 8
            src = gm_all.ap().rearrange("(cc r) t -> cc r t", r=16)
            d_asm = emit("PO", lambda eng, e_i=e_i, s1=s1, p0_=p0_, src=src:
                         eng.dma_start(
                             out=maskg(0, 2048, p0_, p0_ + 8),
                             in_=src[:, 8 + e_i, s1 * 2048:(s1 + 1) * 2048]),
                         inc=16,
                         C=c_gm, V=barV if (e_i == 0 and s1 == 0) else 0,
                         T=barT if (e_i == 0 and s1 == 0) else 0,
                         S=barS if (e_i == 0 and s1 == 0) else 0)

    v_ones = emit("V", lambda eng: eng.memset(offsg(), 1.0), PO=d_asm)
    v_cum = emit("V", lambda eng: eng.tensor_tensor_scan(
        out=cumg(), data0=offsg(), data1=maskg(), initial=0.0,
        op0=OP.mult, op1=OP.add), V=v_ones)
    v_tot = emit("V", lambda eng: eng.tensor_copy(out=totc(),
                                                  in_=cumg(2047, 2048)),
                 V=v_cum)
    t_off = emit("T", lambda eng: eng.matmul(pstat[:, 0:1], selmat(), totc(),
                                             start=True, stop=True),
                 V=max(v_tot, M.get("pstat_V", 0)), S=M.get("pstat_S", 0))
    v_offc = emit("V", lambda eng: eng.tensor_copy(out=offc(),
                                                   in_=pstat[:, 0:1]),
                  T=t_off)
    M["pstat_V"] = v_offc

    def _posval(eng):
        eng.tensor_scalar(out=posg(), in0=cumg(), scalar1=offc(),
                          scalar2=-1.0, op0=OP.add, op1=OP.add)
        eng.tensor_scalar(out=cumg(), in0=posg(), scalar1=float(CAP),
                          scalar2=None, op0=OP.is_lt)
        eng.tensor_tensor(out=maskg(), in0=maskg(), in1=cumg(), op=OP.mult)
        eng.tensor_scalar(out=offsg(), in0=posg(), scalar1=-float(CAP),
                          scalar2=None, op0=OP.add)
        eng.tensor_tensor(out=offsg(), in0=offsg(), in1=maskg(), op=OP.mult)
        eng.tensor_scalar(out=offsg(), in0=offsg(), scalar1=float(CAP),
                          scalar2=None, op0=OP.add)
        eng.tensor_scalar(out=offsg(), in0=offsg(), scalar1=0.0,
                          scalar2=float(CAP), op0=OP.max, op1=OP.min)
        return eng.tensor_copy(out=offsi(), in_=offsg())
    v_pv = emit("V", _posval, V=v_offc)

    d_po = 0
    pos_r = pos_dram.ap().rearrange("(e s2 two g) c -> e s2 two (g c)", e=8, s2=8, two=2)
    off_r = offs_dram.ap().rearrange("(e s2 two g) c -> e s2 two (g c)", e=8, s2=8, two=2)
    for e_i in range(E):
        for s1 in range(2):
            p0_ = e_i * 16 + s1 * 8
            d_po = emit("PO", lambda eng, e_i=e_i, s1=s1, p0_=p0_:
                        eng.dma_start(out=pos_r[e_i, :, s1, :],
                                      in_=posg(0, 2048, p0_, p0_ + 8)),
                        inc=16, V=v_pv)
            d_po = emit("PO", lambda eng, e_i=e_i, s1=s1, p0_=p0_:
                        eng.dma_start(out=off_r[e_i, :, s1, :],
                                      in_=offsi(0, 2048, p0_, p0_ + 8)),
                        inc=16, V=v_pv)

    if KOFF == "host":
        g_oc = dma("G", offcol(), offhost_h.ap(), D=d_po)
    else:
        g_oc = emit("G", lambda eng: eng.indirect_dma_start(
            out=offcol(),
            out_offset=None,
            in_=offs_dram.ap(),
            in_offset=IndirectOffsetOnAxis(ap=myrows(), axis=0)),
            inc=16, PO=d_po)
    g_p0 = emit("G", lambda eng: eng.indirect_dma_start(
        out=posr0(), out_offset=None,
        in_=pos_dram.ap(),
        in_offset=IndirectOffsetOnAxis(ap=mysel0(), axis=0)),
        inc=16, PO=d_po, T=barT, V=barV)
    g_p1 = emit("G", lambda eng: eng.indirect_dma_start(
        out=posr1(), out_offset=None,
        in_=pos_dram.ap(),
        in_offset=IndirectOffsetOnAxis(ap=mysel1(), axis=0)),
        inc=16, PO=d_po)
    t_pt0 = emit("T", lambda eng: eng.transpose(ptrf[:, 0:128], posr0(),
                                                identf()), G=g_p0,
                 V=M.get("ptrf_V", 0), S=M.get("gmev_S", 0))
    v_pt0 = emit("V", lambda eng: eng.tensor_copy(out=postm(0, 128),
                                                  in_=ptrf[:, 0:128]),
                 T=t_pt0)
    t_pt1 = emit("T", lambda eng: eng.transpose(ptrf[:, 0:128], posr1(),
                                                identf()), G=g_p1, V=v_pt0)
    v_pt1 = emit("V", lambda eng: eng.tensor_copy(out=postm(128, 256),
                                                  in_=ptrf[:, 0:128]),
                 T=t_pt1)
    M["ptrf_V"] = v_pt1
    if KDBG:
        dma("D", dbg_xm.ap(), xmid.ap(), E2=cnt["E2"], E3=cnt["E3"])
        dma("D", dbg_h2s.ap(), h2stage.ap(), E0=cnt["E0"], E1=cnt["E1"])
        dma("D", dbg_ptm.ap(), postm(), V=v_pt1)
        dma("D", dbg_gtm.ap(), gtm(), V=v_pv)
        dma("D", dbg_oc.ap(), offcol(), G=g_oc)
        dma("D", dbg_off.ap(), offs_dram.ap().rearrange("(a b) c -> a (b c)", b=8), PO=d_po)
        dma("D", dbg_pos.ap(), posg(), V=v_pv)
        dma("D", dbg_msk.ap(), maskg(), V=v_pv)
        dma("D", dbg_ltm.ap(), ltm(), V=v_pv)
        dma("D", dbg_m1.ap(), m1c(), V=v_pv)

    if KPH < 3:
        d_final = cnt["D"]
        emit("D", lambda eng: None, inc=0, D=d_final, G=cnt["G"], T=cnt["T"],
             V=cnt["V"], S=cnt["S"],
             C=cnt["C"] if int(os.environ.get("KWH2", "0")) else 0)
        return _finish(nc, sched)
    d_w1 = dma("D", w1sb().rearrange("p (a m) -> p a m", m=DH),
               st_w1.ap().rearrange("a p m -> p a m"),
               T=barT, V=barV, S=barS, G=g_moe_w)
    d_w2 = dma("D", w2sb().rearrange("p (a m) -> p a m", m=D),
               st_w2.ap().rearrange("a p m -> p a m"))

    t_gbt = emit("T", lambda eng: eng.transpose(ptrf[0:8, 0:128],
                                                modc(40, 48), identf()),
                 V=max(v_pt1, barV), S=barS)
    v_gbr = emit("V", lambda eng: eng.tensor_copy(out=gbrow(),
                                                  in_=ptrf[0:8, 0:128]),
                 T=t_gbt)
    M["ptrf_V"] = v_gbr
    v_gbb = emit("V", lambda eng: (
        eng.tensor_copy(out=gbrowb(), in_=gbrow()),
        eng.tensor_copy(out=b2rowb(), in_=b2row()))[-1], V=v_gbr)
    for dc in range(8):
        t_bb = emit("T", lambda eng, dc=dc: eng.matmul(
            pbc[:, 0:128], rowsel(dc * 128, (dc + 1) * 128, 0, 8),
            gbrowb(), start=True, stop=True), V=v_gbb,
            S=M.get("pbc_S", 0))
        s_b2 = emit("S", lambda eng, dc=dc: eng.activation(
            out=gbbc(dc * 128, (dc + 1) * 128), in_=pbc[:, 0:128],
            func=AF.Copy), T=t_bb)
        t_bb = emit("T", lambda eng, dc=dc: eng.matmul(
            pbc[:, 0:128], rowsel(dc * 128, (dc + 1) * 128, 0, 8),
            b2rowb(), start=True, stop=True), S=s_b2)
        s_b2 = emit("S", lambda eng, dc=dc: eng.activation(
            out=b2bc(dc * 128, (dc + 1) * 128), in_=pbc[:, 0:128],
            func=AF.Copy), T=t_bb)
        M["pbc_S"] = s_b2

    h2r = h2_all.ap().rearrange("(a b) d -> a b d", b=256)
    for bt in range(64 if KP3 >= 1 else 0):
        buf_v = [scat0, scat1][bt % 2]
        sem_k = ["SA", "SB"][bt % 2]
        d_t = emit(sem_k, lambda eng, bt=bt, buf_v=buf_v: eng.dma_start(
            out=buf_v().rearrange("p (a d) -> p a d", d=D),
            in_=h2r[:, 4 * bt:4 * bt + 4, :]), inc=16,
            G=max(M.get(f"scat{bt % 2}_G", 0), g_bufz),
            C=c_h2 if bt < 2 else 0,
            V=v_pv if bt < 2 else 0, T=barT if bt < 2 else 0)
        g_last = 0
        for q in range(4):
            j = 4 * bt + q
            g_last = emit(
                "G", lambda eng, j=j, q=q, buf_v=buf_v: eng.indirect_dma_start(
                    out=buf_dram.ap(),
                    out_offset=IndirectOffsetOnAxis(ap=offcol(j, j + 1),
                                                    axis=0),
                    in_=buf_v(q * D, (q + 1) * D), in_offset=None),
                inc=16, G=g_oc if j == 0 else 0,
                **{sem_k: d_t} if q == 0 else {})
        M[f"scat{bt % 2}_G"] = g_last
    g_scat = cnt["G"]

    NTT = CAP // 256
    for tt in range((NTT if KP3 >= 2 else 0)):
        d_bf = dma("D", bft().rearrange("p (a d) -> p a d", d=D),
                   buf_dram.ap()[0:CAP, :].rearrange("(a b) d -> a b d", b=128)
                   [2 * tt:2 * tt + 2, :, :].rearrange("a p d -> p a d"),
                   G=g_scat, T=M.get("bft_T", 0))
        t_bt = 0
        for dc in range(8):
            for tb in range(2):
                t_bt = emit("T", lambda eng, dc=dc, tb=tb: eng.transpose(
                    ptr0[:, tb * 128:(tb + 1) * 128],
                    bft(tb * 1024 + dc * 128, tb * 1024 + (dc + 1) * 128),
                    identb()),
                    D=d_bf if (dc == 0 and tb == 0) else 0,
                    V=M.get("ptr0_V", 0) if tb == 0 else 0)
            v_bf = emit("V", lambda eng, dc=dc: eng.tensor_copy(
                out=buffm(dc * 256, (dc + 1) * 256), in_=ptr0[:, 0:256]),
                T=t_bt)
            M["ptr0_V"] = v_bf
        M["bft_T"] = t_bt
        v_bfm = M["ptr0_V"]

        s_h1 = 0
        for hc in range(NHC):
            bi = M["mi"] % 2
            bank = [pmm0, pmm1][bi]
            pw = pm_wait(bi)

            def mm_h1(eng, hc=hc, bank=bank):
                r = None
                for k in range(NMC):
                    r = eng.matmul(bank[:, 0:256],
                                   w1sb(k * DH + hc * 128,
                                        k * DH + (hc + 1) * 128),
                                   buffm(k * 256, (k + 1) * 256),
                                   start=(k == 0), stop=(k == NMC - 1))
                return r
            t_h1 = emit("T", mm_h1, D=d_w1 if (tt == 0 and hc == 0) else 0,
                        V=max(v_bfm, pw["V"]), S=pw["S"])
            s_h1 = emit("S", lambda eng, hc=hc, bank=bank: eng.activation(
                out=h1sb(hc * 256, (hc + 1) * 256), in_=bank[:, 0:256],
                func=AF.Gelu, bias=vb1(hc, hc + 1)), T=t_h1)
            M[f"pmm{bi}_S"] = s_h1
            M["mi"] += 1

        for tb in range(2):
            t_o2 = 0
            for nh in range(2):
                pw = pm_wait(nh)

                def mm_o2(eng, tb=tb, nh=nh):
                    r = None
                    for k in range(NHC):
                        r = eng.matmul(
                            [pmm0, pmm1][nh][:, :],
                            h1sb(k * 256 + tb * 128, k * 256 + (tb + 1) * 128),
                            w2sb(k * D + nh * 512, k * D + (nh + 1) * 512),
                            start=(k == 0), stop=(k == NHC - 1))
                    return r
                t_o2 = emit("T", mm_o2, S=max(s_h1, pw["S"]), V=pw["V"],
                            D=d_w2 if (tt == 0 and tb == 0 and nh == 0) else 0)

            def _oev(eng):
                eng.tensor_tensor(out=otm(0, 512), in0=pmm0[:, :],
                                  in1=b2bc(0, 512), op=OP.add)
                return eng.tensor_tensor(out=otm(512, 1024), in0=pmm1[:, :],
                                         in1=b2bc(512, 1024), op=OP.add)
            v_oev = emit("V", _oev, T=t_o2, OT=M.get("otm_D", 0))
            M["pmm0_V"] = v_oev
            M["pmm1_V"] = v_oev
            M["pmm0_S"] = 0
            M["pmm1_S"] = 0
            M["otm_D"] = emit(
                "OT", lambda eng, tt=tt, tb=tb: eng.dma_start(
                    out=outstage.ap()[tt * 256 + tb * 128:
                                      tt * 256 + (tb + 1) * 128, :],
                    in_=otm()), inc=16, V=v_oev)
    d_ost = cnt["D"]

    if KP3 < 3:
        d_final = cnt["D"]
        emit("D", lambda eng: None, inc=0, D=d_final, G=cnt["G"], T=cnt["T"],
             V=cnt["V"], S=cnt["S"])
        return _finish(nc, sched)
    c_out = emit("C", lambda eng: eng.collective_compute(
        "AllGather", OP.bypass, replica_groups=[list(range(8))],
        ins=[outstage.ap().opt()], outs=[out_all.ap().opt()]),
        D=d_ost, OT=M.get("otm_D", 0))

    if KPH < 4:
        d_final = cnt["D"]
        emit("D", lambda eng: None, inc=0, D=d_final, G=cnt["G"], T=cnt["T"],
             V=cnt["V"], S=cnt["S"])
        return _finish(nc, sched)
    v_own = emit("V", lambda eng: eng.tensor_scalar(
        out=rsc0(), in0=postm(), scalar1=float(CAP), scalar2=None,
        op0=OP.is_lt), V=v_pt1)
    v_own = emit("V", lambda eng: eng.tensor_tensor(
        out=rsc0(), in0=gtm(), in1=rsc0(), op=OP.mult), V=v_own)

    def red(mask, other, wsum, psum_, esum, v_in):
        v = emit("V", lambda eng: eng.tensor_tensor(
            out=rsc1(), in0=mask(), in1=rsc0(), op=OP.mult), V=v_in)
        v = emit("V", lambda eng: eng.tensor_reduce(
            out=wsum(), in_=rsc1().rearrange("p (a b) -> p a b", b=8),
            axis=AX.X, op=OP.add), V=v)
        v = emit("V", lambda eng: eng.tensor_tensor(
            out=rsc1(), in0=mask(), in1=postm(), op=OP.mult), V=v)
        v = emit("V", lambda eng: eng.tensor_reduce(
            out=psum_(), in_=rsc1().rearrange("p (a b) -> p a b", b=8),
            axis=AX.X, op=OP.add), V=v)
        v = emit("V", lambda eng: eng.tensor_tensor(
            out=rsc1(), in0=mask(), in1=iota8(), op=OP.mult), V=v)
        v = emit("V", lambda eng: eng.tensor_reduce(
            out=esum(), in_=rsc1().rearrange("p (a b) -> p a b", b=8),
            axis=AX.X, op=OP.add), V=v)
        return v
    v_own = red(m1c, None, w0c, p0c, e0c, v_own)
    v_own = red(m2c, None, w1c, p1c, e1c, v_own)
    v_own = emit("V", lambda eng: (
        eng.tensor_scalar(out=p0c(), in0=p0c(), scalar1=float(CAP - 1),
                          scalar2=None, op0=OP.min),
        eng.tensor_scalar(out=p1c(), in0=p1c(), scalar1=float(CAP - 1),
                          scalar2=None, op0=OP.min))[-1], V=v_own)
    v_own = emit("V", lambda eng: (
        eng.scalar_tensor_tensor(out=p0c(), in0=e0c(), scalar=float(CAP),
                                 in1=p0c(), op0=OP.mult, op1=OP.add),
        eng.scalar_tensor_tensor(out=p1c(), in0=e1c(), scalar=float(CAP),
                                 in1=p1c(), op0=OP.mult, op1=OP.add))[-1],
        V=v_own)
    v_own = emit("V", lambda eng: (
        eng.tensor_scalar(out=p0c(), in0=p0c(), scalar1=0.0,
                          scalar2=float(E * CAP - 1), op0=OP.max, op1=OP.min),
        eng.tensor_scalar(out=p1c(), in0=p1c(), scalar1=0.0,
                          scalar2=float(E * CAP - 1), op0=OP.max,
                          op1=OP.min))[-1], V=v_own)
    v_own = emit("V", lambda eng: (
        eng.tensor_copy(out=r0i(), in_=p0c()),
        eng.tensor_copy(out=r1i(), in_=p1c()))[-1], V=v_own)

    v_cb = 0
    for j in range(32):
        g_g1 = 0
        if KJ4 >= 1:
            g_g0 = emit("G", lambda eng, j=j: eng.indirect_dma_start(
                out=g0v(), out_offset=None, in_=out_all.ap(),
                in_offset=IndirectOffsetOnAxis(ap=r0i(j, j + 1), axis=0)),
                inc=16, V=max(v_own, v_cb), C=c_out if KWC else 0)
            g_g1 = g_g0
        if KJ4 >= 2:
            g_g1 = emit("G", lambda eng, j=j: eng.indirect_dma_start(
                out=g1v(), out_offset=None, in_=out_all.ap(),
                in_offset=IndirectOffsetOnAxis(ap=r1i(j, j + 1), axis=0)),
                inc=16, V=max(v_own, v_cb))
        d_xt = dma("D", xtt(), xmid.ap()[j * 128:(j + 1) * 128, :],
                   V=v_cb, C=c_out if (j == 0 and KWC) else 0,
                   E2=barE2 if j == 0 else 0, E3=barE3 if j == 0 else 0)

        def _comb(eng, j=j):
            if KJ4 >= 3:
                eng.tensor_scalar(out=yout(), in0=g0v(), scalar1=w0c(j, j + 1),
                                  scalar2=None, op0=OP.mult)
                eng.scalar_tensor_tensor(out=yout(), in0=g1v(),
                                         scalar=w1c(j, j + 1), in1=yout(),
                                         op0=OP.mult, op1=OP.add)
                eng.tensor_tensor(out=yout(), in0=yout(), in1=gbbc(),
                                  op=OP.mult)
                return eng.tensor_tensor(out=yout(), in0=yout(), in1=xtt(),
                                         op=OP.add)
            return eng.tensor_copy(out=yout(), in_=xtt())
        v_cb = emit("V", _comb, G=g_g1, D=d_xt,
                    V=v_own if j == 0 else 0)
        dma("D", out_h.ap()[j * 128:(j + 1) * 128, :], yout(), V=v_cb)

    d_final = cnt["D"]
    emit("D", lambda eng: None, inc=0, D=d_final, G=cnt["G"],
         SA=cnt["SA"], SB=cnt["SB"], PO=cnt["PO"], W0=cnt["W0"],
         W1=cnt["W1"], E0=cnt["E0"], E1=cnt["E1"], E2=cnt["E2"],
         E3=cnt["E3"], OT=cnt["OT"], WM=cnt["WM"])
    return _finish(nc, sched)


def kernel(**inputs):
    inp = {k: np.asarray(v) for k, v in inputs.items()}
    nc = build_nc()

    ident = np.eye(128)
    colconsts = np.zeros((128, 4), np.float32)
    colconsts[:, 0] = 1.0
    colconsts[:, 1] = 1.0 / 1024.0
    colconsts[:, 2] = 1.0 / 1023.0
    onerow = np.ones((1, 128), np.float32)
    sl = np.empty(16, np.int64)
    for s1 in range(2):
        for s2 in range(8):
            sl[s1 * 8 + s2] = s2 * 2 + s1
    selmat = np.zeros((128, 128), np.float32)
    for e_i in range(8):
        for a in range(16):
            for b_ in range(16):
                if sl[a] < sl[b_]:
                    selmat[e_i * 16 + a, e_i * 16 + b_] = 1.0
    rowsel = np.zeros((32, 32 * 128), np.float32)
    for s in range(32):
        rowsel[s, s * 128:(s + 1) * 128] = 1.0
    iota8 = np.tile(np.arange(8, dtype=np.float32), 32)[None, :].repeat(128, 0)

    x = inp["L0_attn_x"]
    c_all = inp["L0_c"]

    in_maps = []
    for ci in range(8):
        m = {
            "x_fm": np.ascontiguousarray(x[ci].T),
            "W_in": inp["L0_attn_W_in"],
            "b_inp": np.ascontiguousarray(inp["L0_attn_b_in"].reshape(32, 128).T),
            "conv_wp": np.ascontiguousarray(inp["L0_attn_conv_w"][:, 0, :].reshape(16, 128, 4).transpose(1, 0, 2).reshape(128, 64)),
            "conv_bp": np.ascontiguousarray(inp["L0_attn_conv_b"].reshape(16, 128).T),
            "W_xpp": np.ascontiguousarray(inp["L0_attn_W_xp"].reshape(16, 128, 32).transpose(1, 0, 2).reshape(128, 512)),
            "b_xp": inp["L0_attn_b_xp"],
            "W_dt": inp["L0_attn_W_dt"],
            "b_dtp": np.ascontiguousarray(inp["L0_attn_b_dt"].reshape(16, 128).T),
            "A_logp": np.ascontiguousarray(inp["L0_attn_A_log"].reshape(16, 128, 16).transpose(1, 0, 2).reshape(128, 256)),
            "D_skipp": np.ascontiguousarray(inp["L0_attn_D_skip"].reshape(16, 128).T),
            "W_out": inp["L0_attn_W_out"],
            "b_outp": np.ascontiguousarray(inp["L0_attn_b_out"].reshape(8, 128).T),
            "alphap": np.ascontiguousarray(inp["L0_attn_alpha"].reshape(8, 128).T),
            "betap": np.ascontiguousarray(inp["L0_attn_beta"].reshape(8, 128).T),
            "router_wp": np.ascontiguousarray(inp["L0_ffn_router_w"].reshape(8, 128, 8).transpose(1, 0, 2).reshape(128, 64)),
            "router_b": inp["L0_ffn_router_b"],
            "w1e": np.ascontiguousarray(inp["L0_ffn_w1"][ci]),
            "b1ep": np.ascontiguousarray(inp["L0_ffn_b1"][ci].reshape(32, 128).T),
            "w2e": np.ascontiguousarray(inp["L0_ffn_w2"][ci]),
            "b2e": np.ascontiguousarray(inp["L0_ffn_b2"][ci]),
            "c_row": np.ascontiguousarray(c_all[ci].reshape(8, 128).T),
            "wmod": inp["L0_wmod"],
            "identb": ident.astype(BF),
            "identf": ident.astype(np.float32),
            "colconsts": colconsts.astype(BF),
            "onerow": onerow.astype(BF),
            "selmat": selmat,
            "rowsel": rowsel.astype(BF),
            "iota8": iota8,
            "myrows": (np.arange(128, dtype=np.int32)
                       + ci * 128).reshape(128, 1),
            "mysel0": np.array([(p % 8) * 256 + ci * 32 + (p // 8)
                                for p in range(128)], np.int32).reshape(128, 1),
            "mysel1": np.array([(p % 8) * 256 + ci * 32 + 16 + (p // 8)
                                for p in range(128)], np.int32).reshape(128, 1),
            "fconst": np.tile(np.array([1e-5, -0.5, 0.5, 0.0],
                                       np.float32), (128, 1)),
            "offhost": np.random.default_rng(7).integers(
                0, 5121, (128, 256)).astype(np.int32),
        }
        in_maps.append(m)

    res = run_bass_kernel_spmd(nc, in_maps, core_ids=list(range(8)))
    out = np.stack([np.asarray(r["out"]) for r in res.results], 0)
    return out.astype(np.float32)


if __name__ == "__main__":
    import os
    import reference as R
    import jax
    with jax.default_device(jax.devices("cpu")[0]):
        inp = R.setup_inputs(0)
        inp = {k: np.asarray(v) for k, v in inp.items()}
        ref = np.asarray(jax.jit(R.reference)(**{k: R.jnp.asarray(v)
                                                 for k, v in inp.items()}))
    got = kernel(**inp)
    err = got - ref
    rvr = float((err ** 2).mean() / (ref ** 2).mean())
    print("resvar ratio:", rvr)


# version 15
# speedup vs baseline: 1.0813x; 1.0010x over previous
import numpy as np
import sys

sys.path.insert(0, "/opt/trn_rl_repo")
import concourse.bass as bass
import concourse.mybir as mybir
from concourse.alu_op_type import AluOpType as OP
from concourse.bass import IndirectOffsetOnAxis
from concourse.bass_utils import run_bass_kernel_spmd
import ml_dtypes

F32 = mybir.dt.float32
BF16 = mybir.dt.bfloat16
I32 = mybir.dt.int32
U8 = mybir.dt.uint8
AF = mybir.ActivationFunctionType
AX = mybir.AxisListType
BF = ml_dtypes.bfloat16

B, T, D = 8, 4096, 1024
DI, DS = 2048, 16
DH = 4096
E = 8
CAP = 5120
TC = 512
NCH = T // TC
NDC = DI // 128
NMC = D // 128
NHC = DH // 128
SG = 4
POOL_B = 204000
import os
KPH = int(os.environ.get("KPH", "4"))
KCH = int(os.environ.get("KCH", "8"))
KSUB = int(os.environ.get("KSUB", "99"))
KJ4 = int(os.environ.get("KJ4", "3"))
KWC = int(os.environ.get("KWC", "1"))
KP3 = int(os.environ.get("KP3", "9"))
KOFF = os.environ.get("KOFF", "dev")
KDBG = int(os.environ.get("KDBG", "0"))

_ESZ = {F32: 4, BF16: 2, I32: 4, U8: 1}


class View:
    def __init__(self, pool, off, n, dt, p0=0, p1=128):
        self.pool, self.off, self.n, self.dt = pool, off, n, dt
        self.p0, self.p1 = p0, p1
        self.esz = _ESZ[dt]

    def __call__(self, a=0, b=None, p0=None, p1=None):
        if b is None:
            b = self.n
        p0 = self.p0 if p0 is None else p0
        p1 = self.p1 if p1 is None else p1
        ap = self.pool[p0:p1, self.off + a * self.esz: self.off + b * self.esz]
        if self.dt != U8:
            ap = ap.bitcast(self.dt)
        return ap


def _finish(nc, sched):
    with nc.Block() as block:
        @block.sync
        def _(eng):
            for step in sched["D"]:
                step(eng)

        @block.vector
        def _(eng):
            for step in sched["V"]:
                step(eng)

        @block.scalar
        def _(eng):
            for step in sched["S"]:
                step(eng)

        @block.tensor
        def _(eng):
            for step in sched["T"]:
                step(eng)

        @block.gpsimd
        def _(eng):
            for step in sched["G"]:
                step(eng)
    return nc


def build_nc():
    nc = bass.Bass("TRN2", target_bir_lowering=False)

    def ein(name, shape, dt=F32):
        return nc.dram_tensor(name, shape, dt, kind="ExternalInput")

    x_in_h = ein("x_fm", [D, T])
    W_in_h = ein("W_in", [D, 2 * DI])
    b_in_h = ein("b_inp", [128, 32])
    convw_h = ein("conv_wp", [128, NDC * 4])
    convb_h = ein("conv_bp", [128, NDC])
    W_xp_h = ein("W_xpp", [128, NDC * 32])
    b_xp_h = ein("b_xp", [2 * DS])
    W_dt_h = ein("W_dt", [DI, DI])
    b_dt_h = ein("b_dtp", [128, NDC])
    A_log_h = ein("A_logp", [128, NDC * DS])
    Dsk_h = ein("D_skipp", [128, NDC])
    W_out_h = ein("W_out", [DI, D])
    b_out_h = ein("b_outp", [128, NMC])
    alpha_h = ein("alphap", [128, NMC])
    beta_h = ein("betap", [128, NMC])
    rw_h = ein("router_wp", [128, NMC * 8])
    rb_h = ein("router_b", [E])
    w1_h = ein("w1e", [D, DH])
    b1_h = ein("b1ep", [128, NHC])
    w2_h = ein("w2e", [DH, D])
    b2_h = ein("b2e", [D])
    c_h = ein("c_row", [128, 8])
    wmod_h = ein("wmod", [D, 6 * D])
    identb_h = ein("identb", [128, 128], BF16)
    identf_h = ein("identf", [128, 128], F32)
    cols_h = ein("colconsts", [128, 4], BF16)
    onerow_h = ein("onerow", [1, 128], BF16)
    selmat_h = ein("selmat", [128, 128], F32)
    rowsel_h = ein("rowsel", [32, 32 * 128], BF16)
    iota8_h = ein("iota8", [128, 256], F32)
    myrows_h = ein("myrows", [128, 1], I32)
    mysel0_h = ein("mysel0", [128, 1], I32)
    mysel1_h = ein("mysel1", [128, 1], I32)
    fconst_h = ein("fconst", [128, 4], F32)
    offhost_h = ein("offhost", [128, 256], I32)

    out_h = nc.dram_tensor("out", [T, D], F32, kind="ExternalOutput")
    if KDBG:
        dbg_oc = nc.dram_tensor("dbg_oc", [128, 256], I32, kind="ExternalOutput")
        dbg_off = nc.dram_tensor("dbg_off", [128, 2048], I32, kind="ExternalOutput")
        dbg_pos = nc.dram_tensor("dbg_pos", [128, 2048], F32, kind="ExternalOutput")
        dbg_msk = nc.dram_tensor("dbg_msk", [128, 2048], F32, kind="ExternalOutput")
        dbg_ltm = nc.dram_tensor("dbg_ltm", [128, 256], F32, kind="ExternalOutput")
        dbg_m1 = nc.dram_tensor("dbg_m1", [128, 256], F32, kind="ExternalOutput")
        dbg_xm = nc.dram_tensor("dbg_xm", [T, D], BF16, kind="ExternalOutput")
        dbg_h2s = nc.dram_tensor("dbg_h2s", [T, D], BF16, kind="ExternalOutput")
        dbg_ptm = nc.dram_tensor("dbg_ptm", [128, 256], F32, kind="ExternalOutput")
        dbg_gtm = nc.dram_tensor("dbg_gtm", [128, 256], F32, kind="ExternalOutput")

    st_Win = nc.dram_tensor("st_Win", [16, 128, 2048], BF16)
    st_Wdt = nc.dram_tensor("st_Wdt", [16, 128, 2048], BF16)
    st_Wout = nc.dram_tensor("st_Wout", [8, 128, 2048], BF16)
    st_w1 = nc.dram_tensor("st_w1", [NMC, 128, DH], BF16)
    st_w2 = nc.dram_tensor("st_w2", [NHC, 128, D], BF16)
    h2stage = nc.dram_tensor("h2stage", [T, D], BF16)
    xmid = nc.dram_tensor("xmid", [T, D], BF16)
    gm_in = nc.dram_tensor("gm_in", [16, T], F32)
    gm_all = nc.dram_tensor("gm_all", [128, T], F32)
    h2_all = nc.dram_tensor("h2_all", [B * T, D], BF16)
    pos_dram = nc.dram_tensor("pos_dram", [2048, 128], F32)
    offs_dram = nc.dram_tensor("offs_dram", [1024, 256], I32)
    buf_dram = nc.dram_tensor("buf_dram", [CAP + 1, D], BF16)
    outstage = nc.dram_tensor("outstage", [CAP, D], BF16)
    out_all = nc.dram_tensor("out_all", [E * CAP, D], BF16)

    import contextlib
    ctx = contextlib.ExitStack()
    semT = ctx.enter_context(nc.semaphore("semT"))
    semV = ctx.enter_context(nc.semaphore("semV"))
    semS = ctx.enter_context(nc.semaphore("semS"))
    semD = ctx.enter_context(nc.semaphore("semD"))
    semG = ctx.enter_context(nc.semaphore("semG"))
    semC = ctx.enter_context(nc.semaphore("semC"))
    semSA = ctx.enter_context(nc.semaphore("semSA"))
    semSB = ctx.enter_context(nc.semaphore("semSB"))
    semPO = ctx.enter_context(nc.semaphore("semPO"))
    semW0 = ctx.enter_context(nc.semaphore("semW0"))
    semW1 = ctx.enter_context(nc.semaphore("semW1"))
    semE0 = ctx.enter_context(nc.semaphore("semE0"))
    semE1 = ctx.enter_context(nc.semaphore("semE1"))
    semE2 = ctx.enter_context(nc.semaphore("semE2"))
    semE3 = ctx.enter_context(nc.semaphore("semE3"))
    semOT = ctx.enter_context(nc.semaphore("semOT"))
    semWM = ctx.enter_context(nc.semaphore("semWM"))
    SEM = {"T": semT, "V": semV, "S": semS, "D": semD, "G": semG,
           "C": semC, "SA": semSA, "SB": semSB, "PO": semPO,
           "W0": semW0, "W1": semW1, "E0": semE0, "E1": semE1,
           "E2": semE2, "E3": semE3, "OT": semOT, "WM": semWM}

    pool = ctx.enter_context(nc.sbuf_tensor("pool", [128, POOL_B], U8))
    offcol_t = ctx.enter_context(nc.sbuf_tensor("offcol_t", [128, 256], I32))
    myrows_t = ctx.enter_context(nc.sbuf_tensor("myrows_t", [128, 1], I32))
    mysel0_t = ctx.enter_context(nc.sbuf_tensor("mysel0_t", [128, 1], I32))
    mysel1_t = ctx.enter_context(nc.sbuf_tensor("mysel1_t", [128, 1], I32))
    r0i_t = ctx.enter_context(nc.sbuf_tensor("r0i_t", [128, 32], I32))
    r1i_t = ctx.enter_context(nc.sbuf_tensor("r1i_t", [128, 32], I32))
    posr0_t = ctx.enter_context(nc.sbuf_tensor("posr0_t", [128, 128], F32))
    posr1_t = ctx.enter_context(nc.sbuf_tensor("posr1_t", [128, 128], F32))
    g0v_t = ctx.enter_context(nc.sbuf_tensor("g0v_t", [128, 1024], BF16))
    g1v_t = ctx.enter_context(nc.sbuf_tensor("g1v_t", [128, 1024], BF16))

    class TW:
        def __init__(self, t):
            self.t = t
        def __call__(self, a=0, b=None, p0=0, p1=128):
            if b is None:
                return self.t[p0:p1, :]
            return self.t[p0:p1, a:b]
    _off = [0]

    def alloc(n, dt, p0=0, p1=128):
        off = _off[0]
        sz = (n * _ESZ[dt] + 31) & ~31
        _off[0] = off + sz
        assert _off[0] <= POOL_B, f"pool overflow {_off[0]}"
        return View(pool, off, n, dt, p0, p1)

    identb = alloc(128, BF16)
    identf = alloc(128, F32)
    colc = alloc(4, BF16)
    onerow = alloc(128, BF16, 0, 1)
    selmat = alloc(128, F32)
    rowsel = alloc(32 * 128, BF16, 0, 32)
    iota8 = alloc(256, F32)
    myrows = TW(myrows_t)
    mysel0 = TW(mysel0_t)
    mysel1 = TW(mysel1_t)
    vb_in = alloc(32, F32)
    vconvw = alloc(NDC * 4, F32)
    vconvb = alloc(NDC, F32)
    vb_xp = alloc(1, F32, 0, 32)
    vb_dt = alloc(NDC, F32)
    vA = alloc(NDC * DS, F32)
    vD = alloc(NDC, F32)
    vb_out = alloc(NMC, F32)
    valpha = alloc(NMC, F32)
    vbeta = alloc(NMC, F32)
    vb1 = alloc(NHC, F32)
    vrb = alloc(1, F32, 0, 8)
    csil = alloc(8, F32)
    modc = alloc(48, F32)
    cac = alloc(8, F32)
    cbc = alloc(8, F32)
    wxp = alloc(NDC * 32, BF16)
    wrt = alloc(NMC * 8, BF16)
    state = alloc(NDC * DS, F32)
    ltm = alloc(256, F32)
    gtm = alloc(256, F32)
    m1c = alloc(256, F32)
    m2c = alloc(256, F32)
    rsc0 = alloc(256, F32)
    rsc1 = alloc(256, F32)
    rsc2 = alloc(32, F32)
    rsc3 = alloc(32, F32)
    b2row = alloc(128, F32, 0, 8)
    b2rowb = alloc(128, BF16, 0, 8)
    gbrow = alloc(128, F32, 0, 8)
    gbrowb = alloc(128, BF16, 0, 8)
    b2bc = alloc(D, BF16)
    gbbc = alloc(D, BF16)
    postm = alloc(256, F32)
    offcol = TW(offcol_t)
    fco = alloc(4, F32)
    totc = alloc(1, F32)
    offc = alloc(1, F32)
    e0c = alloc(32, F32)
    p0c = alloc(32, F32)
    w0c = alloc(32, F32)
    e1c = alloc(32, F32)
    p1c = alloc(32, F32)
    w1c = alloc(32, F32)
    r0i = TW(r0i_t)
    r1i = TW(r1i_t)
    gmev = alloc(512, F32, 0, 16)
    gmslab = alloc(512, F32)
    zrow = alloc(1024, BF16)
    xtail = alloc(NDC * 3, BF16)
    persist_end = _off[0]

    _off[0] = (persist_end + 31) & ~31
    arena = _off[0]
    xs = alloc(NMC * TC, BF16)
    tn = alloc(NMC * TC, BF16)
    hmod = alloc(NMC * TC, BF16)
    xin = alloc(NDC * (TC + 3), BF16)
    zs = alloc(NDC * TC, BF16)
    xc = alloc(NDC * TC, BF16)
    dxc = alloc(NDC * TC, BF16)
    ys = alloc(NDC * TC, BF16)
    Bb0 = alloc(SG * TC, BF16)
    Cb0 = alloc(SG * TC, BF16)
    dA0 = alloc(SG * TC, BF16)
    dA1 = alloc(SG * TC, BF16)
    dBxb = alloc(SG * TC, BF16)
    Hsb = alloc(SG * TC, BF16)
    wbuf0 = alloc(2048, BF16)
    wbuf1 = alloc(2048, BF16)
    srow = alloc(2048, F32, 0, 1)
    srowb = alloc(1024, BF16, 0, 1)
    bcm = alloc(TC, BF16)
    bcr = alloc(TC, BF16)
    bcrow = alloc(TC, BF16, 0, 32)
    l8 = alloc(TC, F32, 0, 8)
    evrow0 = alloc(1024, BF16)
    evrow1 = alloc(1024, BF16)
    evrow2 = alloc(1024, BF16)
    evrow3 = alloc(1024, BF16)
    mamba_end = _off[0]
    spl = View(pool, bcm.off, 512, F32)
    dl = View(pool, xin.off, NDC * TC, BF16)
    outs = View(pool, zs.off, NMC * TC, BF16)
    wmscr = View(pool, xs.off, 6144, F32)
    modacc = View(pool, ys.off, 48, F32)

    _off[0] = arena
    w1sb = alloc(NMC * DH, BF16)
    w2sb = alloc(NHC * D, BF16)
    rbase = _off[0]
    maskg = alloc(2048, F32)
    cumg = alloc(2048, F32)
    posg = alloc(2048, F32)
    offsg = alloc(2048, F32)
    offsi = View(pool, maskg.off, 2048, I32)
    r1_end = _off[0]
    posr0 = TW(posr0_t)
    posr1 = TW(posr1_t)
    _off[0] = rbase
    h1sb = alloc(NHC * 256, BF16)
    buffm = alloc(8 * 256, BF16)
    bft = alloc(2 * 1024, BF16)
    otm = alloc(1024, BF16)
    r1b_end = _off[0]
    scat0 = View(pool, h1sb.off, 4 * D, BF16)
    scat1 = View(pool, h1sb.off + 8192, 4 * D, BF16)
    g0v = TW(g0v_t)
    g1v = TW(g1v_t)
    xtt = View(pool, bft.off, D, BF16)
    yout = View(pool, bft.off + 2048, D, F32)
    moe_end = max(r1_end, r1b_end)
    assert max(mamba_end, moe_end) <= POOL_B, (mamba_end, moe_end)

    pmm0 = ctx.enter_context(nc.psum_tensor("pmm0", [128, 512], F32))
    pmm1 = ctx.enter_context(nc.psum_tensor("pmm1", [128, 512], F32))
    pmm2 = ctx.enter_context(nc.psum_tensor("pmm2", [128, 512], F32))
    pbc = ctx.enter_context(nc.psum_tensor("pbc", [128, 512], F32))
    pstat = ctx.enter_context(nc.psum_tensor("pstat", [128, 512], F32))
    ptrf = ctx.enter_context(nc.psum_tensor("ptrf", [128, 128], F32))
    ptr0 = ctx.enter_context(nc.psum_tensor("ptr0", [128, 1024], BF16))
    ptr1 = ctx.enter_context(nc.psum_tensor("ptr1", [128, 1024], BF16))

    sched = {k: [] for k in "TVSDG"}
    cnt = {k: 0 for k in "TVSDGC"}
    for _k in ("SA", "SB", "PO", "W0", "W1", "E0", "E1", "E2", "E3", "OT", "WM"):
        cnt[_k] = 0

    def emit(e, fn, inc=1, **waits):
        w = [(SEM[k], v) for k, v in waits.items() if v]

        def step(eng, fn=fn, w=w, inc=inc, e=e):
            for sem, val in w:
                eng.wait_ge(sem, val)
            r = fn(eng)
            if inc and r is not None:
                r.then_inc(SEM[e], inc)
        q = {"C": "G", "SA": "D", "SB": "D", "PO": "D", "W0": "D",
             "W1": "D", "E0": "D", "E1": "D", "E2": "D", "E3": "D",
             "OT": "D", "WM": "D"}.get(e, e)
        sched[q].append(step)
        if inc:
            cnt[e] += inc
        return cnt[e]

    def dma(q, out, in_, **waits):
        return emit(q, lambda eng, o=out, i=in_: eng.dma_start(out=o, in_=i),
                    inc=16, **waits)

    M = {"mi": 0}

    for dst, src in [
        (identb(), identb_h.ap()), (identf(), identf_h.ap()),
        (colc(), cols_h.ap()), (onerow(), onerow_h.ap()),
        (selmat(), selmat_h.ap()), (rowsel(), rowsel_h.ap()),
        (iota8(), iota8_h.ap()), (myrows(), myrows_h.ap()),
        (mysel0(), mysel0_h.ap()), (mysel1(), mysel1_h.ap()),
        (csil(), c_h.ap()),
        (vb_in(), b_in_h.ap()),
        (vconvw(), convw_h.ap()),
        (vconvb(), convb_h.ap()),
        (vb_xp(), b_xp_h.ap().rearrange("(s o) -> s o", o=1)),
        (vb_dt(), b_dt_h.ap()),
        (vA(), A_log_h.ap()),
        (vD(), Dsk_h.ap()),
        (vb_out(), b_out_h.ap()),
        (valpha(), alpha_h.ap()),
        (vbeta(), beta_h.ap()),
        (vb1(), b1_h.ap()),
        (vrb(), rb_h.ap().rearrange("(s o) -> s o", o=1)),
        (b2row(), b2_h.ap().rearrange("(a b) -> a b", b=128)),
        (fconst(), fconst_h.ap()) if False else (fco(), fconst_h.ap()),
    ]:
        dma("D", dst, src)
    d_consts = cnt["D"]

    for k in range(NMC):
        dma("G", st_Win.ap()[:, :, k * 256:(k + 1) * 256]
            .rearrange("e p c -> p e c"),
            W_in_h.ap()[k * 128:(k + 1) * 128, :]
            .rearrange("p (e c) -> p e c", c=256))
    for k in range(NDC):
        dma("G", st_Wdt.ap()[:, :, k * 128:(k + 1) * 128]
            .rearrange("e p c -> p e c"),
            W_dt_h.ap()[k * 128:(k + 1) * 128, :]
            .rearrange("p (e c) -> p e c", c=128))
    for k in range(NDC):
        dma("G", st_Wout.ap()[:, :, k * 128:(k + 1) * 128]
            .rearrange("e p c -> p e c"),
            W_out_h.ap()[k * 128:(k + 1) * 128, :]
            .rearrange("p (e c) -> p e c", c=128))
    dma("G", wxp(), W_xp_h.ap())
    g_mamba_w = dma("G", wrt(), rw_h.ap())
    dma("G", st_w1.ap().rearrange("a p m -> (a p) m"), w1_h.ap())
    g_moe_w = dma("G", st_w2.ap().rearrange("a p m -> (a p) m"), w2_h.ap())

    s_ = emit("S", lambda eng: eng.activation(out=vA(), in_=vA(), func=AF.Exp),
              D=d_consts)
    emit("V", lambda eng: eng.tensor_scalar(
        out=vA(), in0=vA(), scalar1=-1.0, scalar2=None, op0=OP.mult), S=s_)
    s_c = emit("S", lambda eng: eng.activation(out=csil(), in_=csil(),
                                               func=AF.Silu), D=d_consts)

    v_acc = 0
    for k in range(NMC):
        d_w = emit("WM", lambda eng, k=k: eng.dma_start(
            out=wmscr(), in_=wmod_h.ap()[k * 128:(k + 1) * 128, :]),
            inc=16, V=v_acc)

        def mmk(eng, k=k):
            r = None
            for mc in range(48):
                r = eng.matmul(pstat[:, mc:mc + 1],
                               wmscr(mc * 128, (mc + 1) * 128),
                               csil(k, k + 1), start=True, stop=True)
            return r
        t_mm = emit("T", mmk, WM=d_w, S=s_c, V=v_acc)
        if k == 0:
            v_acc = emit("V", lambda eng: eng.tensor_copy(
                out=modacc(), in_=pstat[:, 0:48]), T=t_mm)
        else:
            v_acc = emit("V", lambda eng: eng.tensor_tensor(
                out=modacc(), in0=modacc(), in1=pstat[:, 0:48], op=OP.add),
                T=t_mm)
    M["pstat_V"] = v_acc
    v_mod0 = emit("V", lambda eng: eng.tensor_copy(out=modc(), in_=modacc()),
                  V=v_acc)
    s_ab = emit("S", lambda eng: (
        eng.activation(out=rsc2(0, 8), in_=modc(16, 24), func=AF.Abs),
        eng.activation(out=rsc3(0, 8), in_=modc(40, 48), func=AF.Abs))[-1],
        V=v_mod0)

    def _cl1(eng):
        eng.tensor_scalar(out=rsc2(0, 8), in0=rsc2(0, 8), scalar1=0.25,
                          scalar2=None, op0=OP.max)
        eng.tensor_scalar(out=rsc3(0, 8), in0=rsc3(0, 8), scalar1=0.25,
                          scalar2=None, op0=OP.max)
        eng.tensor_scalar(out=rsc0(0, 8), in0=modc(16, 24), scalar1=0.0,
                          scalar2=None, op0=OP.is_ge)
        return eng.tensor_scalar(out=rsc0(8, 16), in0=modc(40, 48),
                                 scalar1=0.0, scalar2=None, op0=OP.is_ge)
    v_mod = emit("V", _cl1, S=s_ab)
    v_mod = emit("V", lambda eng: eng.tensor_scalar(
        out=rsc0(0, 16), in0=rsc0(0, 16), scalar1=2.0, scalar2=-1.0,
        op0=OP.mult, op1=OP.add), V=v_mod)

    def _cl3(eng):
        eng.tensor_tensor(out=modc(16, 24), in0=rsc0(0, 8), in1=rsc2(0, 8),
                          op=OP.mult)
        eng.tensor_tensor(out=modc(40, 48), in0=rsc0(8, 16), in1=rsc3(0, 8),
                          op=OP.mult)
        eng.tensor_scalar(out=cac(), in0=modc(8, 16), scalar1=1.0,
                          scalar2=None, op0=OP.add)
        return eng.tensor_scalar(out=cbc(), in0=modc(32, 40), scalar1=1.0,
                                 scalar2=None, op0=OP.add)
    v_mod = emit("V", _cl3, V=v_mod)

    v_z = emit("V", lambda eng: eng.memset(zrow(), 0.0))
    for r in range(0, CAP + 1, 128):
        n = min(128, CAP + 1 - r)
        dma("G", buf_dram.ap()[r:r + n, :], zrow()[0:n, :], V=v_z)
    g_bufz = cnt["G"]
    emit("V", lambda eng: eng.memset(state(), 0.0), D=d_consts)
    v_tail0 = emit("V", lambda eng: eng.memset(xtail(), 0.0))

    def bcast3(view):
        return view().unsqueeze(1).broadcast_to([128, NMC, TC])

    def stat_ln(xview, s_sq, msq_col, unbiased, eps, xV=0, xG=0):
        def mm_mean(eng):
            r = None
            for k in range(NMC):
                r = eng.matmul(pstat[0:1, :], colc(1, 2),
                               xview(k * TC, (k + 1) * TC),
                               start=(k == 0), stop=(k == NMC - 1))
            return r

        def mm_msq(eng):
            r = None
            for k in range(NMC):
                r = eng.matmul(pbc[0:1, :], colc(msq_col, msq_col + 1),
                               tn(k * TC, (k + 1) * TC),
                               start=(k == 0), stop=(k == NMC - 1))
            return r
        emit("T", mm_mean, S=M.get("pstat_S", 0),
             V=max(M.get("pstat_V", 0), xV), G=xG)
        t2 = emit("T", mm_msq, S=max(s_sq, M.get("pbc_S", 0)))

        def _stats(eng):
            eng.tensor_copy(out=srow(0, 512), in_=pstat[0:1, :])
            eng.tensor_copy(out=srow(512, 1024), in_=pbc[0:1, :])
            eng.tensor_tensor(out=srow(1536, 2048), in0=srow(0, 512),
                              in1=srow(0, 512), op=OP.mult)
            if unbiased:
                eng.tensor_scalar(out=srow(1536, 2048), in0=srow(1536, 2048),
                                  scalar1=1024.0 / 1023.0, scalar2=None,
                                  op0=OP.mult)
            return eng.tensor_tensor(out=srow(512, 1024), in0=srow(512, 1024),
                                     in1=srow(1536, 2048), op=OP.subtract)
        v1 = emit("V", _stats, T=t2)
        M["pstat_V"] = v1
        if not unbiased:
            s1 = emit("S", lambda eng: (
                eng.activation(out=srow(1536, 2048), in_=srow(512, 1024),
                               func=AF.Ln, bias=fco(0, 1, 0, 1)),
                eng.activation(out=srow(1024, 1536), in_=srow(1536, 2048),
                               func=AF.Exp, scale=fco(1, 2, 0, 1)))[-1], V=v1)
            v3 = emit("V", lambda eng: eng.nop(), S=s1)
        else:
            s1 = emit("S", lambda eng: (
                eng.activation(out=srow(1536, 2048), in_=srow(512, 1024),
                               func=AF.Ln),
                eng.activation(out=srow(1536, 2048), in_=srow(1536, 2048),
                               func=AF.Exp, scale=fco(2, 3, 0, 1)))[-1], V=v1)
            v2 = emit("V", lambda eng: eng.tensor_scalar(
                out=srow(1536, 2048), in0=srow(1536, 2048), scalar1=eps,
                scalar2=None, op0=OP.add), S=s1)
            v3 = emit("V", lambda eng: eng.reciprocal(
                out=srow(1024, 1536), in_=srow(1536, 2048)), V=v2)
        v4 = emit("V", lambda eng: (
            eng.tensor_copy(out=srowb(0, 512), in_=srow(0, 512)),
            eng.tensor_copy(out=srowb(512, 1024), in_=srow(1024, 1536)))[-1],
            V=v3)
        t3 = emit("T", lambda eng: eng.matmul(
            pbc[:, :], onerow(), srowb(0, 512), start=True, stop=True),
            V=v4, S=M.get("pbc_S", 0))
        s2 = emit("S", lambda eng: eng.activation(out=bcm(), in_=pbc[:, :],
                                                  func=AF.Copy), T=t3)
        t4 = emit("T", lambda eng: eng.matmul(
            pbc[:, :], onerow(), srowb(512, 1024), start=True, stop=True),
            S=s2)
        s3 = emit("S", lambda eng: eng.activation(out=bcr(), in_=pbc[:, :],
                                                  func=AF.Copy), T=t4)
        M["pbc_S"] = s3
        return s3

    def pm_wait(b):
        return dict(S=M.get(f"pmm{b}_S", 0), V=M.get(f"pmm{b}_V", 0))

    def stream_mm(stash, total_cols, slice_cols, k_cnt, rhs_view, evac_fn,
                  rhs_V, rhs_S=0):
        nsl = total_cols // slice_cols
        mper = slice_cols // 128
        wb = [wbuf0, wbuf1]
        dwait = {}
        for e_s in range(min(2, nsl)):
            dwait[e_s] = emit(
                ["W0", "W1"][e_s % 2],
                lambda eng, e_s=e_s: eng.dma_start(
                    out=wb[e_s % 2](0, k_cnt * slice_cols),
                    in_=stash.ap()[e_s]),
                inc=16, T=M.get(f"wb{e_s % 2}_T", 0), G=g_mamba_w)
        s_last = 0
        for e_s in range(nsl):
            t_mm = 0
            for m in range(mper):
                bi = M["mi"] % 3
                bank = [pmm0, pmm1, pmm2][bi]
                pw = pm_wait(bi)

                def mm(eng, e_s=e_s, m=m, bank=bank):
                    r = None
                    for k in range(k_cnt):
                        r = eng.matmul(
                            bank[:, :],
                            wb[e_s % 2](k * slice_cols + m * 128,
                                        k * slice_cols + (m + 1) * 128),
                            rhs_view(k * TC, (k + 1) * TC),
                            start=(k == 0), stop=(k == k_cnt - 1))
                    return r
                t_mm = emit("T", mm, V=max(rhs_V, pw["V"]),
                            S=max(pw["S"], rhs_S),
                            **{["W0", "W1"][e_s % 2]: dwait[e_s]})
                mc = e_s * mper + m
                s_last = emit("S", lambda eng, mc=mc, bank=bank:
                              evac_fn(eng, mc, bank), T=t_mm)
                M[f"pmm{bi}_S"] = s_last
                M["mi"] += 1
            M[f"wb{e_s % 2}_T"] = t_mm
            nxt = e_s + 2
            if nxt < nsl:
                dwait[nxt] = emit(
                    ["W0", "W1"][nxt % 2],
                    lambda eng, nxt=nxt: eng.dma_start(
                        out=wb[nxt % 2](0, k_cnt * slice_cols),
                        in_=stash.ap()[nxt]),
                    inc=16, T=M.get(f"wb{nxt % 2}_T", 0))
        return s_last

    for c in range(KCH if KPH >= 1 else 0):
        c0 = c * TC
        prevT = M.get("chunk_T", 0)
        prevV = M.get("chunk_V", 0)
        prevS = M.get("chunk_S", 0)
        g_x = dma("G", xs().rearrange("p (a t) -> p a t", t=TC),
                  x_in_h.ap()[:, c0:c0 + TC].rearrange("(a p) t -> p a t",
                                                       p=128),
                  T=prevT, V=max(prevV, v_mod if c == 0 else 0))
        s_sq = emit("S", lambda eng: eng.activation(out=tn(), in_=xs(),
                                                    func=AF.Square),
                    G=g_x, T=prevT, V=prevV, S=prevS)
        s_bc = stat_ln(xs, s_sq, 1, False, 1e-5, xG=g_x)

        def _norm(src):
            def f(eng, src=src):
                bm = bcast3(bcm)
                br = bcast3(bcr)
                t3 = tn().rearrange("p (a t) -> p a t", t=TC)
                x3 = src().rearrange("p (a t) -> p a t", t=TC)
                eng.tensor_tensor(out=t3, in0=x3, in1=bm, op=OP.subtract)
                return eng.tensor_tensor(out=t3, in0=t3, in1=br, op=OP.mult)
            return f
        v_n1 = emit("V", _norm(xs), S=s_bc)

        def _hmod(eng):
            r = None
            for dc in range(NMC):
                r = eng.tensor_scalar(
                    out=hmod(dc * TC, (dc + 1) * TC),
                    in0=tn(dc * TC, (dc + 1) * TC),
                    scalar1=cac(dc, dc + 1), scalar2=modc(dc, dc + 1),
                    op0=OP.mult, op1=OP.add)
            return r
        v_hm = emit("V", _hmod, V=v_n1, T=prevT)

        if KSUB < 1:
            continue
        def ev_xz(eng, mc, bank):
            if mc < 16:
                o = xin(mc * (TC + 3) + 3, mc * (TC + 3) + 3 + TC)
            else:
                o = zs((mc - 16) * TC, (mc - 15) * TC)
            return eng.activation(out=o, in_=bank[:, :], func=AF.Identity,
                                  bias=vb_in(mc, mc + 1))
        s_xz = stream_mm(st_Win, 2 * DI, 256, NMC, hmod, ev_xz, v_hm)

        if KSUB < 2:
            continue
        v_rt = emit("V", lambda eng: eng.tensor_copy(
            out=xin().rearrange("p (a t) -> p a t", t=TC + 3)[:, :, 0:3],
            in_=xtail().rearrange("p (a t) -> p a t", t=3)),
            S=s_xz, V=v_tail0 if c == 0 else 0)

        def _conv(eng):
            r = None
            for dc in range(NDC):
                base = dc * (TC + 3)
                eng.tensor_scalar(
                    out=xc(dc * TC, (dc + 1) * TC),
                    in0=xin(base, base + TC),
                    scalar1=vconvw(dc * 4, dc * 4 + 1),
                    scalar2=vconvb(dc, dc + 1), op0=OP.mult, op1=OP.add)
                for k2 in (1, 2, 3):
                    r = eng.scalar_tensor_tensor(
                        out=xc(dc * TC, (dc + 1) * TC),
                        in0=xin(base + k2, base + k2 + TC),
                        scalar=vconvw(dc * 4 + k2, dc * 4 + k2 + 1),
                        in1=xc(dc * TC, (dc + 1) * TC),
                        op0=OP.mult, op1=OP.add)
            return r
        v_cv = emit("V", _conv, V=v_rt)
        v_tail = emit("V", lambda eng: eng.tensor_copy(
            out=xtail().rearrange("p (a t) -> p a t", t=3),
            in_=xin().rearrange("p (a t) -> p a t", t=TC + 3)[:, :, TC:TC + 3]),
            V=v_cv)
        s_si = emit("S", lambda eng: eng.activation(out=xc(), in_=xc(),
                                                    func=AF.Silu), V=v_cv)
        s_zsil = emit("S", lambda eng: eng.activation(out=zs(), in_=zs(),
                                                      func=AF.Silu), S=s_xz,
                      V=M.get("zs_V", 0))

        if KSUB < 3:
            continue
        def mm_xp(eng):
            r = None
            for k in range(NDC):
                r = eng.matmul(pstat[0:32, :], wxp(k * 32, (k + 1) * 32),
                               xc(k * TC, (k + 1) * TC),
                               start=(k == 0), stop=(k == NDC - 1))
            return r
        t_xp = emit("T", mm_xp, S=max(s_si, M.get("pstat_S", 0)),
                    V=M.get("pstat_V", 0), G=g_mamba_w if c == 0 else 0)
        s_bcr2 = emit("S", lambda eng: eng.activation(
            out=bcrow(), in_=pstat[0:32, :], func=AF.Identity, bias=vb_xp()),
            T=t_xp)
        M["pstat_S"] = s_bcr2

        if KSUB < 4:
            continue
        def ev_dt(eng, mc, bank):
            eng.activation(out=spl(), in_=bank[:, :], func=AF.Exp,
                           bias=vb_dt(mc, mc + 1))
            return eng.activation(out=dl(mc * TC, (mc + 1) * TC),
                                  in_=spl(), func=AF.Ln, bias=1.0)
        s_dt = stream_mm(st_Wdt, DI, 128, NDC, xc, ev_dt,
                         max(v_cv, v_tail), rhs_S=s_si)

        if KSUB < 5:
            continue
        v_dxc = emit("V", lambda eng: eng.tensor_tensor(
            out=dxc(), in0=dl(), in1=xc(), op=OP.mult), S=s_dt)

        def _yinit(eng):
            r = None
            for dc in range(NDC):
                r = eng.tensor_scalar(out=ys(dc * TC, (dc + 1) * TC),
                                      in0=xc(dc * TC, (dc + 1) * TC),
                                      scalar1=vD(dc, dc + 1), scalar2=None,
                                      op0=OP.mult)
            return r
        v_yi = emit("V", _yinit, V=v_dxc)

        if KSUB < 6:
            continue
        v_sc = v_yi
        for sg in range(DS // SG):
            Bbv, Cbv = Bb0, Cb0
            bbfree = v_sc
            s_bb = M.get("pbc_S", 0)
            for s4 in range(SG):
                s_idx = sg * SG + s4
                t_b = emit("T", lambda eng, s_idx=s_idx: eng.matmul(
                    pbc[:, :], rowsel(s_idx * 128, (s_idx + 1) * 128),
                    bcrow(), start=True, stop=True),
                    S=max(s_bcr2, s_bb),
                    V=bbfree if s4 == 0 else 0)
                s_bb = emit("S", lambda eng, s4=s4, Bbv=Bbv: eng.activation(
                    out=Bbv(s4 * TC, (s4 + 1) * TC), in_=pbc[:, :],
                    func=AF.Copy), T=t_b)
                t_cq = emit("T", lambda eng, s_idx=s_idx: eng.matmul(
                    pbc[:, :], rowsel((16 + s_idx) * 128, (17 + s_idx) * 128),
                    bcrow(), start=True, stop=True), S=s_bb)
                s_bb = emit("S", lambda eng, s4=s4, Cbv=Cbv: eng.activation(
                    out=Cbv(s4 * TC, (s4 + 1) * TC), in_=pbc[:, :],
                    func=AF.Copy), T=t_cq)
            M["pbc_S"] = s_bb
            for dc in range(NDC):
                dAv = dA0 if dc % 2 == 0 else dA1
                dafree = M.get(f"da{dc % 2}_V", 0)

                def _dA(eng, dc=dc, sg=sg, dAv=dAv):
                    r = None
                    for s4 in range(SG):
                        r = eng.activation(
                            out=dAv(s4 * TC, (s4 + 1) * TC),
                            in_=dl(dc * TC, (dc + 1) * TC), func=AF.Exp,
                            scale=vA(dc * DS + sg * SG + s4,
                                     dc * DS + sg * SG + s4 + 1))
                    return r
                s_dA = emit("S", _dA, S=s_bb, V=dafree)

                def _scan(eng, dc=dc, sg=sg, dAv=dAv, Bbv=Bbv):
                    eng.tensor_tensor(
                        out=dBxb().rearrange("p (a t) -> p a t", t=TC),
                        in0=dxc(dc * TC, (dc + 1) * TC).unsqueeze(1)
                        .broadcast_to([128, SG, TC]),
                        in1=Bbv().rearrange("p (a t) -> p a t", t=TC),
                        op=OP.mult)
                    r = None
                    for s4 in range(SG):
                        r = eng.tensor_tensor_scan(
                            out=Hsb(s4 * TC, (s4 + 1) * TC),
                            data0=dAv(s4 * TC, (s4 + 1) * TC),
                            data1=dBxb(s4 * TC, (s4 + 1) * TC),
                            initial=state(dc * DS + sg * SG + s4,
                                          dc * DS + sg * SG + s4 + 1),
                            op0=OP.mult, op1=OP.add)
                    return r
                v_mid = emit("V", _scan, S=s_dA)

                def _rdout(eng, dc=dc, sg=sg, Cbv=Cbv):
                    eng.tensor_copy(
                        out=state(dc * DS + sg * SG, dc * DS + sg * SG + SG),
                        in_=Hsb().rearrange("p (a t) -> p a t", t=TC)
                        [:, :, TC - 1:TC])
                    eng.tensor_tensor(out=dBxb(), in0=Hsb(), in1=Cbv(),
                                      op=OP.mult)
                    p3 = dBxb().rearrange("p (a t) -> p a t", t=TC)
                    r = None
                    for s4 in range(SG):
                        r = eng.tensor_tensor(out=ys(dc * TC, (dc + 1) * TC),
                                              in0=ys(dc * TC, (dc + 1) * TC),
                                              in1=p3[:, s4, :], op=OP.add)
                    return r
                v_sc = emit("V", _rdout, V=v_mid)
                M[f"da{dc % 2}_V"] = v_sc

        if KSUB < 7:
            continue
        v_g = emit("V", lambda eng: eng.tensor_tensor(
            out=ys(), in0=ys(), in1=zs(), op=OP.mult), V=v_sc, S=s_zsil)
        M["zs_V"] = v_g

        def ev_o(eng, mc, bank):
            return eng.activation(out=outs(mc * TC, (mc + 1) * TC),
                                  in_=bank[:, :], func=AF.Identity,
                                  bias=vb_out(mc, mc + 1))
        s_o = stream_mm(st_Wout, D, 128, NDC, ys, ev_o, v_g)

        if KSUB < 8:
            continue
        v_r2 = emit("V", lambda eng: eng.tensor_tensor(
            out=hmod(), in0=hmod(), in1=outs(), op=OP.add), S=s_o)
        s_sq2 = emit("S", lambda eng: eng.activation(out=tn(), in_=hmod(),
                                                     func=AF.Square), V=v_r2)
        s_bc2 = stat_ln(hmod, s_sq2, 2, True, 1e-6, xV=v_r2)
        v_n2 = emit("V", _norm(hmod), S=s_bc2)

        def _lnfin(eng):
            r = None
            for dc in range(NMC):
                eng.tensor_scalar(out=outs(dc * TC, (dc + 1) * TC),
                                  in0=tn(dc * TC, (dc + 1) * TC),
                                  scalar1=valpha(dc, dc + 1),
                                  scalar2=vbeta(dc, dc + 1),
                                  op0=OP.mult, op1=OP.add)
                r = eng.scalar_tensor_tensor(
                    out=xs(dc * TC, (dc + 1) * TC),
                    in0=outs(dc * TC, (dc + 1) * TC),
                    scalar=modc(16 + dc, 17 + dc),
                    in1=xs(dc * TC, (dc + 1) * TC), op0=OP.mult, op1=OP.add)
            return r
        v_xu = emit("V", _lnfin, V=v_n2)

        s_sq3 = emit("S", lambda eng: eng.activation(out=tn(), in_=xs(),
                                                     func=AF.Square), V=v_xu)
        s_bc3 = stat_ln(xs, s_sq3, 1, False, 1e-5, xV=v_xu)
        v_n3 = emit("V", _norm(xs), S=s_bc3)

        def _h2(eng):
            r = None
            for dc in range(NMC):
                r = eng.tensor_scalar(
                    out=hmod(dc * TC, (dc + 1) * TC),
                    in0=tn(dc * TC, (dc + 1) * TC),
                    scalar1=cbc(dc, dc + 1), scalar2=modc(24 + dc, 25 + dc),
                    op0=OP.mult, op1=OP.add)
            return r
        v_h2 = emit("V", _h2, V=v_n3)

        if KSUB < 9:
            continue
        def mm_rt(eng):
            r = None
            for k in range(NMC):
                r = eng.matmul(pstat[0:8, :], wrt(k * 8, (k + 1) * 8),
                               hmod(k * TC, (k + 1) * TC),
                               start=(k == 0), stop=(k == NMC - 1))
            return r
        t_rt = emit("T", mm_rt, V=max(v_h2, M.get("pstat_V", 0)),
                    S=M.get("pstat_S", 0), G=g_mamba_w if c == 0 else 0)
        s_l8 = emit("S", lambda eng: eng.activation(
            out=l8(), in_=pstat[0:8, :], func=AF.Identity, bias=vrb()), T=t_rt)
        M["pstat_S"] = s_l8
        t_lt = 0
        for tb in range(4):
            t_lt = emit("T", lambda eng, tb=tb: eng.transpose(
                ptrf[:, tb * 8:(tb + 1) * 8], l8(tb * 128, (tb + 1) * 128),
                identf(0, 8, 0, 8)), S=s_l8,
                V=M.get("ptrf_V", 0) if tb == 0 else 0)
        v_lt = emit("V", lambda eng, c=c: eng.tensor_copy(
            out=ltm(c * 32, (c + 1) * 32), in_=ptrf[:, 0:32]), T=t_lt)
        M["ptrf_V"] = v_lt

        ev = [evrow0, evrow1]
        evx = [evrow2, evrow3]
        for tb in range(4):
            t_h = 0
            for mc2 in range(NMC):
                t_h = emit("T", lambda eng, tb=tb, mc2=mc2: eng.transpose(
                    ptr0[:, mc2 * 128:(mc2 + 1) * 128],
                    hmod(mc2 * TC + tb * 128, mc2 * TC + (tb + 1) * 128),
                    identb()),
                    V=max(v_h2, M.get("ptr0_V", 0)) if mc2 == 0 else 0,
                    **{["E0", "E1"][tb % 2]:
                       M.get(f"evh{tb % 2}_D", 0) if mc2 == 0 else 0})
            v_eh = emit("V", lambda eng, tb=tb: eng.tensor_copy(
                out=ev[tb % 2](), in_=ptr0[:, :]), T=t_h)
            M["ptr0_V"] = v_eh
            M[f"evh{tb % 2}_D"] = emit(
                ["E0", "E1"][tb % 2],
                lambda eng, tb=tb, c0=c0: eng.dma_start(
                    out=h2stage.ap()[c0 + tb * 128:c0 + (tb + 1) * 128, :],
                    in_=ev[tb % 2]()), inc=16, V=v_eh)
            t_x = 0
            for mc2 in range(NMC):
                t_x = emit("T", lambda eng, tb=tb, mc2=mc2: eng.transpose(
                    ptr1[:, mc2 * 128:(mc2 + 1) * 128],
                    xs(mc2 * TC + tb * 128, mc2 * TC + (tb + 1) * 128),
                    identb()),
                    V=max(v_xu, M.get("ptr1_V", 0)) if mc2 == 0 else 0,
                    **{["E2", "E3"][tb % 2]:
                       M.get(f"evx{tb % 2}_D", 0) if mc2 == 0 else 0})
            v_ex = emit("V", lambda eng, tb=tb: eng.tensor_copy(
                out=evx[tb % 2](), in_=ptr1[:, :]), T=t_x)
            M["ptr1_V"] = v_ex
            M[f"evx{tb % 2}_D"] = emit(
                ["E2", "E3"][tb % 2],
                lambda eng, tb=tb, c0=c0: eng.dma_start(
                    out=xmid.ap()[c0 + tb * 128:c0 + (tb + 1) * 128, :],
                    in_=evx[tb % 2]()), inc=16, V=v_ex)
        M["chunk_T"] = cnt["T"]
        M["chunk_V"] = cnt["V"]
        M["chunk_S"] = cnt["S"]

    d_stage = cnt["D"]
    barT, barV, barS = cnt["T"], cnt["V"], cnt["S"]
    barE0, barE1, barE2, barE3 = cnt["E0"], cnt["E1"], cnt["E2"], cnt["E3"]
    if KPH < 2:
        d_final = cnt["D"]
        emit("D", lambda eng: None, inc=0, D=d_final, G=cnt["G"], T=cnt["T"],
             V=cnt["V"], S=cnt["S"])
        return _finish(nc, sched)

    c_h2 = emit("C", lambda eng: eng.collective_compute(
        "AllGather", OP.bypass, replica_groups=[list(range(8))],
        ins=[h2stage.ap().opt()], outs=[h2_all.ap().opt()]),
        D=d_stage, E0=barE0, E1=barE1)

    s_exp = emit("S", lambda eng: eng.activation(out=gtm(), in_=ltm(),
                                                 func=AF.Exp),
                 V=barV, T=barT)

    g3v = lambda: gtm().rearrange("p (a b) -> p a b", b=8)
    v_sm = emit("V", lambda eng: eng.tensor_reduce(
        out=rsc2(), in_=g3v(), axis=AX.X, op=OP.add), S=s_exp)
    v_sm = emit("V", lambda eng: eng.reciprocal(out=rsc2(), in_=rsc2()),
                V=v_sm)
    v_sm = emit("V", lambda eng: eng.tensor_tensor(
        out=g3v(), in0=g3v(), in1=rsc2().unsqueeze(2)
        .broadcast_to([128, 32, 8]), op=OP.mult), V=v_sm)

    def maskpass(srcv, dst, v_in):
        s3 = lambda: srcv().rearrange("p (a b) -> p a b", b=8)
        d3 = lambda: dst().rearrange("p (a b) -> p a b", b=8)
        r0 = lambda: rsc0().rearrange("p (a b) -> p a b", b=8)
        r1 = lambda: rsc1().rearrange("p (a b) -> p a b", b=8)
        v = emit("V", lambda eng: eng.tensor_reduce(
            out=rsc2(), in_=s3(), axis=AX.X, op=OP.max), V=v_in)
        v = emit("V", lambda eng: eng.tensor_tensor(
            out=d3(), in0=s3(), in1=rsc2().unsqueeze(2)
            .broadcast_to([128, 32, 8]), op=OP.is_ge), V=v)
        v = emit("V", lambda eng: eng.tensor_copy(out=r0(), in_=d3()), V=v)
        for sh, (sa, sd) in zip((1, 2, 4), ((r0, r1), (r1, r0), (r0, r1))):
            v = emit("V", lambda eng, sh=sh, sa=sa, sd=sd: (
                eng.tensor_copy(out=sd()[:, :, 0:sh], in_=sa()[:, :, 0:sh]),
                eng.tensor_tensor(out=sd()[:, :, sh:8], in0=sa()[:, :, sh:8],
                                  in1=sa()[:, :, 0:8 - sh], op=OP.add))[-1],
                V=v)
        v = emit("V", lambda eng: eng.tensor_scalar(
            out=rsc0(), in0=rsc1(), scalar1=1.0, scalar2=None, op0=OP.is_le),
            V=v)
        v = emit("V", lambda eng: eng.tensor_tensor(
            out=dst(), in0=dst(), in1=rsc0(), op=OP.mult), V=v)
        return v
    v_tk = maskpass(lambda: ltm(), m1c, v_sm)
    v_tk = emit("V", lambda eng: eng.scalar_tensor_tensor(
        out=gmslab(0, 256), in0=m1c(), scalar=-1000.0, in1=ltm(),
        op0=OP.mult, op1=OP.add), V=v_tk)
    v_tk = maskpass(lambda: gmslab(0, 256), m2c, v_tk)

    def _gmpack(eng):
        g4 = gmslab().rearrange("p (a b) -> p a b", b=16)
        eng.tensor_copy(out=g4[:, :, 0:8],
                        in_=gtm().rearrange("p (a b) -> p a b", b=8))
        return eng.tensor_tensor(
            out=g4[:, :, 8:16],
            in0=m1c().rearrange("p (a b) -> p a b", b=8),
            in1=m2c().rearrange("p (a b) -> p a b", b=8), op=OP.add)
    v_gm = emit("V", _gmpack, V=v_tk)

    for j in range(32):
        t_g = emit("T", lambda eng, j=j: eng.transpose(
            ptrf[0:16, 0:128], gmslab(j * 16, (j + 1) * 16), identf()),
            V=v_gm if j == 0 else 0, S=M.get("gmev_S", 0))
        s_ge = emit("S", lambda eng, j=j: eng.activation(
            out=gmev((j % 4) * 128, (j % 4 + 1) * 128), in_=ptrf[0:16, 0:128],
            func=AF.Copy), T=t_g,
            E0=M.get("gmev_D", 0) if j % 4 == 0 else 0)
        M["gmev_S"] = s_ge
        if j % 4 == 3:
            M["gmev_D"] = emit("E0", lambda eng, j=j: eng.dma_start(
                out=gm_in.ap()[:, (j - 3) * 128:(j + 1) * 128],
                in_=gmev()), inc=16, S=s_ge)
    d_gm = cnt["D"]

    c_gm = emit("C", lambda eng: eng.collective_compute(
        "AllGather", OP.bypass, replica_groups=[list(range(8))],
        ins=[gm_in.ap().opt()], outs=[gm_all.ap().opt()]),
        E0=M.get("gmev_D", 0))

    d_asm = 0
    for e_i in range(E):
        for s1 in range(2):
            p0_ = e_i * 16 + s1 * 8
            src = gm_all.ap().rearrange("(cc r) t -> cc r t", r=16)
            d_asm = emit("PO", lambda eng, e_i=e_i, s1=s1, p0_=p0_, src=src:
                         eng.dma_start(
                             out=maskg(0, 2048, p0_, p0_ + 8),
                             in_=src[:, 8 + e_i, s1 * 2048:(s1 + 1) * 2048]),
                         inc=16,
                         C=c_gm, V=barV if (e_i == 0 and s1 == 0) else 0,
                         T=barT if (e_i == 0 and s1 == 0) else 0,
                         S=barS if (e_i == 0 and s1 == 0) else 0)

    v_ones = emit("V", lambda eng: eng.memset(offsg(), 1.0), PO=d_asm)
    v_cum = emit("V", lambda eng: eng.tensor_tensor_scan(
        out=cumg(), data0=offsg(), data1=maskg(), initial=0.0,
        op0=OP.mult, op1=OP.add), V=v_ones)
    v_tot = emit("V", lambda eng: eng.tensor_copy(out=totc(),
                                                  in_=cumg(2047, 2048)),
                 V=v_cum)
    t_off = emit("T", lambda eng: eng.matmul(pstat[:, 0:1], selmat(), totc(),
                                             start=True, stop=True),
                 V=max(v_tot, M.get("pstat_V", 0)), S=M.get("pstat_S", 0))
    v_offc = emit("V", lambda eng: eng.tensor_copy(out=offc(),
                                                   in_=pstat[:, 0:1]),
                  T=t_off)
    M["pstat_V"] = v_offc

    def _posval(eng):
        eng.tensor_scalar(out=posg(), in0=cumg(), scalar1=offc(),
                          scalar2=-1.0, op0=OP.add, op1=OP.add)
        eng.tensor_scalar(out=cumg(), in0=posg(), scalar1=float(CAP),
                          scalar2=None, op0=OP.is_lt)
        eng.tensor_tensor(out=maskg(), in0=maskg(), in1=cumg(), op=OP.mult)
        eng.tensor_scalar(out=offsg(), in0=posg(), scalar1=-float(CAP),
                          scalar2=None, op0=OP.add)
        eng.tensor_tensor(out=offsg(), in0=offsg(), in1=maskg(), op=OP.mult)
        eng.tensor_scalar(out=offsg(), in0=offsg(), scalar1=float(CAP),
                          scalar2=None, op0=OP.add)
        eng.tensor_scalar(out=offsg(), in0=offsg(), scalar1=0.0,
                          scalar2=float(CAP), op0=OP.max, op1=OP.min)
        return eng.tensor_copy(out=offsi(), in_=offsg())
    v_pv = emit("V", _posval, V=v_offc)

    d_po = 0
    pos_r = pos_dram.ap().rearrange("(e s2 two g) c -> e s2 two (g c)", e=8, s2=8, two=2)
    off_r = offs_dram.ap().rearrange("(e s2 two g) c -> e s2 two (g c)", e=8, s2=8, two=2)
    for e_i in range(E):
        for s1 in range(2):
            p0_ = e_i * 16 + s1 * 8
            d_po = emit("PO", lambda eng, e_i=e_i, s1=s1, p0_=p0_:
                        eng.dma_start(out=pos_r[e_i, :, s1, :],
                                      in_=posg(0, 2048, p0_, p0_ + 8)),
                        inc=16, V=v_pv)
            d_po = emit("PO", lambda eng, e_i=e_i, s1=s1, p0_=p0_:
                        eng.dma_start(out=off_r[e_i, :, s1, :],
                                      in_=offsi(0, 2048, p0_, p0_ + 8)),
                        inc=16, V=v_pv)

    if KOFF == "host":
        g_oc = dma("G", offcol(), offhost_h.ap(), D=d_po)
    else:
        g_oc = emit("G", lambda eng: eng.indirect_dma_start(
            out=offcol(),
            out_offset=None,
            in_=offs_dram.ap(),
            in_offset=IndirectOffsetOnAxis(ap=myrows(), axis=0)),
            inc=16, PO=d_po)
    g_p0 = emit("G", lambda eng: eng.indirect_dma_start(
        out=posr0(), out_offset=None,
        in_=pos_dram.ap(),
        in_offset=IndirectOffsetOnAxis(ap=mysel0(), axis=0)),
        inc=16, PO=d_po, T=barT, V=barV)
    g_p1 = emit("G", lambda eng: eng.indirect_dma_start(
        out=posr1(), out_offset=None,
        in_=pos_dram.ap(),
        in_offset=IndirectOffsetOnAxis(ap=mysel1(), axis=0)),
        inc=16, PO=d_po)
    t_pt0 = emit("T", lambda eng: eng.transpose(ptrf[:, 0:128], posr0(),
                                                identf()), G=g_p0,
                 V=M.get("ptrf_V", 0), S=M.get("gmev_S", 0))
    v_pt0 = emit("V", lambda eng: eng.tensor_copy(out=postm(0, 128),
                                                  in_=ptrf[:, 0:128]),
                 T=t_pt0)
    t_pt1 = emit("T", lambda eng: eng.transpose(ptrf[:, 0:128], posr1(),
                                                identf()), G=g_p1, V=v_pt0)
    v_pt1 = emit("V", lambda eng: eng.tensor_copy(out=postm(128, 256),
                                                  in_=ptrf[:, 0:128]),
                 T=t_pt1)
    M["ptrf_V"] = v_pt1
    if KDBG:
        dma("D", dbg_xm.ap(), xmid.ap(), E2=cnt["E2"], E3=cnt["E3"])
        dma("D", dbg_h2s.ap(), h2stage.ap(), E0=cnt["E0"], E1=cnt["E1"])
        dma("D", dbg_ptm.ap(), postm(), V=v_pt1)
        dma("D", dbg_gtm.ap(), gtm(), V=v_pv)
        dma("D", dbg_oc.ap(), offcol(), G=g_oc)
        dma("D", dbg_off.ap(), offs_dram.ap().rearrange("(a b) c -> a (b c)", b=8), PO=d_po)
        dma("D", dbg_pos.ap(), posg(), V=v_pv)
        dma("D", dbg_msk.ap(), maskg(), V=v_pv)
        dma("D", dbg_ltm.ap(), ltm(), V=v_pv)
        dma("D", dbg_m1.ap(), m1c(), V=v_pv)

    if KPH < 3:
        d_final = cnt["D"]
        emit("D", lambda eng: None, inc=0, D=d_final, G=cnt["G"], T=cnt["T"],
             V=cnt["V"], S=cnt["S"],
             C=cnt["C"] if int(os.environ.get("KWH2", "0")) else 0)
        return _finish(nc, sched)
    d_w1 = dma("D", w1sb().rearrange("p (a m) -> p a m", m=DH),
               st_w1.ap().rearrange("a p m -> p a m"),
               T=barT, V=barV, S=barS, G=g_moe_w)
    d_w2 = dma("D", w2sb().rearrange("p (a m) -> p a m", m=D),
               st_w2.ap().rearrange("a p m -> p a m"))

    t_gbt = emit("T", lambda eng: eng.transpose(ptrf[0:8, 0:128],
                                                modc(40, 48), identf()),
                 V=max(v_pt1, barV), S=barS)
    v_gbr = emit("V", lambda eng: eng.tensor_copy(out=gbrow(),
                                                  in_=ptrf[0:8, 0:128]),
                 T=t_gbt)
    M["ptrf_V"] = v_gbr
    v_gbb = emit("V", lambda eng: (
        eng.tensor_copy(out=gbrowb(), in_=gbrow()),
        eng.tensor_copy(out=b2rowb(), in_=b2row()))[-1], V=v_gbr)
    for dc in range(8):
        t_bb = emit("T", lambda eng, dc=dc: eng.matmul(
            pbc[:, 0:128], rowsel(dc * 128, (dc + 1) * 128, 0, 8),
            gbrowb(), start=True, stop=True), V=v_gbb,
            S=M.get("pbc_S", 0))
        s_b2 = emit("S", lambda eng, dc=dc: eng.activation(
            out=gbbc(dc * 128, (dc + 1) * 128), in_=pbc[:, 0:128],
            func=AF.Copy), T=t_bb)
        t_bb = emit("T", lambda eng, dc=dc: eng.matmul(
            pbc[:, 0:128], rowsel(dc * 128, (dc + 1) * 128, 0, 8),
            b2rowb(), start=True, stop=True), S=s_b2)
        s_b2 = emit("S", lambda eng, dc=dc: eng.activation(
            out=b2bc(dc * 128, (dc + 1) * 128), in_=pbc[:, 0:128],
            func=AF.Copy), T=t_bb)
        M["pbc_S"] = s_b2

    h2r = h2_all.ap().rearrange("(a b) d -> a b d", b=256)
    for bt in range(64 if KP3 >= 1 else 0):
        buf_v = [scat0, scat1][bt % 2]
        sem_k = ["SA", "SB"][bt % 2]
        d_t = emit(sem_k, lambda eng, bt=bt, buf_v=buf_v: eng.dma_start(
            out=buf_v().rearrange("p (a d) -> p a d", d=D),
            in_=h2r[:, 4 * bt:4 * bt + 4, :]), inc=16,
            G=max(M.get(f"scat{bt % 2}_G", 0), g_bufz),
            C=c_h2 if bt < 2 else 0,
            V=v_pv if bt < 2 else 0, T=barT if bt < 2 else 0)
        g_last = 0
        for q in range(4):
            j = 4 * bt + q
            g_last = emit(
                "G", lambda eng, j=j, q=q, buf_v=buf_v: eng.indirect_dma_start(
                    out=buf_dram.ap(),
                    out_offset=IndirectOffsetOnAxis(ap=offcol(j, j + 1),
                                                    axis=0),
                    in_=buf_v(q * D, (q + 1) * D), in_offset=None),
                inc=16, G=g_oc if j == 0 else 0,
                **{sem_k: d_t} if q == 0 else {})
        M[f"scat{bt % 2}_G"] = g_last
    g_scat = cnt["G"]

    NTT = CAP // 256
    for tt in range((NTT if KP3 >= 2 else 0)):
        d_bf = dma("D", bft().rearrange("p (a d) -> p a d", d=D),
                   buf_dram.ap()[0:CAP, :].rearrange("(a b) d -> a b d", b=128)
                   [2 * tt:2 * tt + 2, :, :].rearrange("a p d -> p a d"),
                   G=g_scat, T=M.get("bft_T", 0))
        t_bt = 0
        for dc in range(8):
            for tb in range(2):
                t_bt = emit("T", lambda eng, dc=dc, tb=tb: eng.transpose(
                    ptr0[:, tb * 128:(tb + 1) * 128],
                    bft(tb * 1024 + dc * 128, tb * 1024 + (dc + 1) * 128),
                    identb()),
                    D=d_bf if (dc == 0 and tb == 0) else 0,
                    V=M.get("ptr0_V", 0) if tb == 0 else 0)
            v_bf = emit("V", lambda eng, dc=dc: eng.tensor_copy(
                out=buffm(dc * 256, (dc + 1) * 256), in_=ptr0[:, 0:256]),
                T=t_bt)
            M["ptr0_V"] = v_bf
        M["bft_T"] = t_bt
        v_bfm = M["ptr0_V"]

        s_h1 = 0
        for hc in range(NHC):
            bi = M["mi"] % 3
            bank = [pmm0, pmm1, pmm2][bi]
            pw = pm_wait(bi)

            def mm_h1(eng, hc=hc, bank=bank):
                r = None
                for k in range(NMC):
                    r = eng.matmul(bank[:, 0:256],
                                   w1sb(k * DH + hc * 128,
                                        k * DH + (hc + 1) * 128),
                                   buffm(k * 256, (k + 1) * 256),
                                   start=(k == 0), stop=(k == NMC - 1))
                return r
            t_h1 = emit("T", mm_h1, D=d_w1 if (tt == 0 and hc == 0) else 0,
                        V=max(v_bfm, pw["V"]), S=pw["S"])
            s_h1 = emit("S", lambda eng, hc=hc, bank=bank: eng.activation(
                out=h1sb(hc * 256, (hc + 1) * 256), in_=bank[:, 0:256],
                func=AF.Gelu, bias=vb1(hc, hc + 1)), T=t_h1)
            M[f"pmm{bi}_S"] = s_h1
            M["mi"] += 1

        for tb in range(2):
            t_o2 = 0
            for nh in range(2):
                pw = pm_wait(nh)

                def mm_o2(eng, tb=tb, nh=nh):
                    r = None
                    for k in range(NHC):
                        r = eng.matmul(
                            [pmm0, pmm1][nh][:, :],
                            h1sb(k * 256 + tb * 128, k * 256 + (tb + 1) * 128),
                            w2sb(k * D + nh * 512, k * D + (nh + 1) * 512),
                            start=(k == 0), stop=(k == NHC - 1))
                    return r
                t_o2 = emit("T", mm_o2, S=max(s_h1, pw["S"]), V=pw["V"],
                            D=d_w2 if (tt == 0 and tb == 0 and nh == 0) else 0)

            def _oev(eng):
                eng.tensor_tensor(out=otm(0, 512), in0=pmm0[:, :],
                                  in1=b2bc(0, 512), op=OP.add)
                return eng.tensor_tensor(out=otm(512, 1024), in0=pmm1[:, :],
                                         in1=b2bc(512, 1024), op=OP.add)
            v_oev = emit("V", _oev, T=t_o2, OT=M.get("otm_D", 0))
            M["pmm0_V"] = v_oev
            M["pmm1_V"] = v_oev
            M["pmm0_S"] = 0
            M["pmm1_S"] = 0
            M["otm_D"] = emit(
                "OT", lambda eng, tt=tt, tb=tb: eng.dma_start(
                    out=outstage.ap()[tt * 256 + tb * 128:
                                      tt * 256 + (tb + 1) * 128, :],
                    in_=otm()), inc=16, V=v_oev)
    d_ost = cnt["D"]

    if KP3 < 3:
        d_final = cnt["D"]
        emit("D", lambda eng: None, inc=0, D=d_final, G=cnt["G"], T=cnt["T"],
             V=cnt["V"], S=cnt["S"])
        return _finish(nc, sched)
    c_out = emit("C", lambda eng: eng.collective_compute(
        "AllGather", OP.bypass, replica_groups=[list(range(8))],
        ins=[outstage.ap().opt()], outs=[out_all.ap().opt()]),
        D=d_ost, OT=M.get("otm_D", 0))

    if KPH < 4:
        d_final = cnt["D"]
        emit("D", lambda eng: None, inc=0, D=d_final, G=cnt["G"], T=cnt["T"],
             V=cnt["V"], S=cnt["S"])
        return _finish(nc, sched)
    v_own = emit("V", lambda eng: eng.tensor_scalar(
        out=rsc0(), in0=postm(), scalar1=float(CAP), scalar2=None,
        op0=OP.is_lt), V=v_pt1)
    v_own = emit("V", lambda eng: eng.tensor_tensor(
        out=rsc0(), in0=gtm(), in1=rsc0(), op=OP.mult), V=v_own)

    def red(mask, other, wsum, psum_, esum, v_in):
        v = emit("V", lambda eng: eng.tensor_tensor(
            out=rsc1(), in0=mask(), in1=rsc0(), op=OP.mult), V=v_in)
        v = emit("V", lambda eng: eng.tensor_reduce(
            out=wsum(), in_=rsc1().rearrange("p (a b) -> p a b", b=8),
            axis=AX.X, op=OP.add), V=v)
        v = emit("V", lambda eng: eng.tensor_tensor(
            out=rsc1(), in0=mask(), in1=postm(), op=OP.mult), V=v)
        v = emit("V", lambda eng: eng.tensor_reduce(
            out=psum_(), in_=rsc1().rearrange("p (a b) -> p a b", b=8),
            axis=AX.X, op=OP.add), V=v)
        v = emit("V", lambda eng: eng.tensor_tensor(
            out=rsc1(), in0=mask(), in1=iota8(), op=OP.mult), V=v)
        v = emit("V", lambda eng: eng.tensor_reduce(
            out=esum(), in_=rsc1().rearrange("p (a b) -> p a b", b=8),
            axis=AX.X, op=OP.add), V=v)
        return v
    v_own = red(m1c, None, w0c, p0c, e0c, v_own)
    v_own = red(m2c, None, w1c, p1c, e1c, v_own)
    v_own = emit("V", lambda eng: (
        eng.tensor_scalar(out=p0c(), in0=p0c(), scalar1=float(CAP - 1),
                          scalar2=None, op0=OP.min),
        eng.tensor_scalar(out=p1c(), in0=p1c(), scalar1=float(CAP - 1),
                          scalar2=None, op0=OP.min))[-1], V=v_own)
    v_own = emit("V", lambda eng: (
        eng.scalar_tensor_tensor(out=p0c(), in0=e0c(), scalar=float(CAP),
                                 in1=p0c(), op0=OP.mult, op1=OP.add),
        eng.scalar_tensor_tensor(out=p1c(), in0=e1c(), scalar=float(CAP),
                                 in1=p1c(), op0=OP.mult, op1=OP.add))[-1],
        V=v_own)
    v_own = emit("V", lambda eng: (
        eng.tensor_scalar(out=p0c(), in0=p0c(), scalar1=0.0,
                          scalar2=float(E * CAP - 1), op0=OP.max, op1=OP.min),
        eng.tensor_scalar(out=p1c(), in0=p1c(), scalar1=0.0,
                          scalar2=float(E * CAP - 1), op0=OP.max,
                          op1=OP.min))[-1], V=v_own)
    v_own = emit("V", lambda eng: (
        eng.tensor_copy(out=r0i(), in_=p0c()),
        eng.tensor_copy(out=r1i(), in_=p1c()))[-1], V=v_own)

    v_cb = 0
    for j in range(32):
        g_g1 = 0
        if KJ4 >= 1:
            g_g0 = emit("G", lambda eng, j=j: eng.indirect_dma_start(
                out=g0v(), out_offset=None, in_=out_all.ap(),
                in_offset=IndirectOffsetOnAxis(ap=r0i(j, j + 1), axis=0)),
                inc=16, V=max(v_own, v_cb), C=c_out if KWC else 0)
            g_g1 = g_g0
        if KJ4 >= 2:
            g_g1 = emit("G", lambda eng, j=j: eng.indirect_dma_start(
                out=g1v(), out_offset=None, in_=out_all.ap(),
                in_offset=IndirectOffsetOnAxis(ap=r1i(j, j + 1), axis=0)),
                inc=16, V=max(v_own, v_cb))
        d_xt = dma("D", xtt(), xmid.ap()[j * 128:(j + 1) * 128, :],
                   V=v_cb, C=c_out if (j == 0 and KWC) else 0,
                   E2=barE2 if j == 0 else 0, E3=barE3 if j == 0 else 0)

        def _comb(eng, j=j):
            if KJ4 >= 3:
                eng.tensor_scalar(out=yout(), in0=g0v(), scalar1=w0c(j, j + 1),
                                  scalar2=None, op0=OP.mult)
                eng.scalar_tensor_tensor(out=yout(), in0=g1v(),
                                         scalar=w1c(j, j + 1), in1=yout(),
                                         op0=OP.mult, op1=OP.add)
                eng.tensor_tensor(out=yout(), in0=yout(), in1=gbbc(),
                                  op=OP.mult)
                return eng.tensor_tensor(out=yout(), in0=yout(), in1=xtt(),
                                         op=OP.add)
            return eng.tensor_copy(out=yout(), in_=xtt())
        v_cb = emit("V", _comb, G=g_g1, D=d_xt,
                    V=v_own if j == 0 else 0)
        dma("D", out_h.ap()[j * 128:(j + 1) * 128, :], yout(), V=v_cb)

    d_final = cnt["D"]
    emit("D", lambda eng: None, inc=0, D=d_final, G=cnt["G"],
         SA=cnt["SA"], SB=cnt["SB"], PO=cnt["PO"], W0=cnt["W0"],
         W1=cnt["W1"], E0=cnt["E0"], E1=cnt["E1"], E2=cnt["E2"],
         E3=cnt["E3"], OT=cnt["OT"], WM=cnt["WM"])
    return _finish(nc, sched)


def kernel(**inputs):
    inp = {k: np.asarray(v) for k, v in inputs.items()}
    nc = build_nc()

    ident = np.eye(128)
    colconsts = np.zeros((128, 4), np.float32)
    colconsts[:, 0] = 1.0
    colconsts[:, 1] = 1.0 / 1024.0
    colconsts[:, 2] = 1.0 / 1023.0
    onerow = np.ones((1, 128), np.float32)
    sl = np.empty(16, np.int64)
    for s1 in range(2):
        for s2 in range(8):
            sl[s1 * 8 + s2] = s2 * 2 + s1
    selmat = np.zeros((128, 128), np.float32)
    for e_i in range(8):
        for a in range(16):
            for b_ in range(16):
                if sl[a] < sl[b_]:
                    selmat[e_i * 16 + a, e_i * 16 + b_] = 1.0
    rowsel = np.zeros((32, 32 * 128), np.float32)
    for s in range(32):
        rowsel[s, s * 128:(s + 1) * 128] = 1.0
    iota8 = np.tile(np.arange(8, dtype=np.float32), 32)[None, :].repeat(128, 0)

    x = inp["L0_attn_x"]
    c_all = inp["L0_c"]

    in_maps = []
    for ci in range(8):
        m = {
            "x_fm": np.ascontiguousarray(x[ci].T),
            "W_in": inp["L0_attn_W_in"],
            "b_inp": np.ascontiguousarray(inp["L0_attn_b_in"].reshape(32, 128).T),
            "conv_wp": np.ascontiguousarray(inp["L0_attn_conv_w"][:, 0, :].reshape(16, 128, 4).transpose(1, 0, 2).reshape(128, 64)),
            "conv_bp": np.ascontiguousarray(inp["L0_attn_conv_b"].reshape(16, 128).T),
            "W_xpp": np.ascontiguousarray(inp["L0_attn_W_xp"].reshape(16, 128, 32).transpose(1, 0, 2).reshape(128, 512)),
            "b_xp": inp["L0_attn_b_xp"],
            "W_dt": inp["L0_attn_W_dt"],
            "b_dtp": np.ascontiguousarray(inp["L0_attn_b_dt"].reshape(16, 128).T),
            "A_logp": np.ascontiguousarray(inp["L0_attn_A_log"].reshape(16, 128, 16).transpose(1, 0, 2).reshape(128, 256)),
            "D_skipp": np.ascontiguousarray(inp["L0_attn_D_skip"].reshape(16, 128).T),
            "W_out": inp["L0_attn_W_out"],
            "b_outp": np.ascontiguousarray(inp["L0_attn_b_out"].reshape(8, 128).T),
            "alphap": np.ascontiguousarray(inp["L0_attn_alpha"].reshape(8, 128).T),
            "betap": np.ascontiguousarray(inp["L0_attn_beta"].reshape(8, 128).T),
            "router_wp": np.ascontiguousarray(inp["L0_ffn_router_w"].reshape(8, 128, 8).transpose(1, 0, 2).reshape(128, 64)),
            "router_b": inp["L0_ffn_router_b"],
            "w1e": np.ascontiguousarray(inp["L0_ffn_w1"][ci]),
            "b1ep": np.ascontiguousarray(inp["L0_ffn_b1"][ci].reshape(32, 128).T),
            "w2e": np.ascontiguousarray(inp["L0_ffn_w2"][ci]),
            "b2e": np.ascontiguousarray(inp["L0_ffn_b2"][ci]),
            "c_row": np.ascontiguousarray(c_all[ci].reshape(8, 128).T),
            "wmod": inp["L0_wmod"],
            "identb": ident.astype(BF),
            "identf": ident.astype(np.float32),
            "colconsts": colconsts.astype(BF),
            "onerow": onerow.astype(BF),
            "selmat": selmat,
            "rowsel": rowsel.astype(BF),
            "iota8": iota8,
            "myrows": (np.arange(128, dtype=np.int32)
                       + ci * 128).reshape(128, 1),
            "mysel0": np.array([(p % 8) * 256 + ci * 32 + (p // 8)
                                for p in range(128)], np.int32).reshape(128, 1),
            "mysel1": np.array([(p % 8) * 256 + ci * 32 + 16 + (p // 8)
                                for p in range(128)], np.int32).reshape(128, 1),
            "fconst": np.tile(np.array([1e-5, -0.5, 0.5, 0.0],
                                       np.float32), (128, 1)),
            "offhost": np.random.default_rng(7).integers(
                0, 5121, (128, 256)).astype(np.int32),
        }
        in_maps.append(m)

    res = run_bass_kernel_spmd(nc, in_maps, core_ids=list(range(8)))
    out = np.stack([np.asarray(r["out"]) for r in res.results], 0)
    return out.astype(np.float32)


if __name__ == "__main__":
    import os
    import reference as R
    import jax
    with jax.default_device(jax.devices("cpu")[0]):
        inp = R.setup_inputs(0)
        inp = {k: np.asarray(v) for k, v in inp.items()}
        ref = np.asarray(jax.jit(R.reference)(**{k: R.jnp.asarray(v)
                                                 for k, v in inp.items()}))
    got = kernel(**inp)
    err = got - ref
    rvr = float((err ** 2).mean() / (ref ** 2).mean())
    print("resvar ratio:", rvr)
